# version 07
# baseline (speedup 1.0000x reference)
import numpy as np
import ml_dtypes
import concourse.bass as bass
import concourse.mybir as mybir
from concourse.bass_utils import run_bass_kernel_spmd
from contextlib import ExitStack

F32 = mybir.dt.float32
BF16 = mybir.dt.bfloat16
U32 = mybir.dt.uint32
AF = mybir.ActivationFunctionType
OP = mybir.AluOpType
AX = mybir.AxisListType

B, S, D = 4, 4096, 1024
TC = 2048
DIN = 2048
NST = 16
DTR = 64
HF = 2048
NE = 8
CAP = 2560
LN_EPS = 1e-5
NEG_LN_EPS12 = 27.631021115928547

import os as _os
DEV_STAGE = int(_os.environ.get("DEV_STAGE", "3"))
MAMBA_CUT = int(_os.environ.get("MAMBA_CUT", "9"))

ENGS = ("sync", "gpsimd", "tensor", "vector", "scalar")
QUEUES = ENGS + ("dsync", "dgpsimd")


class Op:
    __slots__ = ("eng", "queue", "fn", "reads", "writes", "sreads", "tag", "idx",
                 "need_inc", "waits", "inc_amount", "skey", "sidx")

    def __init__(self, eng, queue, fn, reads, writes, tag, inc_amount):
        self.eng = eng
        self.queue = queue
        self.fn = fn
        self.reads = reads
        self.writes = writes
        self.sreads = ()
        self.tag = tag
        self.idx = None
        self.need_inc = False
        self.waits = {}
        self.inc_amount = inc_amount


class KB:
    def __init__(self):
        self.nc = bass.Bass("TRN2", target_bir_lowering=False, debug=False,
                            num_devices=8, dynamic_dma_scratch_size=16384)
        self.es = ExitStack()
        self.ops = []

    def sb(self, name, shape, dt=F32):
        return self.es.enter_context(self.nc.sbuf_tensor(name, list(shape), dt))

    @staticmethod
    def _key(r):
        if isinstance(r, str):
            return r
        n = getattr(r, "name", None)
        if n is not None:
            return n
        t = getattr(r, "tensor", None)
        if t is not None:
            return t.name
        raise TypeError(f"bad resource {r!r}")

    def add(self, eng, fn, reads=(), writes=(), sreads=(), tag="", dma=False,
            skey=None):
        queue = ("d" + eng) if dma else eng
        amount = 16 if dma else 1
        op = Op(eng, queue, fn,
                tuple(self._key(r) for r in reads) + tuple(self._key(r) for r in sreads),
                tuple(self._key(r) for r in writes), tag, amount)
        op.sreads = tuple(self._key(r) for r in sreads)
        if dma:
            op.skey = skey or (op.writes[0] if op.writes else queue)
        else:
            op.skey = queue
        self.ops.append(op)
        return op

    def dma(self, eng, out_ap, in_ap, reads=(), writes=(), tag="", skey=None, **kw):
        return self.add(
            eng,
            lambda g, o=out_ap, i=in_ap, kw=kw: g.dma_start(out=o, in_=i, **kw),
            reads=reads, writes=writes, tag=tag, dma=True, skey=skey)

    def analyze(self):
        counters = {q: 0 for q in QUEUES}
        scount = {}
        cur_last = {}
        last_writer = {}
        readers = {}
        for op in self.ops:
            counters[op.queue] += 1
            op.idx = counters[op.queue]
            scount[op.skey] = scount.get(op.skey, 0) + 1
            op.sidx = scount[op.skey]
            if op.queue.startswith("d"):
                op.need_inc = True
            deps = []
            for r in op.reads:
                w = last_writer.get(r)
                if w is not None:
                    deps.append((w, r))
            for w_res in op.writes:
                w = last_writer.get(w_res)
                if w is not None:
                    deps.append((w, w_res))
                deps.extend((x, w_res) for x in readers.get(w_res, ()))
            sread_set = set(op.sreads)
            for d, dres in deps:
                if d is op:
                    continue
                if d.queue == op.queue and d.queue.startswith("d"):
                    pass
                elif d.eng == op.eng and d.queue == d.eng:
                    if op.eng == "tensor":
                        continue
                    if dres not in sread_set and (op.idx - d.idx) > 4:
                        continue
                d.need_inc = True
                if d.queue.startswith("d"):
                    tgt = cur_last[d.skey]
                    cur = op.waits.get(d.skey, 0)
                    op.waits[d.skey] = max(cur, tgt.sidx)
                else:
                    cur = op.waits.get(d.queue, 0)
                    op.waits[d.queue] = max(cur, d.idx)
            cur_last[op.skey] = op
            for r in op.reads:
                readers.setdefault(r, []).append(op)
            for w_res in op.writes:
                last_writer[w_res] = op
                readers[w_res] = []

    def materialize(self, sems):
        inc_prefix = {}
        run = {}
        for op in self.ops:
            if op.need_inc:
                run[op.skey] = run.get(op.skey, 0) + op.inc_amount
            inc_prefix.setdefault(op.skey, {})[op.sidx] = run.get(op.skey, 0)
        per_eng = {e: [] for e in ENGS}
        for op in self.ops:
            waits = []
            for q, idx in sorted(op.waits.items()):
                val = inc_prefix[q].get(idx)
                assert val is not None
                waits.append((sems[q], val))
            per_eng[op.eng].append((op, waits))
        return per_eng


def build_program():
    kb = KB()
    nc = kb.nc
    sb = kb.sb

    def ein(name, shape, dt=F32):
        return nc.dram_tensor(name, list(shape), dt, kind="ExternalInput").ap()

    x_tm = ein("x_tm", (TC, D))
    w_in = ein("w_in", (D, 2 * DIN))
    convw_c = [ein(f"convw{k}", (128, 16)) for k in range(4)]
    convb_c = ein("convb_c", (128, 16))
    bdt_c = ein("bdt_c", (128, 16))
    dvec_c = ein("dvec_c", (128, 16))
    w_x = ein("w_x", (DIN, 96))
    w_dt = ein("w_dt", (DTR, DIN))
    w_out = ein("w_out", (DIN, D))
    gw_t = ein("gw_t", (D, NE))
    wg_t = ein("wg_t", (NE, D, HF))
    wu_t = ein("wu_t", (NE, D, HF))
    wd_t = ein("wd_t", (NE, HF, D))
    rank_in = ein("rank_in", (1, 1), U32)
    rankf_in = ein("rankf_in", (16, 1))
    rankc_in = ein("rankc_in", (128, 1))
    nonces = ein("nonces", (1, 8), U32)

    out_tm = nc.dram_tensor("out_tm", [TC, D], F32, kind="ExternalOutput").ap()
    dbg_comb = nc.dram_tensor("dbg_comb", [NE, TC], F32, kind="ExternalOutput").ap()
    dbg_cnt = nc.dram_tensor("dbg_cnt", [16, 2], F32, kind="ExternalOutput").ap()
    dbg_glog = nc.dram_tensor("dbg_glog", [128, 128], F32, kind="ExternalOutput").ap()
    dbg_ohf = nc.dram_tensor("dbg_ohf", [16, TC], F32, kind="ExternalOutput").ap()
    dbg_xn = nc.dram_tensor("dbg_xn", [128, TC], BF16, kind="ExternalOutput").ap()
    dbg_rkf = nc.dram_tensor("dbg_rkf", [16, TC], F32, kind="ExternalOutput").ap()

    shared = lambda n, sh, dt=F32: nc.dram_tensor(n, list(sh), dt, addr_space="Shared").ap()
    sh_flag = shared("sh_flag", (1, 16), U32)
    sh_cnt = [shared(f"sh_cnt{i}", (2, 16)) for i in range(2)]
    if DEV_STAGE >= 2:
        sh_halo = shared("sh_halo", (DIN, 6))
        sh_xis = shared("sh_xis", (DIN, S), BF16)
        sh_xdbl = shared("sh_xdbl", (96, S))
        sh_y = shared("sh_y", (DIN, S), BF16)
    dr_comb = nc.dram_tensor("dr_comb", [NE, TC], F32).ap()
    dr_res = nc.dram_tensor("dr_res", [16, TC * 128], BF16).ap()
    dr_x = nc.dram_tensor("dr_x", [8, TC * 128], F32).ap()
    dr_bcrows = nc.dram_tensor("dr_bcrows", [32, S], BF16).ap()
    dr_uloc = nc.dram_tensor("dr_uloc", [1024, S], BF16).ap()
    dr_yloc = nc.dram_tensor("dr_yloc", [1024, S], BF16).ap()
    dr_yepi = nc.dram_tensor("dr_yepi", [DIN, TC], BF16).ap()
    dr_uepi = nc.dram_tensor("dr_uepi", [DIN, TC], BF16).ap()
    dr_xdbl = nc.dram_tensor("dr_xdbl", [96, TC], F32).ap()

    ident_h = nc.inline_tensor(np.eye(128, dtype=np.float32), name="c_ident")
    ones_r = nc.inline_tensor(np.ones((1, 128), np.float32), name="c_ones_r")
    ones_c = nc.inline_tensor(np.ones((128, 1), np.float32), name="c_ones_c")
    selk = np.zeros((16, 2), np.float32); selk[:8, 0] = 1; selk[8:, 1] = 1
    selk_h = nc.inline_tensor(selk, name="c_selk")
    selkT_h = nc.inline_tensor(np.ascontiguousarray(selk.T), name="c_selkT")
    sele = np.zeros((16, 8), np.float32)
    for i in range(16): sele[i, i % 8] = 1
    sele_h = nc.inline_tensor(sele, name="c_sele")
    eps_h = nc.inline_tensor(np.full((1, 1), LN_EPS, np.float32), name="c_eps")
    one_h = nc.inline_tensor(np.full((128, 1), 1.0, np.float32), name="c_one")
    b27_h = nc.inline_tensor(np.full((128, 1), NEG_LN_EPS12, np.float32), name="c_b27")

    s_ident = sb("s_ident", (128, 128))
    s_ones_r = sb("s_ones_r", (1, 128))
    s_ones_c = sb("s_ones_c", (128, 1))
    s_selk = sb("s_selk", (16, 2))
    s_selkT = sb("s_selkT", (2, 16))
    s_sele = sb("s_sele", (16, 8))
    s_rank = sb("s_rank", (1, 1), U32)
    s_rankf = sb("s_rankf", (16, 1))
    s_nonce = sb("s_nonce", (1, 8), U32)
    s_flags = sb("s_flags", (1, 16), U32)
    s_convw = [sb(f"s_convw{k}", (128, 16)) for k in range(4)]
    s_convb = sb("s_convb", (128, 16))
    s_bdt = sb("s_bdt", (128, 16))
    s_dvec = sb("s_dvec", (128, 16))
    s_gcol = sb("s_gcol", (128, 8))
    s_glog = sb("s_glog", (128, 128))
    s_g1 = sb("s_g1", (128, 8))
    s_g2 = sb("s_g2", (128, 8))
    s_srt = sb("s_srt", (128, 8))
    s_z8 = sb("s_z8", (128, 8))
    s_packT = sb("s_packT", (128, 18))
    s_cnt2 = sb("s_cnt2", (2, 16))
    s_cntT = sb("s_cntT", (16, 2))
    s_off = sb("s_off", (16, 1))
    s_gw = [sb(f"s_gw{i}", (128, NE), BF16) for i in range(8)]

    s_xA = sb("s_xA", (128, 8 * TC))
    s_x = [s_xA[:, i * TC:(i + 1) * TC] for i in range(8)]
    XR = [f"sx{i}" for i in range(8)]
    s_xnA = sb("s_xnA", (128, 8 * TC), BF16)
    s_xn = [s_xnA[:, i * TC:(i + 1) * TC] for i in range(8)]
    XNR = [f"sxn{i}" for i in range(8)]
    s_di = s_xnA[0:64, 12288:16384]
    DIR = ["sxn6", "sxn7"]
    s_tt = sb("s_tt", (128, TC))
    s_stage = [sb(f"s_stage{i}", (128, D)) for i in range(2)]
    STGR = ["stg0", "stg1"]
    s_rowsA = sb("s_rowsA", (128, TC))
    s_rowsB = sb("s_rowsB", (128, TC))
    ohf = s_rowsA[0:16, :]
    comb = s_rowsA[64:72, :]
    r_mean = s_rowsB[0:1, :]
    r_tmp = s_rowsA[0:1, :]
    wrow = s_rowsB[0:2, :]
    s_off0 = sb("s_off0", (16, 1))
    s_eps = sb("s_eps", (1, 1))
    s_one11 = sb("s_one11", (128, 1))
    s_b27 = sb("s_b27", (128, 1))
    s_rankc = sb("s_rankc", (128, 1))
    s_dcol = sb("s_dcol", (128, 1))
    s_wx = [sb(f"s_wx{k}", (128, 96), BF16) for k in range(2)]
    s_wdtloc = sb("s_wdtloc", (64, 1024), BF16)
    s_bdtloc = sb("s_bdtloc", (128, 8))
    s_hx = sb("s_hx", (128, 8))
    s_resst = sb("s_resst", (128, 512), BF16)
    s_xiex = sb("s_xiex", (128, 12288), BF16)

    s_bigB = sb("s_bigB", (128, 24576), BF16)
    bigB_f32 = s_bigB[:, :].bitcast(F32)
    BB = [f"bb{i}" for i in range(12)]
    def bslot(i, w=1):
        return s_bigB[:, i * 2048:(i + w) * 2048]
    s_wgq = [bslot(0, 2)[:, k * 512:(k + 1) * 512] for k in range(8)]
    WGR = [BB[0] if k < 4 else BB[1] for k in range(8)]
    s_wuq = [bslot(2, 2)[:, k * 512:(k + 1) * 512] for k in range(8)]
    WUR = [BB[2] if k < 4 else BB[3] for k in range(8)]
    s_wdq = [bslot(4, 2)[:, k * 1024:(k + 1) * 1024] for k in range(4)]
    WDR = [BB[4] if k < 2 else BB[5] for k in range(4)]
    s_p = [bslot(6 + k) for k in range(4)]
    PR = [BB[6 + k] for k in range(4)]
    s_sg = sb("s_sg", (128, 256), BF16)
    s_combbc = sb("s_combbc", (128, 256), BF16)

    pbank = [kb.es.enter_context(nc.psum_tensor(f"pbank{i}", [128, 512], F32))
             for i in range(8)]
    PB_R = [f"pb{i}" for i in range(8)]

    state = {"rank": None, "bk": 0, "pollbase": 0}

    for h_, d_ in ((ident_h, s_ident), (ones_r, s_ones_r), (ones_c, s_ones_c),
                   (selk_h, s_selk), (selkT_h, s_selkT), (sele_h, s_sele)):
        kb.dma("sync", d_[:], h_.ap(), writes=[d_], tag="consts", skey="consts")
    kb.dma("sync", s_eps[:], eps_h.ap(), writes=[s_eps], tag="consts", skey="consts")
    kb.dma("sync", s_one11[:], one_h.ap(), writes=[s_one11], tag="consts", skey="consts")
    kb.dma("sync", s_b27[:], b27_h.ap(), writes=[s_b27], tag="consts", skey="consts")
    kb.dma("sync", s_rankc[:], rankc_in, writes=[s_rankc], tag="consts", skey="consts")
    kb.dma("sync", s_rank[:], rank_in, writes=[s_rank], tag="consts", skey="consts")
    kb.dma("sync", s_rankf[:], rankf_in, writes=[s_rankf], tag="consts", skey="consts")
    kb.dma("sync", s_nonce[:], nonces, writes=[s_nonce], tag="consts", skey="consts")
    for k in range(4):
        kb.dma("sync", s_convw[k][:], convw_c[k], writes=[s_convw[k]], tag="consts", skey="consts")
    kb.dma("sync", s_convb[:], convb_c, writes=[s_convb], tag="consts", skey="consts")
    kb.dma("sync", s_bdt[:], bdt_c, writes=[s_bdt], tag="consts", skey="consts")
    kb.dma("sync", s_dvec[:], dvec_c, writes=[s_dvec], tag="consts", skey="consts")
    for dc in range(8):
        kb.dma("gpsimd", s_gw[dc][:], gw_t[dc * 128:(dc + 1) * 128, :],
               writes=[s_gw[dc]], tag="gw", skey="gws")

    def _ld_rank(g):
        state["rank"] = g.value_load(s_rank[0:1, 0:1])
        state["pollreg"] = g.alloc_register("pollcnt")
        g.reg_mov(state["pollreg"], 0)
        for rn in ("b_want", "b_f0", "b_f1", "b_nd"):
            state[rn] = g.alloc_register(rn)
        return None
    kb.add("gpsimd", _ld_rank, reads=[s_rank], tag="rank")

    def barrier(extra_reads=()):
        k = state["bk"]; state["bk"] += 1
        def _go(g, k=k):
            rank = state["rank"]
            g.dma_start(out=sh_flag[0:1, 2 * k + 0:2 * k + 2][0:1, bass.ts(rank, 1)],
                        in_=s_nonce[0:1, k:k + 1]).then_inc(sems["flag"], 16)
            g.wait_ge(sems["flag"], 16 * (k + 1))
            want, f0, f1, nd = (state["b_want"], state["b_f0"],
                                state["b_f1"], state["b_nd"])
            g.reg_load(want, s_nonce[0:1, k:k + 1])
            pc = state["pollreg"]
            g.reg_mov(nd, 1)
            with g.While(nd):
                g.reg_add(pc, pc, 16)
                g.dma_start(out=s_flags[0:1, 2 * k:2 * k + 2],
                            in_=sh_flag[0:1, 2 * k:2 * k + 2]).then_inc(sems["poll"], 16)
                g.wait_ge(sems["poll"], pc)
                g.reg_load(f0, s_flags[0:1, 2 * k:2 * k + 1])
                g.reg_load(f1, s_flags[0:1, 2 * k + 1:2 * k + 2])
                g.reg_alu(f0, f0, want, OP.subtract)
                g.reg_alu(f1, f1, want, OP.subtract)
                g.reg_alu(nd, f0, f1, OP.bitwise_or)
            return g.nop()
        return kb.add("gpsimd", _go, reads=[s_nonce, s_flags, *extra_reads],
                      writes=["barrier", s_flags], tag=f"barrier{k}")

    for tt in range(16):
        stg = s_stage[tt % 2]
        kb.dma("sync", stg[:, :], x_tm[tt * 128:(tt + 1) * 128, :],
               writes=[STGR[tt % 2]], tag="xload")
        for dc in range(8):
            bank = pbank[dc % 4]
            kb.add("tensor",
                   lambda g, stg=stg, dc=dc, bank=bank:
                   g.transpose(out=bank[0:128, 0:128],
                               in_=stg[:, dc * 128:(dc + 1) * 128], identity=s_ident[:]),
                   reads=[STGR[tt % 2], s_ident], writes=[PB_R[dc % 4]], tag="xtp")
            kb.add("vector",
                   lambda g, tt=tt, dc=dc, bank=bank:
                   g.tensor_copy(out=s_x[dc][:, tt * 128:(tt + 1) * 128],
                                 in_=bank[0:128, 0:128]),
                   reads=[PB_R[dc % 4]], writes=[XR[dc]], tag="xev")

    def layernorm():
        for qt in range(4):
            sl = slice(qt * 512, (qt + 1) * 512)
            for dc in range(8):
                kb.add("scalar",
                       lambda g, dc=dc, sl=sl:
                       g.activation(out=s_tt[:, 0:512], in_=s_x[dc][:, sl], func=AF.Square),
                       reads=[XR[dc]], writes=["tt_a"], tag="ln_sq")
                kb.add("tensor",
                       lambda g, dc=dc, sl=sl:
                       g.matmul(out=pbank[4][0:1, :], lhsT=s_ones_c[:], rhs=s_x[dc][:, sl],
                                start=(dc == 0), stop=(dc == 7)),
                       reads=[XR[dc], s_ones_c], writes=[PB_R[4]], tag="ln_s1")
                kb.add("tensor",
                       lambda g, dc=dc:
                       g.matmul(out=pbank[7][0:1, :], lhsT=s_ones_c[:], rhs=s_tt[:, 0:512],
                                start=(dc == 0), stop=(dc == 7)),
                       reads=["tt_a", s_ones_c], writes=[PB_R[7]], tag="ln_s2")
            kb.add("vector",
                   lambda g, sl=sl: g.tensor_scalar(out=r_mean[0:1, sl], in0=pbank[4][0:1, :],
                                                    scalar1=1.0 / D, scalar2=None, op0=OP.mult),
                   reads=[PB_R[4]], writes=["rB0"], tag="ln_mean")
            kb.add("vector",
                   lambda g, sl=sl: g.tensor_scalar(out=r_tmp[0:1, sl], in0=pbank[7][0:1, :],
                                                    scalar1=1.0 / D, scalar2=None, op0=OP.mult),
                   reads=[PB_R[7]], writes=["rA"], tag="ln_ssq")
            kb.add("scalar",
                   lambda g, sl=sl: g.activation(out=pbank[7][0:1, 0:512],
                                                 in_=r_mean[0:1, sl], func=AF.Square),
                   reads=["rB0"], writes=[PB_R[7]], tag="ln_m2")
            kb.add("vector",
                   lambda g, sl=sl: g.tensor_sub(out=r_tmp[0:1, sl], in0=r_tmp[0:1, sl],
                                                 in1=pbank[7][0:1, 0:512]),
                   reads=[PB_R[7], "rA"], writes=["rA"], tag="ln_var")
            kb.add("scalar",
                   lambda g, sl=sl: g.activation(out=pbank[7][0:1, 0:512], in_=r_tmp[0:1, sl],
                                                 func=AF.Ln, bias=s_eps[0:1, 0:1]),
                   reads=["rA", s_eps], writes=[PB_R[7]], tag="ln_ln")
            kb.add("scalar",
                   lambda g, sl=sl: g.activation(out=r_tmp[0:1, sl], in_=pbank[7][0:1, 0:512],
                                                 func=AF.Exp, scale=-0.5),
                   reads=[PB_R[7]], writes=["rA"], tag="ln_exp")
            kb.add("tensor",
                   lambda g, sl=sl: g.matmul(out=pbank[5][:, :], lhsT=s_ones_r[:],
                                             rhs=r_mean[0:1, sl], start=True, stop=True),
                   reads=["rB0", s_ones_r], writes=[PB_R[5]], tag="ln_bc1")
            kb.add("tensor",
                   lambda g, sl=sl: g.matmul(out=pbank[6][:, :], lhsT=s_ones_r[:],
                                             rhs=r_tmp[0:1, sl], start=True, stop=True),
                   reads=["rA", s_ones_r], writes=[PB_R[6]], tag="ln_bc2")
            for dc in range(8):
                kb.add("vector",
                       lambda g, dc=dc, sl=sl:
                       g.tensor_sub(out=s_tt[:, 512:1024], in0=s_x[dc][:, sl],
                                    in1=pbank[5][:, :]),
                       reads=[XR[dc], PB_R[5]], writes=["tt_b"], tag="ln_xn1")
                kb.add("vector",
                       lambda g, dc=dc, sl=sl:
                       g.tensor_mul(out=s_xn[dc][:, sl], in0=s_tt[:, 512:1024],
                                    in1=pbank[6][:, :]),
                       reads=["tt_b", PB_R[6]], writes=[XNR[dc]], tag="ln_xn2")

    def gating(which):
        shc = sh_cnt[which]
        for qt in range(4):
            sl = slice(qt * 512, (qt + 1) * 512)
            for dc in range(8):
                kb.add("tensor",
                       lambda g, dc=dc, sl=sl:
                       g.matmul(out=pbank[7][0:8, :], lhsT=s_gw[dc][:], rhs=s_xn[dc][:, sl],
                                start=(dc == 0), stop=(dc == 7)),
                       reads=[s_gw[dc], XNR[dc]], writes=[PB_R[7]], tag="g_log")
            kb.add("vector",
                   lambda g, sl=sl: g.tensor_copy(out=ohf[0:8, sl], in_=pbank[7][0:8, :]),
                   reads=[PB_R[7]], writes=["rA"], tag="g_ev")
        for tt in range(16):
            kb.add("tensor",
                   lambda g, tt=tt:
                   g.transpose(out=pbank[7][0:128, 0:8],
                               in_=ohf[0:8, tt * 128:(tt + 1) * 128],
                               identity=s_ident[0:8, 0:8]),
                   reads=["rA", s_ident], writes=[PB_R[7]], tag="g_tp")
            kb.add("vector",
                   lambda g, tt=tt:
                   g.tensor_copy(out=s_glog[:, tt * 8:(tt + 1) * 8], in_=pbank[7][0:128, 0:8]),
                   reads=[PB_R[7]], writes=[s_glog], tag="g_tpev")
        kb.add("vector", lambda g: g.memset(s_z8[:], 0.0), writes=[s_z8], tag="g_z")
        V = "vector"
        for tt in range(16):
            lg = s_glog[:, tt * 8:(tt + 1) * 8]
            kb.add(V, lambda g, lg=lg: g.reduce_max(out=s_gcol[:, 0:1], in_=lg, axis=AX.X),
                   reads=[s_glog], writes=[s_gcol], tag="gm")
            kb.add(V, lambda g, lg=lg: g.tensor_scalar(out=s_g1[:], in0=lg,
                                                       scalar1=s_gcol[:, 0:1], scalar2=None,
                                                       op0=OP.subtract),
                   reads=[s_glog], sreads=[s_gcol], writes=[s_g1], tag="gs")
            kb.add("scalar", lambda g: g.activation(out=s_g2[:], in_=s_g1[:], func=AF.Exp),
                   reads=[s_g1], writes=[s_g2], tag="ge")
            kb.add(V, lambda g: g.reduce_sum(out=s_gcol[:, 1:2], in_=s_g2[:], axis=AX.X),
                   reads=[s_g2], writes=[s_gcol], tag="gr")
            kb.add(V, lambda g: g.reciprocal(out=s_gcol[:, 2:3], in_=s_gcol[:, 1:2]),
                   reads=[], sreads=[s_gcol], writes=[s_gcol], tag="gi")
            kb.add(V, lambda g: g.tensor_scalar(out=s_g1[:], in0=s_g2[:],
                                                scalar1=s_gcol[:, 2:3], scalar2=None,
                                                op0=OP.mult),
                   reads=[s_g2], sreads=[s_gcol], writes=[s_g1], tag="gp")
            kb.add(V, lambda g: g.max(out=s_srt[:], in_=s_g1[:]),
                   reads=[s_g1], writes=[s_srt], tag="gmx")
            kb.add(V, lambda g: g.tensor_add(out=s_gcol[:, 3:4], in0=s_srt[:, 0:1],
                                             in1=s_srt[:, 1:2]),
                   reads=[], sreads=[s_srt], writes=[s_gcol], tag="gw1")
            kb.add(V, lambda g: g.tensor_scalar(out=s_gcol[:, 3:4], in0=s_gcol[:, 3:4],
                                                scalar1=1e-9, scalar2=None, op0=OP.add),
                   reads=[], sreads=[s_gcol], writes=[s_gcol], tag="gw2")
            kb.add(V, lambda g: g.reciprocal(out=s_gcol[:, 3:4], in_=s_gcol[:, 3:4]),
                   reads=[], sreads=[s_gcol], writes=[s_gcol], tag="gw3")
            kb.add(V, lambda g: g.tensor_scalar(out=s_gcol[:, 3:4], in0=s_gcol[:, 3:4],
                                                scalar1=0.5, scalar2=None, op0=OP.mult),
                   reads=[], sreads=[s_gcol], writes=[s_gcol], tag="gw4")
            kb.add(V, lambda g: g.tensor_scalar(out=s_packT[:, 16:17], in0=s_srt[:, 0:1],
                                                scalar1=s_gcol[:, 3:4], scalar2=None,
                                                op0=OP.mult),
                   reads=[], sreads=[s_srt, s_gcol], writes=[s_packT], tag="gw5")
            kb.add(V, lambda g: g.tensor_scalar(out=s_packT[:, 17:18], in0=s_srt[:, 1:2],
                                                scalar1=s_gcol[:, 3:4], scalar2=None,
                                                op0=OP.mult),
                   reads=[], sreads=[s_srt, s_gcol], writes=[s_packT], tag="gw6")
            kb.add(V, lambda g: g.tensor_scalar(out=s_g2[:], in0=s_g1[:],
                                                scalar1=s_srt[:, 0:1], scalar2=None,
                                                op0=OP.is_equal),
                   reads=[s_g1], sreads=[s_srt], writes=[s_g2], tag="geq0")
            kb.add(V, lambda g: g.tensor_tensor_scan(out=s_packT[:, 0:8], data0=s_g2[:],
                                                     data1=s_z8[:], initial=0.0,
                                                     op0=OP.add, op1=OP.add),
                   reads=[s_z8], sreads=[s_g2], writes=[s_packT], tag="gcs0")
            kb.add(V, lambda g: g.tensor_scalar(out=s_packT[:, 0:8], in0=s_packT[:, 0:8],
                                                scalar1=1.0, scalar2=None, op0=OP.is_equal),
                   reads=[], sreads=[s_packT], writes=[s_packT], tag="gff0a")
            kb.add(V, lambda g: g.tensor_mul(out=s_packT[:, 0:8], in0=s_packT[:, 0:8],
                                             in1=s_g2[:]),
                   reads=[s_g2], sreads=[s_packT], writes=[s_packT], tag="gff0b")
            kb.add(V, lambda g: g.tensor_scalar(out=s_g2[:], in0=s_g1[:],
                                                scalar1=s_srt[:, 1:2], scalar2=None,
                                                op0=OP.is_equal),
                   reads=[s_g1], sreads=[s_srt, s_packT], writes=[s_g2], tag="geq1")
            kb.add(V, lambda g: g.tensor_scalar(out=s_g1[:], in0=s_packT[:, 0:8],
                                                scalar1=-1.0, scalar2=1.0,
                                                op0=OP.mult, op1=OP.add),
                   reads=[], sreads=[s_packT], writes=[s_g1], tag="gnot")
            kb.add(V, lambda g: g.tensor_mul(out=s_g2[:], in0=s_g2[:], in1=s_g1[:]),
                   reads=[], sreads=[s_g1, s_g2], writes=[s_g2], tag="gm1")
            kb.add(V, lambda g: g.tensor_tensor_scan(out=s_g1[:], data0=s_g2[:],
                                                     data1=s_z8[:], initial=0.0,
                                                     op0=OP.add, op1=OP.add),
                   reads=[s_z8], sreads=[s_g2], writes=[s_g1], tag="gcs1")
            kb.add(V, lambda g: g.tensor_scalar(out=s_g1[:], in0=s_g1[:], scalar1=1.0,
                                                scalar2=None, op0=OP.is_equal),
                   reads=[], sreads=[s_g1], writes=[s_g1], tag="gff1a")
            kb.add(V, lambda g: g.tensor_mul(out=s_packT[:, 8:16], in0=s_g1[:], in1=s_g2[:]),
                   reads=[], sreads=[s_g1, s_g2], writes=[s_packT], tag="gff1b")
            kb.add("tensor",
                   lambda g: g.transpose(out=pbank[7][0:16, 0:128], in_=s_packT[:, 0:16],
                                         identity=s_ident[:]),
                   reads=[s_packT, s_ident], writes=[PB_R[7]], tag="g_tp2")
            kb.add("vector",
                   lambda g, tt=tt:
                   g.tensor_copy(out=ohf[:, tt * 128:(tt + 1) * 128], in_=pbank[7][0:16, 0:128]),
                   reads=[PB_R[7]], writes=["rA"], tag="g_tpev2")
            kb.add("tensor",
                   lambda g: g.transpose(out=pbank[6][0:2, 0:128], in_=s_packT[:, 16:18],
                                         identity=s_ident[:]),
                   reads=[s_packT, s_ident], writes=[PB_R[6]], tag="g_tp3")
            kb.add("vector",
                   lambda g, tt=tt:
                   g.tensor_copy(out=wrow[:, tt * 128:(tt + 1) * 128], in_=pbank[6][0:2, 0:128]),
                   reads=[PB_R[6]], writes=["rB0"], tag="g_tpev3")
        rk16 = s_tt[0:16, :]
        TTRES = ["tt_a", "tt_b"]
        kb.add(V, lambda g: g.tensor_tensor_scan(out=rk16[:, :], data0=ohf[0:16, :],
                                                 data1=ohf[0:16, :], initial=0.0,
                                                 op0=OP.add, op1=OP.bypass),
               sreads=["rA"], writes=TTRES, tag="g_rsc")
        kb.add(V, lambda g: g.tensor_copy(out=s_cntT[:, 0:1], in_=rk16[:, TC - 1:TC]),
               sreads=TTRES, writes=[s_cntT], tag="g_cnt")
        def _wcnt(g, shc=shc):
            return g.dma_start(out=shc[bass.ts(state["rank"], 1), 0:16], in_=s_cntT[:, 0:1])
        kb.add("gpsimd", _wcnt, reads=[s_cntT], writes=["shmem"], tag="g_wcnt", dma=True)
        barrier(extra_reads=["shmem"])
        kb.dma("gpsimd", s_cnt2[:, :], shc[:, :], reads=["barrier"], writes=[s_cnt2],
               tag="g_rcnt")
        kb.add("tensor",
               lambda g: g.transpose(out=pbank[7][0:16, 0:2], in_=s_cnt2[0:2, 0:16],
                                     identity=s_ident[0:2, 0:2]),
               reads=[s_cnt2, s_ident], writes=[PB_R[7]], tag="g_tpc")
        kb.add(V, lambda g: g.tensor_copy(out=s_cntT[:, 0:2], in_=pbank[7][0:16, 0:2]),
               reads=[PB_R[7]], writes=[s_cntT], tag="g_cntT")
        kb.add(V, lambda g: g.tensor_scalar(out=s_off0[:, 0:1], in0=s_cntT[:, 0:1],
                                            scalar1=s_rankf[:, 0:1], scalar2=-1.0,
                                            op0=OP.mult, op1=OP.add),
               reads=[s_rankf], sreads=[s_cntT], writes=[s_off0], tag="g_off")
        for qt in range(4):
            sl = slice(qt * 512, (qt + 1) * 512)
            kb.add(V, lambda g, sl=sl: g.tensor_scalar(out=pbank[5][0:16, :],
                                                       in0=rk16[:, sl],
                                                       scalar1=s_off0[:, 0:1], scalar2=None,
                                                       op0=OP.add),
                   reads=TTRES, sreads=[s_off0], writes=[PB_R[5]], tag="g_rk1")
            kb.add(V, lambda g, sl=sl: g.tensor_mul(out=rk16[:, sl], in0=ohf[0:16, sl],
                                                    in1=pbank[5][0:16, :]),
                   reads=["rA", PB_R[5]], writes=TTRES, tag="g_rk2")
        for qt in range(4):
            sl = slice(qt * 512, (qt + 1) * 512)
            kb.add("tensor",
                   lambda g, sl=sl: g.matmul(out=pbank[7][0:2, :], lhsT=s_selk[:, :],
                                             rhs=rk16[:, sl], start=True, stop=True),
                   reads=[s_selk] + TTRES, writes=[PB_R[7]], tag="g_mr")
            kb.add(V, lambda g, sl=sl: g.tensor_scalar(out=pbank[6][0:2, :],
                                                       in0=pbank[7][0:2, :],
                                                       scalar1=float(CAP) - 0.5,
                                                       scalar2=None, op0=OP.is_lt),
                   reads=[PB_R[7]], writes=[PB_R[6]], tag="g_keep1")
            kb.add(V, lambda g, sl=sl: g.tensor_mul(out=wrow[:, sl], in0=wrow[:, sl],
                                                    in1=pbank[6][0:2, :]),
                   reads=["rB0", PB_R[6]], writes=["rB0"], tag="g_keep2")
            kb.add("tensor",
                   lambda g, sl=sl: g.matmul(out=pbank[7][0:16, :], lhsT=s_selkT[:, :],
                                             rhs=wrow[:, sl], start=True, stop=True),
                   reads=[s_selkT, "rB0"], writes=[PB_R[7]], tag="g_wkbc")
            kb.add(V, lambda g, sl=sl: g.tensor_mul(out=rk16[:, sl], in0=ohf[0:16, sl],
                                                    in1=pbank[7][0:16, :]),
                   reads=[PB_R[7], "rA"], writes=TTRES, tag="g_wko")
            kb.add("tensor",
                   lambda g, sl=sl: g.matmul(out=pbank[6][0:8, :], lhsT=s_sele[:, :],
                                             rhs=rk16[:, sl], start=True, stop=True),
                   reads=[s_sele] + TTRES, writes=[PB_R[6]], tag="g_comb")
            kb.add(V, lambda g, sl=sl: g.tensor_copy(out=comb[:, sl], in_=pbank[6][0:8, :]),
                   reads=[PB_R[6]], writes=["comb"], tag="g_combev")
        kb.dma("sync", dr_comb[:, :], comb[:, :], reads=["comb"], writes=["dr_comb"],
               tag="g_spill")

    def ffn_call(which):
        layernorm()
        gating(which)
        if which == 0:
            kb.dma("sync", dbg_comb, comb[:, :], reads=["comb"], writes=["dbg"], tag="dbg")
            kb.dma("sync", dbg_cnt, s_cntT[:, :], reads=[s_cntT], writes=["dbg"], tag="dbg")
            kb.dma("sync", dbg_glog, s_glog[:, :], reads=[s_glog], writes=["dbg"], tag="dbg")
            kb.dma("sync", dbg_ohf, ohf[:, :], reads=["rA"], writes=["dbg"], tag="dbg")
            kb.dma("sync", dbg_xn, s_xn[0][:, :], reads=[XNR[0]], writes=["dbg"], tag="dbg")
            kb.dma("sync", dbg_rkf, s_tt[0:16, :], reads=["tt_a", "tt_b"], writes=["dbg"], tag="dbg")
        groups = [(e, q) for e in range(NE) for q in range(4)]

        def load_gu(e, q):
            for k in range(8):
                kb.dma("gpsimd", s_wgq[k][:],
                       wg_t[e, k * 128:(k + 1) * 128, q * 512:(q + 1) * 512],
                       writes=[WGR[k]], tag="w_gu")
                kb.dma("gpsimd", s_wuq[k][:],
                       wu_t[e, k * 128:(k + 1) * 128, q * 512:(q + 1) * 512],
                       writes=[WUR[k]], tag="w_gu")

        def load_wd(e, q):
            for k in range(4):
                kb.dma("gpsimd", s_wdq[k][:],
                       wd_t[e, q * 512 + k * 128: q * 512 + (k + 1) * 128, :],
                       writes=[WDR[k]], tag="w_d")

        load_gu(*groups[0])
        load_wd(*groups[0])
        for gi, (e, q) in enumerate(groups):
            for t8 in range(8):
                tsl = slice(t8 * 256, (t8 + 1) * 256)
                kb.dma("gpsimd", s_combbc[:],
                       dr_comb[e:e + 1, tsl].broadcast_to([128, 256]),
                       reads=["dr_comb"], writes=[s_combbc], tag="combbc")
                for hc in range(4):
                    gbank, gsl = pbank[4 + (hc % 2)], slice(0, 256)
                    ubank, usl = pbank[6 + (hc % 2)], slice(0, 256)
                    for k in range(8):
                        kb.add("tensor",
                               lambda g, k=k, hc=hc, tsl=tsl, gbank=gbank, gsl=gsl:
                               g.matmul(out=gbank[0:128, gsl],
                                        lhsT=s_wgq[k][:, hc * 128:(hc + 1) * 128],
                                        rhs=s_xn[k][:, tsl], start=(k == 0), stop=(k == 7)),
                               reads=[WGR[k], XNR[k]], writes=[PB_R[4 + (hc % 2)]],
                               tag="mm_g")
                    for k in range(8):
                        kb.add("tensor",
                               lambda g, k=k, hc=hc, tsl=tsl, ubank=ubank, usl=usl:
                               g.matmul(out=ubank[0:128, usl],
                                        lhsT=s_wuq[k][:, hc * 128:(hc + 1) * 128],
                                        rhs=s_xn[k][:, tsl], start=(k == 0), stop=(k == 7)),
                               reads=[WUR[k], XNR[k]], writes=[PB_R[6 + (hc % 2)]],
                               tag="mm_u")
                    kb.add("scalar",
                           lambda g, gbank=gbank, gsl=gsl:
                           g.activation(out=s_sg[:], in_=gbank[0:128, gsl], func=AF.Silu),
                           reads=[PB_R[4 + (hc % 2)]], writes=[s_sg], tag="silu")
                    kb.add("vector",
                           lambda g, ubank=ubank, usl=usl:
                           g.scalar_tensor_tensor(out=s_tt[:, 0:256], in0=ubank[0:128, usl],
                                                  scalar=1.0, in1=s_sg[:],
                                                  op0=OP.mult, op1=OP.mult),
                           reads=[PB_R[6 + (hc % 2)], s_sg], writes=["tt_a"], tag="pmul1")
                    kb.add("vector",
                           lambda g, hc=hc, tsl=tsl:
                           g.tensor_mul(out=s_p[hc][:, tsl], in0=s_tt[:, 0:256],
                                        in1=s_combbc[:]),
                           reads=[s_combbc], sreads=["tt_a"], writes=[PR[hc]], tag="pmul2")
            if gi + 1 < len(groups):
                load_gu(*groups[gi + 1])
            for t8 in range(8):
                tsl = slice(t8 * 256, (t8 + 1) * 256)
                for dc in range(8):
                    bank = pbank[dc % 4]
                    bsl = slice((dc // 4) * 256, (dc // 4) * 256 + 256)
                    for hc in range(4):
                        kb.add("tensor",
                               lambda g, hc=hc, dc=dc, tsl=tsl, bank=bank, bsl=bsl:
                               g.matmul(out=bank[0:128, bsl],
                                        lhsT=s_wdq[hc][:, dc * 128:(dc + 1) * 128],
                                        rhs=s_p[hc][:, tsl], start=(hc == 0), stop=(hc == 3)),
                               reads=[WDR[hc], PR[hc]], writes=[PB_R[dc % 4]],
                               tag="mm_d")
                    kb.add("vector",
                           lambda g, dc=dc, tsl=tsl, bank=bank, bsl=bsl:
                           g.tensor_add(out=s_x[dc][:, tsl], in0=s_x[dc][:, tsl],
                                        in1=bank[0:128, bsl]),
                           reads=[PB_R[dc % 4], XR[dc]], writes=[XR[dc]], tag="dn_add")
            if gi + 1 < len(groups):
                load_wd(*groups[gi + 1])

    def mamba():
        V = "vector"
        layernorm()
        for i in range(8):
            kb.dma("sync", dr_x[i:i + 1, :].rearrange("one (p c) -> p (one c)", p=128),
                   s_x[i], reads=[XR[i]], writes=[f"drx{i}"], tag="xspill")
        if MAMBA_CUT <= 0:
            return
        xA_b = s_xA[:, :].bitcast(BF16)
        xi_t = [bslot(j) for j in range(12)] + \
               [s_xiex[:, j2 * 2048:(j2 + 1) * 2048] for j2 in range(4)]
        XIRl = [[BB[j]] for j in range(12)] + [["xe0"], ["xe1"], ["xe2"], ["xe3"]]
        winr = [s_xiex[:, 8192:10240], s_xiex[:, 10240:12288]]
        WINR = [["xe4"], ["xe5"]]
        def win_chunk(j):
            p = j % 2
            kb.dma("gpsimd", winr[p][:, 0:1024],
                   w_in[:, j * 128:(j + 1) * 128].rearrange("(k p) c -> p k c", p=128),
                   reads=WINR[p], writes=WINR[p], tag="w_in_ld")
            for t8 in range(4):
                tsl = slice(t8 * 512, (t8 + 1) * 512)
                bank = pbank[4 + t8 % 2]
                for k in range(8):
                    kb.add("tensor",
                           lambda g, k=k, tsl=tsl, bank=bank, p=p:
                           g.matmul(out=bank[0:128, 0:512],
                                    lhsT=winr[p][:, k * 128:(k + 1) * 128],
                                    rhs=s_xn[k][:, tsl], start=(k == 0), stop=(k == 7)),
                           reads=WINR[p] + [XNR[k]], writes=[PB_R[4 + t8 % 2]], tag="mm_win")
                if j < 16:
                    kb.add(V, lambda g, j=j, tsl=tsl, bank=bank:
                           g.tensor_copy(out=xi_t[j][:, tsl], in_=bank[0:128, 0:512]),
                           reads=[PB_R[4 + t8 % 2]], writes=XIRl[j], tag="xi_ev")
                else:
                    st = s_resst[:, 0:512]
                    kb.add(V, lambda g, tsl=tsl, bank=bank, st=st:
                           g.tensor_copy(out=st, in_=bank[0:128, 0:512]),
                           reads=[PB_R[4 + t8 % 2]], writes=["resst"], tag="res_ev")
                    kb.dma("sync",
                           dr_res[j - 16:j - 15, :]
                           .rearrange("one (p c) -> p (one c)", p=128)[:, t8 * 512:(t8 + 1) * 512],
                           st, reads=["resst"], writes=["drres"], tag="res_sp")
        for j in range(16):
            win_chunk(j)
        if MAMBA_CUT <= 1:
            return
        kb.add("gpsimd",
               lambda g: g.dma_start(
                   out=sh_halo[0:1536, :][:, bass.ts(state["rank"], 3)]
                   .rearrange("(c p) w -> p c w", p=128),
                   in_=s_bigB[:, 0:24576].rearrange("p (c l) -> p c l",
                                                    l=2048)[:, :, 2045:2048]),
               reads=[BB[j] for j in range(12)], writes=["shmem2"], tag="halo_w1",
               dma=True)
        kb.add("gpsimd",
               lambda g: g.dma_start(
                   out=sh_halo[1536:2048, :][:, bass.ts(state["rank"], 3)]
                   .rearrange("(c p) w -> p c w", p=128),
                   in_=s_xiex[:, 0:8192].rearrange("p (c l) -> p c l",
                                                   l=2048)[:, :, 2045:2048]),
               reads=["xe0", "xe1", "xe2", "xe3"], writes=["shmem2"], tag="halo_w2",
               dma=True)
        barrier(extra_reads=["shmem2"])
        if MAMBA_CUT <= 2:
            return
        for j in range(16):
            win_chunk(16 + j)
            kb.dma("gpsimd", s_hx[:, 0:3], sh_halo[j * 128:(j + 1) * 128, 0:3],
                   reads=["barrier"], writes=[s_hx], tag="halo_r")
            kb.add(V, lambda g: g.tensor_scalar(out=s_hx[:, 0:3], in0=s_hx[:, 0:3],
                                                scalar1=s_rankc[:, 0:1], scalar2=None,
                                                op0=OP.mult),
                   reads=[s_rankc], sreads=[s_hx], writes=[s_hx], tag="halo_m")
            kb.add(V, lambda g, j=j: g.tensor_copy(out=s_hx[:, 3:6], in_=xi_t[j][:, 0:3]),
                   reads=XIRl[j], writes=[s_hx], tag="halo_c")
            kb.add(V, lambda g, j=j: g.tensor_scalar(out=s_tt[:, 3:2048],
                                                     in0=xi_t[j][:, 0:2045],
                                                     scalar1=s_convw[0][:, j:j + 1],
                                                     scalar2=None, op0=OP.mult),
                   reads=XIRl[j] + [s_convw[0]], writes=["tt_a", "tt_b"], tag="cv0")
            for k in range(1, 4):
                kb.add(V, lambda g, j=j, k=k:
                       g.scalar_tensor_tensor(out=s_tt[:, 3:2048], in0=xi_t[j][:, k:2045 + k],
                                              scalar=s_convw[k][:, j:j + 1],
                                              in1=s_tt[:, 3:2048], op0=OP.mult, op1=OP.add),
                       reads=XIRl[j] + [s_convw[k]], sreads=["tt_a", "tt_b"],
                       writes=["tt_a", "tt_b"], tag="cvk")
            kb.add(V, lambda g, j=j: g.tensor_scalar(out=s_tt[:, 0:3], in0=s_hx[:, 0:3],
                                                     scalar1=s_convw[0][:, j:j + 1],
                                                     scalar2=None, op0=OP.mult),
                   reads=[s_convw[0]], sreads=[s_hx], writes=["tt_a"], tag="cvh0")
            for k in range(1, 4):
                kb.add(V, lambda g, j=j, k=k:
                       g.scalar_tensor_tensor(out=s_tt[:, 0:3], in0=s_hx[:, k:k + 3],
                                              scalar=s_convw[k][:, j:j + 1],
                                              in1=s_tt[:, 0:3], op0=OP.mult, op1=OP.add),
                       reads=[s_convw[k]], sreads=[s_hx, "tt_a"], writes=["tt_a"], tag="cvhk")
            kb.add(V, lambda g, j=j: g.tensor_scalar(out=s_tt[:, 0:2048], in0=s_tt[:, 0:2048],
                                                     scalar1=s_convb[:, j:j + 1],
                                                     scalar2=None, op0=OP.add),
                   reads=[s_convb], sreads=["tt_a", "tt_b"], writes=["tt_a", "tt_b"],
                   tag="cvb")
            kb.add("scalar", lambda g, j=j: g.activation(out=xi_t[j][:, :], in_=s_tt[:, 0:2048],
                                                         func=AF.Silu),
                   reads=["tt_a", "tt_b"], writes=XIRl[j], tag="cv_silu")

        kb.add("gpsimd",
               lambda g: g.dma_start(
                   out=sh_xis[0:1536, :][:, bass.ts(state["rank"], TC)]
                   .rearrange("(c p) l -> p c l", p=128),
                   in_=s_bigB[:, 0:24576]),
               reads=[BB[j] for j in range(12)], writes=["shmem3"], tag="xis_w1", dma=True)
        kb.add("gpsimd",
               lambda g: g.dma_start(
                   out=sh_xis[1536:2048, :][:, bass.ts(state["rank"], TC)]
                   .rearrange("(c p) l -> p c l", p=128),
                   in_=s_xiex[:, 0:8192]),
               reads=["xe0", "xe1", "xe2", "xe3"], writes=["shmem3"], tag="xis_w2", dma=True)
        for t8 in range(4):
            tsl = slice(t8 * 512, (t8 + 1) * 512)
            for k in range(16):
                kb.dma("gpsimd", s_wx[k % 2][:], w_x[k * 128:(k + 1) * 128, :],
                       writes=[f"swx{k % 2}"], tag="wx_ld")
                kb.add("tensor",
                       lambda g, k=k, tsl=tsl:
                       g.matmul(out=pbank[6][0:96, :], lhsT=s_wx[k % 2][:], rhs=xi_t[k][:, tsl],
                                start=(k == 0), stop=(k == 15)),
                       reads=[f"swx{k % 2}"] + XIRl[k], writes=[PB_R[6]], tag="mm_wx")
            kb.add(V, lambda g, tsl=tsl: g.tensor_copy(out=s_tt[0:96, 0:512],
                                                       in_=pbank[6][0:96, :]),
                   reads=[PB_R[6]], writes=["tt_a", "tt_b"], tag="wx_ev")
            kb.dma("gpsimd", dr_xdbl[:, tsl], s_tt[0:96, 0:512],
                   reads=["tt_a", "tt_b"], writes=["drxdbl"], tag="xdbl_st")
        kb.add("gpsimd",
               lambda g: g.dma_start(out=sh_xdbl[:, bass.ts(state["rank"], TC)],
                                     in_=dr_xdbl[:, :]),
               reads=["drxdbl"], writes=["shmem3"], tag="xdbl_w", dma=True)
        barrier(extra_reads=["shmem3"])
        if MAMBA_CUT <= 3:
            return
        kb.dma("gpsimd", dr_bcrows[:, :], sh_xdbl[64:96, :],
               reads=["barrier"], writes=["bcrows"], tag="bc_cast")
        kb.dma("gpsimd", s_di[:, :], sh_xdbl[0:64, :], reads=["barrier"],
               writes=DIR, tag="di_ld")
        kb.add("gpsimd",
               lambda g: g.dma_start(out=dr_uloc[:, :],
                                     in_=sh_xis[bass.ts(state["rank"], 1024), :]),
               reads=["barrier"], writes=["druloc"], tag="uloc", dma=True)
        kb.add("gpsimd",
               lambda g: g.dma_start(out=s_wdtloc[:, :],
                                     in_=w_dt[:, bass.ts(state["rank"], 1024)]),
               writes=[s_wdtloc], tag="wdtloc", dma=True)
        kb.add("gpsimd",
               lambda g: g.dma_start(out=s_bdtloc[:, :],
                                     in_=bdt_c[:, bass.ts(state["rank"], 8)]),
               writes=[s_bdtloc], tag="bdtloc", dma=True)
        SC_delta = s_xA[:, 0:4096]
        SC_pd = s_xA[:, 4096:8192]
        W0P = [s_xA[:, 8192:12288], s_xA[:, 12288:16384]]
        R_d = ["sx0", "sx1"]; R_pd = ["sx2", "sx3"]
        R_W0 = [["sx4", "sx5"], ["sx6", "sx7"]]
        xnA_b = s_xnA[:, :]
        SB_u = xnA_b[:, 0:4096]; R_u = ["sxn0", "sxn1"]
        SB_h = xnA_b[:, 4096:8192]; R_h = ["sxn2", "sxn3"]
        SB_c = xnA_b[:, 8192:12288]; R_c = ["sxn4", "sxn5"]
        SB_bm = bslot(8, 2); R_bm = [BB[8], BB[9]]
        SB_cm = bslot(10, 2); R_cm = [BB[10], BB[11]]
        ABF = bslot(4, 2); R_af = [BB[4], BB[5]]
        W1B = bslot(6, 2); R_w1b = [BB[6], BB[7]]
        YF32 = bigB_f32[:, 0:4096]; R_y = [BB[0], BB[1], BB[2], BB[3]]
        YW0 = S - 1024
        YW = slice(YW0, S)
        for ci in range(8):
            rowsl = lambda t: t[bass.ts(state["rank"], 1024), :]

            for c8 in range(8):
                bank = pbank[c8 % 4]
                csl = slice(c8 * 512, (c8 + 1) * 512)
                kb.add("tensor",
                       lambda g, c8=c8, bank=bank, ci=ci:
                       g.matmul(out=bank[0:128, 0:512],
                                lhsT=s_wdtloc[:, ci * 128:(ci + 1) * 128],
                                rhs=s_di[:, c8 * 512:(c8 + 1) * 512], start=True, stop=True),
                       reads=[s_wdtloc] + DIR, writes=[PB_R[c8 % 4]], tag="mm_dt")
                kb.add("scalar",
                       lambda g, bank=bank, ci=ci:
                       g.activation(out=s_tt[:, 0:512], in_=bank[0:128, 0:512],
                                    func=AF.Identity, bias=s_bdtloc[:, ci:ci + 1]),
                       reads=[PB_R[c8 % 4], s_bdtloc], writes=["tt_a"], tag="sp_b")
                kb.add("scalar",
                       lambda g: g.activation(out=s_tt[:, 512:1024], in_=s_tt[:, 0:512],
                                              func=AF.Abs),
                       sreads=["tt_a"], writes=["tt_a"], tag="sp_abs")
                kb.add("scalar",
                       lambda g: g.activation(out=s_tt[:, 1024:1536], in_=s_tt[:, 512:1024],
                                              func=AF.Exp, scale=-1.0),
                       sreads=["tt_a"], writes=["tt_b"], tag="sp_exp")
                kb.add("scalar",
                       lambda g: g.activation(out=s_tt[:, 512:1024], in_=s_tt[:, 1024:1536],
                                              func=AF.Ln, bias=s_one11[:, 0:1]),
                       reads=[s_one11], sreads=["tt_b"], writes=["tt_a"], tag="sp_ln")
                kb.add("scalar",
                       lambda g: g.activation(out=s_tt[:, 1536:2048], in_=s_tt[:, 0:512],
                                              func=AF.Relu),
                       sreads=["tt_a"], writes=["tt_b"], tag="sp_rl")
                kb.add(V, lambda g, csl=csl: g.tensor_add(out=SC_delta[:, csl],
                                                          in0=s_tt[:, 512:1024],
                                                          in1=s_tt[:, 1536:2048]),
                       sreads=["tt_a", "tt_b"], writes=R_d, tag="sp_add")
            kb.add(V, lambda g: g.tensor_tensor_scan(out=SC_pd[:, :], data0=SC_delta[:, :],
                                                     data1=SC_delta[:, :], initial=0.0,
                                                     op0=OP.add, op1=OP.bypass),
                   reads=R_d, writes=R_pd, tag="pd_scan")
            kb.add(V, lambda g: g.tensor_copy(out=s_dcol[:, 0:1], in_=SC_pd[:, 4095:4096]),
                   sreads=R_pd, writes=[s_dcol], tag="dsum")
            kb.add(V, lambda g: g.tensor_scalar(out=SC_pd[:, :], in0=SC_pd[:, :],
                                                scalar1=s_dcol[:, 0:1], scalar2=None,
                                                op0=OP.subtract),
                   sreads=[s_dcol] + R_pd, writes=R_pd, tag="m_sub")
            kb.dma("gpsimd", SB_u[:, :], dr_uloc[ci * 128:(ci + 1) * 128, :],
                   reads=["druloc"], writes=R_u, tag="u_ld")
            kb.add(V, lambda g: g.tensor_mul(out=ABF[:, :], in0=SC_delta[:, :],
                                             in1=SB_u[:, :]),
                   reads=R_d + R_u, writes=R_af, tag="a_mul")
            for n in range(NST):
                kb.add("scalar",
                       lambda g, n=n: g.activation(out=W0P[n % 2][:, :], in_=SC_delta[:, :],
                                                   func=AF.Exp, scale=-(n + 1.0)),
                       reads=R_d, writes=R_W0[n % 2], tag="sc_exp")
                kb.add("scalar",
                       lambda g, n=n: g.activation(out=SB_c[:, YW],
                                                   in_=SC_pd[:, YW], func=AF.Sigmoid,
                                                   scale=(n + 1.0), bias=s_b27[:, 0:1]),
                       reads=R_pd + [s_b27], writes=R_c, tag="sc_sig")
                kb.dma("gpsimd", SB_bm[:, :],
                       dr_bcrows[n:n + 1, :].broadcast_to([128, S]),
                       reads=["bcrows"], writes=R_bm, tag="bm_rep")
                kb.dma("gpsimd", SB_cm[:, :],
                       dr_bcrows[16 + n:17 + n, :].broadcast_to([128, S]),
                       reads=["bcrows"], writes=R_cm, tag="cm_rep")
                kb.add(V, lambda g: g.tensor_mul(out=W1B[:, :], in0=ABF[:, :],
                                                 in1=SB_bm[:, :]),
                       reads=R_af + R_bm, writes=R_w1b, tag="dbu")
                kb.add(V, lambda g, n=n: g.tensor_tensor_scan(out=SB_h[:, :],
                                                         data0=W0P[n % 2][:, :], data1=W1B[:, :],
                                                         initial=0.0, op0=OP.mult, op1=OP.add),
                       reads=R_W0[n % 2] + R_w1b, writes=R_h, tag="hscan")
                kb.add(V, lambda g: g.tensor_mul(out=s_tt[:, 0:1024],
                                                 in0=SB_h[:, YW], in1=SB_c[:, YW]),
                       reads=R_h + R_c, writes=["tt_a"], tag="hc")
                kb.add(V, lambda g: g.tensor_mul(out=s_tt[:, 1024:2048],
                                                 in0=s_tt[:, 0:1024],
                                                 in1=SB_cm[:, YW]),
                       reads=["tt_a"] + R_cm, writes=["tt_b"], tag="hcc")
                if n == 0:
                    kb.add(V, lambda g: g.memset(YF32[:, 0:YW0], 0.0),
                           writes=R_y, tag="ymem")
                    kb.add(V, lambda g: g.tensor_copy(out=YF32[:, YW],
                                                      in_=s_tt[:, 1024:2048]),
                           reads=["tt_b"] + R_y, writes=R_y, tag="y0")
                else:
                    kb.add(V, lambda g: g.tensor_add(out=YF32[:, YW], in0=YF32[:, YW],
                                                     in1=s_tt[:, 1024:2048]),
                           reads=["tt_b"] + R_y, writes=R_y, tag="yadd")
            kb.dma("gpsimd", dr_yloc[ci * 128:(ci + 1) * 128, :], YF32[:, :],
                   reads=R_y, writes=["dryloc"], tag="y_st")
        kb.add("gpsimd",
               lambda g: g.dma_start(out=sh_y[bass.ts(state["rank"], 1024), :],
                                     in_=dr_yloc[:, :]),
               reads=["dryloc"], writes=["shmem4"], tag="y_wr", dma=True)
        barrier(extra_reads=["shmem4"])
        if MAMBA_CUT <= 4:
            return
        yt_t = [bslot(j) for j in range(12)] + \
               [s_xnA[:, (4 + j) * 2048:(5 + j) * 2048] for j in range(4)]
        YTR = [[BB[j]] for j in range(12)] + [["sxn4"], ["sxn5"], ["sxn6"], ["sxn7"]]
        kb.add("gpsimd",
               lambda g: g.dma_start(out=dr_yepi[:, :],
                                     in_=sh_y[:, bass.ts(state["rank"], TC)]),
               reads=["barrier"], writes=["dryepi"], tag="yepi", dma=True)
        kb.add("gpsimd",
               lambda g: g.dma_start(out=dr_uepi[:, :],
                                     in_=sh_xis[:, bass.ts(state["rank"], TC)]),
               reads=["barrier"], writes=["druepi"], tag="uepi", dma=True)
        for j in range(16):
            ybuf = s_xnA[:, 0:2048]
            ubuf = s_xnA[:, 2048:4096]
            rbuf = s_xnA[:, 4096:6144]
            kb.dma("gpsimd", ybuf, dr_yepi[j * 128:(j + 1) * 128, :],
                   reads=["dryepi"], writes=["sxn0"], tag="y_ld")
            kb.dma("gpsimd", ubuf, dr_uepi[j * 128:(j + 1) * 128, :],
                   reads=["druepi"], writes=["sxn1"], tag="u_ld2")
            kb.dma("sync", rbuf,
                   dr_res[j:j + 1, :].rearrange("one (p c) -> p (one c)", p=128),
                   reads=["drres"], writes=["sxn2"], tag="res_ld")
            kb.add("scalar", lambda g: g.activation(out=s_tt[:, 0:2048],
                                                    in_=rbuf,
                                                    func=AF.Silu),
                   reads=["sxn2"], writes=["tt_a", "tt_b"], tag="swish")
            kb.add(V, lambda g, j=j:
                   g.scalar_tensor_tensor(out=s_xA[:, 0:2048],
                                          in0=ubuf, scalar=s_dvec[:, j:j + 1],
                                          in1=ybuf,
                                          op0=OP.mult, op1=OP.add),
                   reads=["sxn0", "sxn1", s_dvec], writes=["sx0"], tag="ud_y")
            kb.add(V, lambda g, j=j: g.tensor_mul(out=yt_t[j][:, :], in0=s_xA[:, 0:2048],
                                                  in1=s_tt[:, 0:2048]),
                   reads=["sx0", "tt_a", "tt_b"], writes=YTR[j], tag="ytot")
        for i in range(8):
            kb.dma("sync", s_x[i], dr_x[i:i + 1, :].rearrange("one (p c) -> p (one c)", p=128),
                   reads=[f"drx{i}"], writes=[XR[i]], tag="x_rel")
        wor = [s_xiex[:, 8192:10240], s_xiex[:, 10240:12288]]
        WOR = [["xe4"], ["xe5"]]
        for m in range(8):
            p = m % 2
            kb.dma("gpsimd", wor[p][:, 0:2048],
                   w_out[:, m * 128:(m + 1) * 128].rearrange("(k p) c -> p k c", p=128),
                   reads=WOR[p], writes=WOR[p], tag="wo_ld")
            for t8 in range(4):
                tsl = slice(t8 * 512, (t8 + 1) * 512)
                bank = pbank[4 + t8 % 2]
                for k in range(16):
                    kb.add("tensor",
                           lambda g, k=k, tsl=tsl, bank=bank, p=p:
                           g.matmul(out=bank[0:128, 0:512],
                                    lhsT=wor[p][:, k * 128:(k + 1) * 128],
                                    rhs=yt_t[k][:, tsl], start=(k == 0), stop=(k == 15)),
                           reads=WOR[p] + YTR[k], writes=[PB_R[4 + t8 % 2]], tag="mm_wo")
                kb.add(V, lambda g, m=m, tsl=tsl, bank=bank:
                       g.tensor_add(out=s_x[m][:, tsl], in0=s_x[m][:, tsl],
                                    in1=bank[0:128, 0:512]),
                       reads=[PB_R[4 + t8 % 2], XR[m]], writes=[XR[m]], tag="wo_add")

    ffn_call(0)
    if DEV_STAGE >= 2:
        mamba()
    if DEV_STAGE >= 3:
        ffn_call(1)

    for tt in range(16):
        stg = s_stage[tt % 2]
        for dc in range(8):
            bank = pbank[dc % 4]
            kb.add("tensor",
                   lambda g, tt=tt, dc=dc, bank=bank:
                   g.transpose(out=bank[0:128, 0:128], in_=s_x[dc][:, tt * 128:(tt + 1) * 128],
                               identity=s_ident[:]),
                   reads=[XR[dc], s_ident], writes=[PB_R[dc % 4]], tag="otp")
            kb.add("vector",
                   lambda g, dc=dc, bank=bank, stg=stg:
                   g.tensor_copy(out=stg[:, dc * 128:(dc + 1) * 128], in_=bank[0:128, 0:128]),
                   reads=[PB_R[dc % 4]], writes=[STGR[tt % 2]], tag="oev")
        kb.dma("sync", out_tm[tt * 128:(tt + 1) * 128, :], stg[:, :],
               reads=[STGR[tt % 2]], writes=["out"], tag="ostore")

    kb.add("sync", lambda g: g.nop(), reads=["out", "dbg"], tag="fence")

    kb.analyze()
    if _os.environ.get("KBDUMP"):
        want = set(_os.environ["KBDUMP"].split(","))
        for op in kb.ops:
            if op.tag in want:
                print(f"[{op.tag}] eng={op.eng} q={op.queue} idx={op.idx} "
                      f"waits={op.waits} inc={op.need_inc}")
    skeys = sorted({op.skey for op in kb.ops} | set(QUEUES) | {"poll", "flag"})
    if _os.environ.get("KBDUMP"):
        print(f"semaphores: {len(skeys)}")
    assert len(skeys) <= 100, f"too many semaphores: {len(skeys)}"
    sems = {}
    for qn in skeys:
        sems[qn] = kb.es.enter_context(nc.semaphore(f"sem_{qn}"))
    per_eng = kb.materialize(sems)

    with nc.Block() as block:
        def runner(eng):
            def run(g):
                for op, waits in per_eng[eng]:
                    for s, v in waits:
                        g.wait_ge(s, v)
                    inst = op.fn(g)
                    if op.need_inc:
                        if inst is None:
                            inst = g.nop()
                        inst.then_inc(sems[op.skey], op.inc_amount)
            return run

        block.sync(runner("sync"))
        block.gpsimd(runner("gpsimd"))
        block.tensor(runner("tensor"))
        block.vector(runner("vector"))
        block.scalar(runner("scalar"))

    kb.es.close()
    return nc, state


_PROG = None

def _get_prog():
    global _PROG
    if _PROG is None:
        _PROG = build_program()[0]
    return _PROG


def kernel(**inputs):
    i = {k: np.asarray(v) for k, v in inputs.items()}
    x = np.ascontiguousarray(i["L0_attn_x"], np.float32)
    w_in = np.ascontiguousarray(i["L0_attn_W_in"], np.float32)
    conv_w = np.ascontiguousarray(i["L0_attn_conv_w"], np.float32)
    conv_b = np.ascontiguousarray(i["L0_attn_conv_b"], np.float32)
    w_x = np.ascontiguousarray(i["L0_attn_W_x"], np.float32)
    w_dt = np.ascontiguousarray(i["L0_attn_W_dt"], np.float32)
    b_dt = np.ascontiguousarray(i["L0_attn_b_dt"], np.float32)
    d_vec = np.ascontiguousarray(i["L0_attn_D"], np.float32)
    w_out = np.ascontiguousarray(i["L0_attn_W_out"], np.float32)
    gw_t = np.ascontiguousarray(i["L0_ffn_gate_w"].T, np.float32)
    wg_t = np.ascontiguousarray(np.swapaxes(i["L0_ffn_gate_proj_w"], 1, 2), np.float32)
    wu_t = np.ascontiguousarray(np.swapaxes(i["L0_ffn_up_proj_w"], 1, 2), np.float32)
    wd_t = np.ascontiguousarray(np.swapaxes(i["L0_ffn_down_proj_w"], 1, 2), np.float32)

    def cols(v):
        return np.ascontiguousarray(v.reshape(16, 128).T, np.float32)
    convw_c = [cols(conv_w[k]) for k in range(4)]
    convb_c = cols(conv_b)
    bdt_c = cols(b_dt)
    dvec_c = cols(d_vec)

    rng = np.random.default_rng()
    nonce = rng.integers(1, 2**31 - 1, size=(1, 8), dtype=np.int64).astype(np.uint32)

    nc = _get_prog()
    in_maps = []
    for c in range(8):
        b, h = c // 2, c % 2
        im = dict(
            x_tm=np.ascontiguousarray(x[b, h * TC:(h + 1) * TC, :]),
            w_in=w_in, convb_c=convb_c, bdt_c=bdt_c, dvec_c=dvec_c,
            w_x=w_x, w_dt=w_dt, w_out=w_out, gw_t=gw_t,
            wg_t=wg_t, wu_t=wu_t, wd_t=wd_t,
            rank_in=np.array([[h]], np.uint32),
            rankf_in=np.full((16, 1), float(h), np.float32),
            rankc_in=np.full((128, 1), float(h), np.float32),
            nonces=nonce,
        )
        for k in range(4):
            im[f"convw{k}"] = convw_c[k]
        in_maps.append(im)

    res = run_bass_kernel_spmd(nc, in_maps, core_ids=list(range(8)))
    out = np.empty((B, S, D), np.float32)
    for c in range(8):
        b, h = c // 2, c % 2
        out[b, h * TC:(h + 1) * TC, :] = res.results[c]["out_tm"]
    return out


# version 08 (final)
# speedup vs baseline: 1.4317x; 1.4317x over previous
import numpy as np
import ml_dtypes
import concourse.bass as bass
import concourse.mybir as mybir
from concourse.bass_utils import run_bass_kernel_spmd
from contextlib import ExitStack

F32 = mybir.dt.float32
BF16 = mybir.dt.bfloat16
U32 = mybir.dt.uint32
AF = mybir.ActivationFunctionType
OP = mybir.AluOpType
AX = mybir.AxisListType

B, S, D = 4, 4096, 1024
TC = 2048
DIN = 2048
NST = 16
DTR = 64
HF = 2048
NE = 8
CAP = 2560
LN_EPS = 1e-5
NEG_LN_EPS12 = 27.631021115928547

import os as _os
DEV_STAGE = int(_os.environ.get("DEV_STAGE", "3"))
MAMBA_CUT = int(_os.environ.get("MAMBA_CUT", "9"))

ENGS = ("sync", "gpsimd", "tensor", "vector", "scalar")
QUEUES = ENGS + ("dsync", "dgpsimd")


class Op:
    __slots__ = ("eng", "queue", "fn", "reads", "writes", "sreads", "tag", "idx",
                 "need_inc", "waits", "inc_amount", "skey", "sidx")

    def __init__(self, eng, queue, fn, reads, writes, tag, inc_amount):
        self.eng = eng
        self.queue = queue
        self.fn = fn
        self.reads = reads
        self.writes = writes
        self.sreads = ()
        self.tag = tag
        self.idx = None
        self.need_inc = False
        self.waits = {}
        self.inc_amount = inc_amount


class KB:
    def __init__(self):
        self.nc = bass.Bass("TRN2", target_bir_lowering=False, debug=False,
                            num_devices=8, dynamic_dma_scratch_size=16384)
        self.es = ExitStack()
        self.ops = []

    def sb(self, name, shape, dt=F32):
        return self.es.enter_context(self.nc.sbuf_tensor(name, list(shape), dt))

    @staticmethod
    def _key(r):
        if isinstance(r, str):
            return r
        n = getattr(r, "name", None)
        if n is not None:
            return n
        t = getattr(r, "tensor", None)
        if t is not None:
            return t.name
        raise TypeError(f"bad resource {r!r}")

    def add(self, eng, fn, reads=(), writes=(), sreads=(), tag="", dma=False,
            skey=None):
        queue = ("d" + eng) if dma else eng
        amount = 16 if dma else 1
        op = Op(eng, queue, fn,
                tuple(self._key(r) for r in reads) + tuple(self._key(r) for r in sreads),
                tuple(self._key(r) for r in writes), tag, amount)
        op.sreads = tuple(self._key(r) for r in sreads)
        if dma:
            op.skey = skey or (op.writes[0] if op.writes else queue)
        else:
            op.skey = queue
        self.ops.append(op)
        return op

    def dma(self, eng, out_ap, in_ap, reads=(), writes=(), tag="", skey=None, **kw):
        return self.add(
            eng,
            lambda g, o=out_ap, i=in_ap, kw=kw: g.dma_start(out=o, in_=i, **kw),
            reads=reads, writes=writes, tag=tag, dma=True, skey=skey)

    def analyze(self):
        counters = {q: 0 for q in QUEUES}
        scount = {}
        cur_last = {}
        last_writer = {}
        readers = {}
        for op in self.ops:
            counters[op.queue] += 1
            op.idx = counters[op.queue]
            scount[op.skey] = scount.get(op.skey, 0) + 1
            op.sidx = scount[op.skey]
            if op.queue.startswith("d"):
                op.need_inc = True
            deps = []
            for r in op.reads:
                w = last_writer.get(r)
                if w is not None:
                    deps.append((w, r))
            for w_res in op.writes:
                w = last_writer.get(w_res)
                if w is not None:
                    deps.append((w, w_res))
                deps.extend((x, w_res) for x in readers.get(w_res, ()))
            sread_set = set(op.sreads)
            for d, dres in deps:
                if d is op:
                    continue
                if d.queue == op.queue and d.queue.startswith("d"):
                    pass
                elif d.eng == op.eng and d.queue == d.eng:
                    if op.eng == "tensor":
                        continue
                    if dres not in sread_set and (op.idx - d.idx) > 4:
                        continue
                d.need_inc = True
                if d.queue.startswith("d"):
                    tgt = cur_last[d.skey]
                    cur = op.waits.get(d.skey, 0)
                    op.waits[d.skey] = max(cur, tgt.sidx)
                else:
                    cur = op.waits.get(d.queue, 0)
                    op.waits[d.queue] = max(cur, d.idx)
            cur_last[op.skey] = op
            for r in op.reads:
                readers.setdefault(r, []).append(op)
            for w_res in op.writes:
                last_writer[w_res] = op
                readers[w_res] = []

    def materialize(self, sems):
        inc_prefix = {}
        run = {}
        for op in self.ops:
            if op.need_inc:
                run[op.skey] = run.get(op.skey, 0) + op.inc_amount
            inc_prefix.setdefault(op.skey, {})[op.sidx] = run.get(op.skey, 0)
        per_eng = {e: [] for e in ENGS}
        for op in self.ops:
            waits = []
            for q, idx in sorted(op.waits.items()):
                val = inc_prefix[q].get(idx)
                assert val is not None
                waits.append((sems[q], val))
            per_eng[op.eng].append((op, waits))
        return per_eng


def build_program():
    kb = KB()
    nc = kb.nc
    sb = kb.sb

    def ein(name, shape, dt=F32):
        return nc.dram_tensor(name, list(shape), dt, kind="ExternalInput").ap()

    x_tm = ein("x_tm", (TC, D))
    w_in = ein("w_in", (D, 2 * DIN))
    convw_c = [ein(f"convw{k}", (128, 16)) for k in range(4)]
    convb_c = ein("convb_c", (128, 16))
    bdt_c = ein("bdt_c", (128, 16))
    dvec_c = ein("dvec_c", (128, 16))
    w_x = ein("w_x", (DIN, 96))
    w_dt = ein("w_dt", (DTR, DIN))
    w_out = ein("w_out", (DIN, D))
    gw_t = ein("gw_t", (D, NE))
    wg_t = ein("wg_t", (NE, D, HF))
    wu_t = ein("wu_t", (NE, D, HF))
    wd_t = ein("wd_t", (NE, HF, D))
    rank_in = ein("rank_in", (1, 1), U32)
    rankf_in = ein("rankf_in", (16, 1))
    rankc_in = ein("rankc_in", (128, 1))
    nonces = ein("nonces", (1, 8), U32)

    out_tm = nc.dram_tensor("out_tm", [TC, D], F32, kind="ExternalOutput").ap()
    dbg_comb = nc.dram_tensor("dbg_comb", [NE, TC], F32, kind="ExternalOutput").ap()
    dbg_cnt = nc.dram_tensor("dbg_cnt", [16, 2], F32, kind="ExternalOutput").ap()
    dbg_glog = nc.dram_tensor("dbg_glog", [128, 128], F32, kind="ExternalOutput").ap()
    dbg_ohf = nc.dram_tensor("dbg_ohf", [16, TC], F32, kind="ExternalOutput").ap()
    dbg_xn = nc.dram_tensor("dbg_xn", [128, TC], BF16, kind="ExternalOutput").ap()
    dbg_rkf = nc.dram_tensor("dbg_rkf", [16, TC], F32, kind="ExternalOutput").ap()

    shared = lambda n, sh, dt=F32: nc.dram_tensor(n, list(sh), dt, addr_space="Shared").ap()
    sh_flag = shared("sh_flag", (1, 16), U32)
    sh_cnt = [shared(f"sh_cnt{i}", (2, 16)) for i in range(2)]
    if DEV_STAGE >= 2:
        sh_halo = shared("sh_halo", (DIN, 6))
        sh_xis = shared("sh_xis", (DIN, S), BF16)
        sh_xdbl = shared("sh_xdbl", (96, S))
        sh_y = shared("sh_y", (DIN, S), BF16)
    dr_comb = nc.dram_tensor("dr_comb", [NE, TC], F32).ap()
    dr_res = nc.dram_tensor("dr_res", [16, TC * 128], BF16).ap()
    dr_x = nc.dram_tensor("dr_x", [8, TC * 128], F32).ap()
    dr_bcrows = nc.dram_tensor("dr_bcrows", [32, S], BF16).ap()
    dr_uloc = nc.dram_tensor("dr_uloc", [1024, S], BF16).ap()
    dr_yloc = nc.dram_tensor("dr_yloc", [1024, S], BF16).ap()
    dr_yepi = nc.dram_tensor("dr_yepi", [DIN, TC], BF16).ap()
    dr_uepi = nc.dram_tensor("dr_uepi", [DIN, TC], BF16).ap()
    dr_xdbl = nc.dram_tensor("dr_xdbl", [96, TC], F32).ap()

    ident_h = nc.inline_tensor(np.eye(128, dtype=np.float32), name="c_ident")
    ones_r = nc.inline_tensor(np.ones((1, 128), np.float32), name="c_ones_r")
    ones_c = nc.inline_tensor(np.ones((128, 1), np.float32), name="c_ones_c")
    selk = np.zeros((16, 2), np.float32); selk[:8, 0] = 1; selk[8:, 1] = 1
    selk_h = nc.inline_tensor(selk, name="c_selk")
    selkT_h = nc.inline_tensor(np.ascontiguousarray(selk.T), name="c_selkT")
    sele = np.zeros((16, 8), np.float32)
    for i in range(16): sele[i, i % 8] = 1
    sele_h = nc.inline_tensor(sele, name="c_sele")
    eps_h = nc.inline_tensor(np.full((1, 1), LN_EPS, np.float32), name="c_eps")
    one_h = nc.inline_tensor(np.full((128, 1), 1.0, np.float32), name="c_one")
    b27_h = nc.inline_tensor(np.full((128, 1), NEG_LN_EPS12, np.float32), name="c_b27")

    s_ident = sb("s_ident", (128, 128))
    s_ones_r = sb("s_ones_r", (1, 128))
    s_ones_c = sb("s_ones_c", (128, 1))
    s_selk = sb("s_selk", (16, 2))
    s_selkT = sb("s_selkT", (2, 16))
    s_sele = sb("s_sele", (16, 8))
    s_rank = sb("s_rank", (1, 1), U32)
    s_rankf = sb("s_rankf", (16, 1))
    s_nonce = sb("s_nonce", (1, 8), U32)
    s_flags = sb("s_flags", (1, 16), U32)
    s_convw = [sb(f"s_convw{k}", (128, 16)) for k in range(4)]
    s_convb = sb("s_convb", (128, 16))
    s_bdt = sb("s_bdt", (128, 16))
    s_dvec = sb("s_dvec", (128, 16))
    s_gcol = sb("s_gcol", (128, 8))
    s_glog = sb("s_glog", (128, 128))
    s_g1 = sb("s_g1", (128, 8))
    s_g2 = sb("s_g2", (128, 8))
    s_srt = sb("s_srt", (128, 8))
    s_z8 = sb("s_z8", (128, 8))
    s_packT = sb("s_packT", (128, 18))
    s_cnt2 = sb("s_cnt2", (2, 16))
    s_cntT = sb("s_cntT", (16, 2))
    s_off = sb("s_off", (16, 1))
    s_gw = [sb(f"s_gw{i}", (128, NE), BF16) for i in range(8)]

    s_xA = sb("s_xA", (128, 8 * TC))
    s_x = [s_xA[:, i * TC:(i + 1) * TC] for i in range(8)]
    XR = [f"sx{i}" for i in range(8)]
    s_xnA = sb("s_xnA", (128, 8 * TC), BF16)
    s_xn = [s_xnA[:, i * TC:(i + 1) * TC] for i in range(8)]
    XNR = [f"sxn{i}" for i in range(8)]
    s_di = s_xnA[0:64, 12288:16384]
    DIR = ["sxn6", "sxn7"]
    s_tt = sb("s_tt", (128, TC))
    s_stage = [sb(f"s_stage{i}", (128, D)) for i in range(2)]
    STGR = ["stg0", "stg1"]
    s_rowsA = sb("s_rowsA", (128, TC))
    s_rowsB = sb("s_rowsB", (128, TC))
    ohf = s_rowsA[0:16, :]
    comb = s_rowsA[64:72, :]
    r_mean = s_rowsB[0:1, :]
    r_tmp = s_rowsA[0:1, :]
    wrow = s_rowsB[0:2, :]
    s_off0 = sb("s_off0", (16, 1))
    s_eps = sb("s_eps", (1, 1))
    s_one11 = sb("s_one11", (128, 1))
    s_b27 = sb("s_b27", (128, 1))
    s_rankc = sb("s_rankc", (128, 1))
    s_dcol = sb("s_dcol", (128, 1))
    s_wx = [sb(f"s_wx{k}", (128, 96), BF16) for k in range(2)]
    s_wdtloc = sb("s_wdtloc", (64, 1024), BF16)
    s_bdtloc = sb("s_bdtloc", (128, 8))
    s_hx = sb("s_hx", (128, 8))
    s_resst = sb("s_resst", (128, 512), BF16)
    s_xiex = sb("s_xiex", (128, 12288), BF16)

    s_bigB = sb("s_bigB", (128, 24576), BF16)
    bigB_f32 = s_bigB[:, :].bitcast(F32)
    BB = [f"bb{i}" for i in range(12)]
    def bslot(i, w=1):
        return s_bigB[:, i * 2048:(i + w) * 2048]
    s_wgq = [bslot(0, 2)[:, k * 512:(k + 1) * 512] for k in range(8)]
    WGR = [BB[0] if k < 4 else BB[1] for k in range(8)]
    s_wuq = [bslot(2, 2)[:, k * 512:(k + 1) * 512] for k in range(8)]
    WUR = [BB[2] if k < 4 else BB[3] for k in range(8)]
    s_wdq = [bslot(4, 2)[:, k * 1024:(k + 1) * 1024] for k in range(4)]
    WDR = [BB[4] if k < 2 else BB[5] for k in range(4)]
    s_p = [bslot(6 + k) for k in range(4)]
    PR = [BB[6 + k] for k in range(4)]
    s_sg = sb("s_sg", (128, 256), BF16)
    s_combbc = sb("s_combbc", (128, 256), BF16)

    pbank = [kb.es.enter_context(nc.psum_tensor(f"pbank{i}", [128, 512], F32))
             for i in range(8)]
    PB_R = [f"pb{i}" for i in range(8)]

    state = {"rank": None, "bk": 0, "pollbase": 0}

    for h_, d_ in ((ident_h, s_ident), (ones_r, s_ones_r), (ones_c, s_ones_c),
                   (selk_h, s_selk), (selkT_h, s_selkT), (sele_h, s_sele)):
        kb.dma("sync", d_[:], h_.ap(), writes=[d_], tag="consts", skey="consts")
    kb.dma("sync", s_eps[:], eps_h.ap(), writes=[s_eps], tag="consts", skey="consts")
    kb.dma("sync", s_one11[:], one_h.ap(), writes=[s_one11], tag="consts", skey="consts")
    kb.dma("sync", s_b27[:], b27_h.ap(), writes=[s_b27], tag="consts", skey="consts")
    kb.dma("sync", s_rankc[:], rankc_in, writes=[s_rankc], tag="consts", skey="consts")
    kb.dma("sync", s_rank[:], rank_in, writes=[s_rank], tag="consts", skey="consts")
    kb.dma("sync", s_rankf[:], rankf_in, writes=[s_rankf], tag="consts", skey="consts")
    kb.dma("sync", s_nonce[:], nonces, writes=[s_nonce], tag="consts", skey="consts")
    for k in range(4):
        kb.dma("sync", s_convw[k][:], convw_c[k], writes=[s_convw[k]], tag="consts", skey="consts")
    kb.dma("sync", s_convb[:], convb_c, writes=[s_convb], tag="consts", skey="consts")
    kb.dma("sync", s_bdt[:], bdt_c, writes=[s_bdt], tag="consts", skey="consts")
    kb.dma("sync", s_dvec[:], dvec_c, writes=[s_dvec], tag="consts", skey="consts")
    for dc in range(8):
        kb.dma("gpsimd", s_gw[dc][:], gw_t[dc * 128:(dc + 1) * 128, :],
               writes=[s_gw[dc]], tag="gw", skey="gws")

    def _ld_rank(g):
        state["rank"] = g.value_load(s_rank[0:1, 0:1])
        state["pollreg"] = g.alloc_register("pollcnt")
        g.reg_mov(state["pollreg"], 0)
        for rn in ("b_want", "b_f0", "b_f1", "b_nd"):
            state[rn] = g.alloc_register(rn)
        return None
    kb.add("gpsimd", _ld_rank, reads=[s_rank], tag="rank")

    def barrier(extra_reads=()):
        k = state["bk"]; state["bk"] += 1
        def _go(g, k=k):
            rank = state["rank"]
            g.dma_start(out=sh_flag[0:1, 2 * k + 0:2 * k + 2][0:1, bass.ts(rank, 1)],
                        in_=s_nonce[0:1, k:k + 1]).then_inc(sems["flag"], 16)
            g.wait_ge(sems["flag"], 16 * (k + 1))
            want, f0, f1, nd = (state["b_want"], state["b_f0"],
                                state["b_f1"], state["b_nd"])
            g.reg_load(want, s_nonce[0:1, k:k + 1])
            pc = state["pollreg"]
            g.reg_mov(nd, 1)
            with g.While(nd):
                g.reg_add(pc, pc, 16)
                g.dma_start(out=s_flags[0:1, 2 * k:2 * k + 2],
                            in_=sh_flag[0:1, 2 * k:2 * k + 2]).then_inc(sems["poll"], 16)
                g.wait_ge(sems["poll"], pc)
                g.reg_load(f0, s_flags[0:1, 2 * k:2 * k + 1])
                g.reg_load(f1, s_flags[0:1, 2 * k + 1:2 * k + 2])
                g.reg_alu(f0, f0, want, OP.subtract)
                g.reg_alu(f1, f1, want, OP.subtract)
                g.reg_alu(nd, f0, f1, OP.bitwise_or)
            return g.nop()
        return kb.add("gpsimd", _go, reads=[s_nonce, s_flags, *extra_reads],
                      writes=["barrier", s_flags], tag=f"barrier{k}")

    for tt in range(16):
        stg = s_stage[tt % 2]
        kb.dma("sync", stg[:, :], x_tm[tt * 128:(tt + 1) * 128, :],
               writes=[STGR[tt % 2]], tag="xload")
        for dc in range(8):
            bank = pbank[dc % 4]
            kb.add("tensor",
                   lambda g, stg=stg, dc=dc, bank=bank:
                   g.transpose(out=bank[0:128, 0:128],
                               in_=stg[:, dc * 128:(dc + 1) * 128], identity=s_ident[:]),
                   reads=[STGR[tt % 2], s_ident], writes=[PB_R[dc % 4]], tag="xtp")
            kb.add("vector",
                   lambda g, tt=tt, dc=dc, bank=bank:
                   g.tensor_copy(out=s_x[dc][:, tt * 128:(tt + 1) * 128],
                                 in_=bank[0:128, 0:128]),
                   reads=[PB_R[dc % 4]], writes=[XR[dc]], tag="xev")

    def layernorm():
        for qt in range(4):
            sl = slice(qt * 512, (qt + 1) * 512)
            for dc in range(8):
                kb.add("scalar",
                       lambda g, dc=dc, sl=sl:
                       g.activation(out=s_tt[:, 0:512], in_=s_x[dc][:, sl], func=AF.Square),
                       reads=[XR[dc]], writes=["tt_a"], tag="ln_sq")
                kb.add("tensor",
                       lambda g, dc=dc, sl=sl:
                       g.matmul(out=pbank[4][0:1, :], lhsT=s_ones_c[:], rhs=s_x[dc][:, sl],
                                start=(dc == 0), stop=(dc == 7)),
                       reads=[XR[dc], s_ones_c], writes=[PB_R[4]], tag="ln_s1")
                kb.add("tensor",
                       lambda g, dc=dc:
                       g.matmul(out=pbank[7][0:1, :], lhsT=s_ones_c[:], rhs=s_tt[:, 0:512],
                                start=(dc == 0), stop=(dc == 7)),
                       reads=["tt_a", s_ones_c], writes=[PB_R[7]], tag="ln_s2")
            kb.add("vector",
                   lambda g, sl=sl: g.tensor_scalar(out=r_mean[0:1, sl], in0=pbank[4][0:1, :],
                                                    scalar1=1.0 / D, scalar2=None, op0=OP.mult),
                   reads=[PB_R[4]], writes=["rB0"], tag="ln_mean")
            kb.add("vector",
                   lambda g, sl=sl: g.tensor_scalar(out=r_tmp[0:1, sl], in0=pbank[7][0:1, :],
                                                    scalar1=1.0 / D, scalar2=None, op0=OP.mult),
                   reads=[PB_R[7]], writes=["rA"], tag="ln_ssq")
            kb.add("scalar",
                   lambda g, sl=sl: g.activation(out=pbank[7][0:1, 0:512],
                                                 in_=r_mean[0:1, sl], func=AF.Square),
                   reads=["rB0"], writes=[PB_R[7]], tag="ln_m2")
            kb.add("vector",
                   lambda g, sl=sl: g.tensor_sub(out=r_tmp[0:1, sl], in0=r_tmp[0:1, sl],
                                                 in1=pbank[7][0:1, 0:512]),
                   reads=[PB_R[7], "rA"], writes=["rA"], tag="ln_var")
            kb.add("scalar",
                   lambda g, sl=sl: g.activation(out=pbank[7][0:1, 0:512], in_=r_tmp[0:1, sl],
                                                 func=AF.Ln, bias=s_eps[0:1, 0:1]),
                   reads=["rA", s_eps], writes=[PB_R[7]], tag="ln_ln")
            kb.add("scalar",
                   lambda g, sl=sl: g.activation(out=r_tmp[0:1, sl], in_=pbank[7][0:1, 0:512],
                                                 func=AF.Exp, scale=-0.5),
                   reads=[PB_R[7]], writes=["rA"], tag="ln_exp")
            kb.add("tensor",
                   lambda g, sl=sl: g.matmul(out=pbank[5][:, :], lhsT=s_ones_r[:],
                                             rhs=r_mean[0:1, sl], start=True, stop=True),
                   reads=["rB0", s_ones_r], writes=[PB_R[5]], tag="ln_bc1")
            kb.add("tensor",
                   lambda g, sl=sl: g.matmul(out=pbank[6][:, :], lhsT=s_ones_r[:],
                                             rhs=r_tmp[0:1, sl], start=True, stop=True),
                   reads=["rA", s_ones_r], writes=[PB_R[6]], tag="ln_bc2")
            for dc in range(8):
                kb.add("vector",
                       lambda g, dc=dc, sl=sl:
                       g.tensor_sub(out=s_tt[:, 512:1024], in0=s_x[dc][:, sl],
                                    in1=pbank[5][:, :]),
                       reads=[XR[dc], PB_R[5]], writes=["tt_b"], tag="ln_xn1")
                kb.add("vector",
                       lambda g, dc=dc, sl=sl:
                       g.tensor_mul(out=s_xn[dc][:, sl], in0=s_tt[:, 512:1024],
                                    in1=pbank[6][:, :]),
                       reads=["tt_b", PB_R[6]], writes=[XNR[dc]], tag="ln_xn2")

    def gating(which):
        shc = sh_cnt[which]
        for qt in range(4):
            sl = slice(qt * 512, (qt + 1) * 512)
            for dc in range(8):
                kb.add("tensor",
                       lambda g, dc=dc, sl=sl:
                       g.matmul(out=pbank[7][0:8, :], lhsT=s_gw[dc][:], rhs=s_xn[dc][:, sl],
                                start=(dc == 0), stop=(dc == 7)),
                       reads=[s_gw[dc], XNR[dc]], writes=[PB_R[7]], tag="g_log")
            kb.add("vector",
                   lambda g, sl=sl: g.tensor_copy(out=ohf[0:8, sl], in_=pbank[7][0:8, :]),
                   reads=[PB_R[7]], writes=["rA"], tag="g_ev")
        for tt in range(16):
            kb.add("tensor",
                   lambda g, tt=tt:
                   g.transpose(out=pbank[7][0:128, 0:8],
                               in_=ohf[0:8, tt * 128:(tt + 1) * 128],
                               identity=s_ident[0:8, 0:8]),
                   reads=["rA", s_ident], writes=[PB_R[7]], tag="g_tp")
            kb.add("vector",
                   lambda g, tt=tt:
                   g.tensor_copy(out=s_glog[:, tt * 8:(tt + 1) * 8], in_=pbank[7][0:128, 0:8]),
                   reads=[PB_R[7]], writes=[s_glog], tag="g_tpev")
        kb.add("vector", lambda g: g.memset(s_z8[:], 0.0), writes=[s_z8], tag="g_z")
        V = "vector"
        for tt in range(16):
            lg = s_glog[:, tt * 8:(tt + 1) * 8]
            kb.add(V, lambda g, lg=lg: g.reduce_max(out=s_gcol[:, 0:1], in_=lg, axis=AX.X),
                   reads=[s_glog], writes=[s_gcol], tag="gm")
            kb.add(V, lambda g, lg=lg: g.tensor_scalar(out=s_g1[:], in0=lg,
                                                       scalar1=s_gcol[:, 0:1], scalar2=None,
                                                       op0=OP.subtract),
                   reads=[s_glog], sreads=[s_gcol], writes=[s_g1], tag="gs")
            kb.add("scalar", lambda g: g.activation(out=s_g2[:], in_=s_g1[:], func=AF.Exp),
                   reads=[s_g1], writes=[s_g2], tag="ge")
            kb.add(V, lambda g: g.reduce_sum(out=s_gcol[:, 1:2], in_=s_g2[:], axis=AX.X),
                   reads=[s_g2], writes=[s_gcol], tag="gr")
            kb.add(V, lambda g: g.reciprocal(out=s_gcol[:, 2:3], in_=s_gcol[:, 1:2]),
                   reads=[], sreads=[s_gcol], writes=[s_gcol], tag="gi")
            kb.add(V, lambda g: g.tensor_scalar(out=s_g1[:], in0=s_g2[:],
                                                scalar1=s_gcol[:, 2:3], scalar2=None,
                                                op0=OP.mult),
                   reads=[s_g2], sreads=[s_gcol], writes=[s_g1], tag="gp")
            kb.add(V, lambda g: g.max(out=s_srt[:], in_=s_g1[:]),
                   reads=[s_g1], writes=[s_srt], tag="gmx")
            kb.add(V, lambda g: g.tensor_add(out=s_gcol[:, 3:4], in0=s_srt[:, 0:1],
                                             in1=s_srt[:, 1:2]),
                   reads=[], sreads=[s_srt], writes=[s_gcol], tag="gw1")
            kb.add(V, lambda g: g.tensor_scalar(out=s_gcol[:, 3:4], in0=s_gcol[:, 3:4],
                                                scalar1=1e-9, scalar2=None, op0=OP.add),
                   reads=[], sreads=[s_gcol], writes=[s_gcol], tag="gw2")
            kb.add(V, lambda g: g.reciprocal(out=s_gcol[:, 3:4], in_=s_gcol[:, 3:4]),
                   reads=[], sreads=[s_gcol], writes=[s_gcol], tag="gw3")
            kb.add(V, lambda g: g.tensor_scalar(out=s_gcol[:, 3:4], in0=s_gcol[:, 3:4],
                                                scalar1=0.5, scalar2=None, op0=OP.mult),
                   reads=[], sreads=[s_gcol], writes=[s_gcol], tag="gw4")
            kb.add(V, lambda g: g.tensor_scalar(out=s_packT[:, 16:17], in0=s_srt[:, 0:1],
                                                scalar1=s_gcol[:, 3:4], scalar2=None,
                                                op0=OP.mult),
                   reads=[], sreads=[s_srt, s_gcol], writes=[s_packT], tag="gw5")
            kb.add(V, lambda g: g.tensor_scalar(out=s_packT[:, 17:18], in0=s_srt[:, 1:2],
                                                scalar1=s_gcol[:, 3:4], scalar2=None,
                                                op0=OP.mult),
                   reads=[], sreads=[s_srt, s_gcol], writes=[s_packT], tag="gw6")
            kb.add(V, lambda g: g.tensor_scalar(out=s_g2[:], in0=s_g1[:],
                                                scalar1=s_srt[:, 0:1], scalar2=None,
                                                op0=OP.is_equal),
                   reads=[s_g1], sreads=[s_srt], writes=[s_g2], tag="geq0")
            kb.add(V, lambda g: g.tensor_tensor_scan(out=s_packT[:, 0:8], data0=s_g2[:],
                                                     data1=s_z8[:], initial=0.0,
                                                     op0=OP.add, op1=OP.add),
                   reads=[s_z8], sreads=[s_g2], writes=[s_packT], tag="gcs0")
            kb.add(V, lambda g: g.tensor_scalar(out=s_packT[:, 0:8], in0=s_packT[:, 0:8],
                                                scalar1=1.0, scalar2=None, op0=OP.is_equal),
                   reads=[], sreads=[s_packT], writes=[s_packT], tag="gff0a")
            kb.add(V, lambda g: g.tensor_mul(out=s_packT[:, 0:8], in0=s_packT[:, 0:8],
                                             in1=s_g2[:]),
                   reads=[s_g2], sreads=[s_packT], writes=[s_packT], tag="gff0b")
            kb.add(V, lambda g: g.tensor_scalar(out=s_g2[:], in0=s_g1[:],
                                                scalar1=s_srt[:, 1:2], scalar2=None,
                                                op0=OP.is_equal),
                   reads=[s_g1], sreads=[s_srt, s_packT], writes=[s_g2], tag="geq1")
            kb.add(V, lambda g: g.tensor_scalar(out=s_g1[:], in0=s_packT[:, 0:8],
                                                scalar1=-1.0, scalar2=1.0,
                                                op0=OP.mult, op1=OP.add),
                   reads=[], sreads=[s_packT], writes=[s_g1], tag="gnot")
            kb.add(V, lambda g: g.tensor_mul(out=s_g2[:], in0=s_g2[:], in1=s_g1[:]),
                   reads=[], sreads=[s_g1, s_g2], writes=[s_g2], tag="gm1")
            kb.add(V, lambda g: g.tensor_tensor_scan(out=s_g1[:], data0=s_g2[:],
                                                     data1=s_z8[:], initial=0.0,
                                                     op0=OP.add, op1=OP.add),
                   reads=[s_z8], sreads=[s_g2], writes=[s_g1], tag="gcs1")
            kb.add(V, lambda g: g.tensor_scalar(out=s_g1[:], in0=s_g1[:], scalar1=1.0,
                                                scalar2=None, op0=OP.is_equal),
                   reads=[], sreads=[s_g1], writes=[s_g1], tag="gff1a")
            kb.add(V, lambda g: g.tensor_mul(out=s_packT[:, 8:16], in0=s_g1[:], in1=s_g2[:]),
                   reads=[], sreads=[s_g1, s_g2], writes=[s_packT], tag="gff1b")
            kb.add("tensor",
                   lambda g: g.transpose(out=pbank[7][0:16, 0:128], in_=s_packT[:, 0:16],
                                         identity=s_ident[:]),
                   reads=[s_packT, s_ident], writes=[PB_R[7]], tag="g_tp2")
            kb.add("vector",
                   lambda g, tt=tt:
                   g.tensor_copy(out=ohf[:, tt * 128:(tt + 1) * 128], in_=pbank[7][0:16, 0:128]),
                   reads=[PB_R[7]], writes=["rA"], tag="g_tpev2")
            kb.add("tensor",
                   lambda g: g.transpose(out=pbank[6][0:2, 0:128], in_=s_packT[:, 16:18],
                                         identity=s_ident[:]),
                   reads=[s_packT, s_ident], writes=[PB_R[6]], tag="g_tp3")
            kb.add("vector",
                   lambda g, tt=tt:
                   g.tensor_copy(out=wrow[:, tt * 128:(tt + 1) * 128], in_=pbank[6][0:2, 0:128]),
                   reads=[PB_R[6]], writes=["rB0"], tag="g_tpev3")
        rk16 = s_tt[0:16, :]
        TTRES = ["tt_a", "tt_b"]
        kb.add(V, lambda g: g.tensor_tensor_scan(out=rk16[:, :], data0=ohf[0:16, :],
                                                 data1=ohf[0:16, :], initial=0.0,
                                                 op0=OP.add, op1=OP.bypass),
               sreads=["rA"], writes=TTRES, tag="g_rsc")
        kb.add(V, lambda g: g.tensor_copy(out=s_cntT[:, 0:1], in_=rk16[:, TC - 1:TC]),
               sreads=TTRES, writes=[s_cntT], tag="g_cnt")
        def _wcnt(g, shc=shc):
            return g.dma_start(out=shc[bass.ts(state["rank"], 1), 0:16], in_=s_cntT[:, 0:1])
        kb.add("gpsimd", _wcnt, reads=[s_cntT], writes=["shmem"], tag="g_wcnt", dma=True)
        barrier(extra_reads=["shmem"])
        kb.dma("gpsimd", s_cnt2[:, :], shc[:, :], reads=["barrier"], writes=[s_cnt2],
               tag="g_rcnt")
        kb.add("tensor",
               lambda g: g.transpose(out=pbank[7][0:16, 0:2], in_=s_cnt2[0:2, 0:16],
                                     identity=s_ident[0:2, 0:2]),
               reads=[s_cnt2, s_ident], writes=[PB_R[7]], tag="g_tpc")
        kb.add(V, lambda g: g.tensor_copy(out=s_cntT[:, 0:2], in_=pbank[7][0:16, 0:2]),
               reads=[PB_R[7]], writes=[s_cntT], tag="g_cntT")
        kb.add(V, lambda g: g.tensor_scalar(out=s_off0[:, 0:1], in0=s_cntT[:, 0:1],
                                            scalar1=s_rankf[:, 0:1], scalar2=-1.0,
                                            op0=OP.mult, op1=OP.add),
               reads=[s_rankf], sreads=[s_cntT], writes=[s_off0], tag="g_off")
        for qt in range(4):
            sl = slice(qt * 512, (qt + 1) * 512)
            kb.add(V, lambda g, sl=sl: g.tensor_scalar(out=pbank[5][0:16, :],
                                                       in0=rk16[:, sl],
                                                       scalar1=s_off0[:, 0:1], scalar2=None,
                                                       op0=OP.add),
                   reads=TTRES, sreads=[s_off0], writes=[PB_R[5]], tag="g_rk1")
            kb.add(V, lambda g, sl=sl: g.tensor_mul(out=rk16[:, sl], in0=ohf[0:16, sl],
                                                    in1=pbank[5][0:16, :]),
                   reads=["rA", PB_R[5]], writes=TTRES, tag="g_rk2")
        for qt in range(4):
            sl = slice(qt * 512, (qt + 1) * 512)
            kb.add("tensor",
                   lambda g, sl=sl: g.matmul(out=pbank[7][0:2, :], lhsT=s_selk[:, :],
                                             rhs=rk16[:, sl], start=True, stop=True),
                   reads=[s_selk] + TTRES, writes=[PB_R[7]], tag="g_mr")
            kb.add(V, lambda g, sl=sl: g.tensor_scalar(out=pbank[6][0:2, :],
                                                       in0=pbank[7][0:2, :],
                                                       scalar1=float(CAP) - 0.5,
                                                       scalar2=None, op0=OP.is_lt),
                   reads=[PB_R[7]], writes=[PB_R[6]], tag="g_keep1")
            kb.add(V, lambda g, sl=sl: g.tensor_mul(out=wrow[:, sl], in0=wrow[:, sl],
                                                    in1=pbank[6][0:2, :]),
                   reads=["rB0", PB_R[6]], writes=["rB0"], tag="g_keep2")
            kb.add("tensor",
                   lambda g, sl=sl: g.matmul(out=pbank[7][0:16, :], lhsT=s_selkT[:, :],
                                             rhs=wrow[:, sl], start=True, stop=True),
                   reads=[s_selkT, "rB0"], writes=[PB_R[7]], tag="g_wkbc")
            kb.add(V, lambda g, sl=sl: g.tensor_mul(out=rk16[:, sl], in0=ohf[0:16, sl],
                                                    in1=pbank[7][0:16, :]),
                   reads=[PB_R[7], "rA"], writes=TTRES, tag="g_wko")
            kb.add("tensor",
                   lambda g, sl=sl: g.matmul(out=pbank[6][0:8, :], lhsT=s_sele[:, :],
                                             rhs=rk16[:, sl], start=True, stop=True),
                   reads=[s_sele] + TTRES, writes=[PB_R[6]], tag="g_comb")
            kb.add(V, lambda g, sl=sl: g.tensor_copy(out=comb[:, sl], in_=pbank[6][0:8, :]),
                   reads=[PB_R[6]], writes=["comb"], tag="g_combev")
        kb.dma("sync", dr_comb[:, :], comb[:, :], reads=["comb"], writes=["dr_comb"],
               tag="g_spill")

    def ffn_call(which):
        layernorm()
        gating(which)
        if which == 0:
            kb.dma("sync", dbg_comb, comb[:, :], reads=["comb"], writes=["dbg"], tag="dbg")
            kb.dma("sync", dbg_cnt, s_cntT[:, :], reads=[s_cntT], writes=["dbg"], tag="dbg")
            kb.dma("sync", dbg_glog, s_glog[:, :], reads=[s_glog], writes=["dbg"], tag="dbg")
            kb.dma("sync", dbg_ohf, ohf[:, :], reads=["rA"], writes=["dbg"], tag="dbg")
            kb.dma("sync", dbg_xn, s_xn[0][:, :], reads=[XNR[0]], writes=["dbg"], tag="dbg")
            kb.dma("sync", dbg_rkf, s_tt[0:16, :], reads=["tt_a", "tt_b"], writes=["dbg"], tag="dbg")
        groups = [(e, q) for e in range(NE) for q in range(4)]

        def load_gu(e, q):
            for k in range(8):
                kb.dma("gpsimd", s_wgq[k][:],
                       wg_t[e, k * 128:(k + 1) * 128, q * 512:(q + 1) * 512],
                       writes=[WGR[k]], tag="w_gu")
                kb.dma("gpsimd", s_wuq[k][:],
                       wu_t[e, k * 128:(k + 1) * 128, q * 512:(q + 1) * 512],
                       writes=[WUR[k]], tag="w_gu")

        def load_wd(e, q):
            for k in range(4):
                kb.dma("gpsimd", s_wdq[k][:],
                       wd_t[e, q * 512 + k * 128: q * 512 + (k + 1) * 128, :],
                       writes=[WDR[k]], tag="w_d")

        load_gu(*groups[0])
        load_wd(*groups[0])
        for gi, (e, q) in enumerate(groups):
            for t8 in range(8):
                tsl = slice(t8 * 256, (t8 + 1) * 256)
                kb.dma("gpsimd", s_combbc[:],
                       dr_comb[e:e + 1, tsl].broadcast_to([128, 256]),
                       reads=["dr_comb"], writes=[s_combbc], tag="combbc")
                for hc in range(4):
                    gbank, gsl = pbank[4 + (hc % 2)], slice(0, 256)
                    ubank, usl = pbank[6 + (hc % 2)], slice(0, 256)
                    for k in range(8):
                        kb.add("tensor",
                               lambda g, k=k, hc=hc, tsl=tsl, gbank=gbank, gsl=gsl:
                               g.matmul(out=gbank[0:128, gsl],
                                        lhsT=s_wgq[k][:, hc * 128:(hc + 1) * 128],
                                        rhs=s_xn[k][:, tsl], start=(k == 0), stop=(k == 7)),
                               reads=[WGR[k], XNR[k]], writes=[PB_R[4 + (hc % 2)]],
                               tag="mm_g")
                    for k in range(8):
                        kb.add("tensor",
                               lambda g, k=k, hc=hc, tsl=tsl, ubank=ubank, usl=usl:
                               g.matmul(out=ubank[0:128, usl],
                                        lhsT=s_wuq[k][:, hc * 128:(hc + 1) * 128],
                                        rhs=s_xn[k][:, tsl], start=(k == 0), stop=(k == 7)),
                               reads=[WUR[k], XNR[k]], writes=[PB_R[6 + (hc % 2)]],
                               tag="mm_u")
                    kb.add("scalar",
                           lambda g, gbank=gbank, gsl=gsl:
                           g.activation(out=s_sg[:], in_=gbank[0:128, gsl], func=AF.Silu),
                           reads=[PB_R[4 + (hc % 2)]], writes=[s_sg], tag="silu")
                    kb.add("vector",
                           lambda g, ubank=ubank, usl=usl:
                           g.scalar_tensor_tensor(out=s_tt[:, 0:256], in0=ubank[0:128, usl],
                                                  scalar=1.0, in1=s_sg[:],
                                                  op0=OP.mult, op1=OP.mult),
                           reads=[PB_R[6 + (hc % 2)], s_sg], writes=["tt_a"], tag="pmul1")
                    kb.add("vector",
                           lambda g, hc=hc, tsl=tsl:
                           g.tensor_mul(out=s_p[hc][:, tsl], in0=s_tt[:, 0:256],
                                        in1=s_combbc[:]),
                           reads=[s_combbc], sreads=["tt_a"], writes=[PR[hc]], tag="pmul2")
            if gi + 1 < len(groups):
                load_gu(*groups[gi + 1])
            for t8 in range(8):
                tsl = slice(t8 * 256, (t8 + 1) * 256)
                for dc in range(8):
                    bank = pbank[dc % 4]
                    bsl = slice((dc // 4) * 256, (dc // 4) * 256 + 256)
                    for hc in range(4):
                        kb.add("tensor",
                               lambda g, hc=hc, dc=dc, tsl=tsl, bank=bank, bsl=bsl:
                               g.matmul(out=bank[0:128, bsl],
                                        lhsT=s_wdq[hc][:, dc * 128:(dc + 1) * 128],
                                        rhs=s_p[hc][:, tsl], start=(hc == 0), stop=(hc == 3)),
                               reads=[WDR[hc], PR[hc]], writes=[PB_R[dc % 4]],
                               tag="mm_d")
                    kb.add("vector",
                           lambda g, dc=dc, tsl=tsl, bank=bank, bsl=bsl:
                           g.tensor_add(out=s_x[dc][:, tsl], in0=s_x[dc][:, tsl],
                                        in1=bank[0:128, bsl]),
                           reads=[PB_R[dc % 4], XR[dc]], writes=[XR[dc]], tag="dn_add")
            if gi + 1 < len(groups):
                load_wd(*groups[gi + 1])

    def mamba():
        V = "vector"
        layernorm()
        for i in range(8):
            kb.dma("sync", dr_x[i:i + 1, :].rearrange("one (p c) -> p (one c)", p=128),
                   s_x[i], reads=[XR[i]], writes=[f"drx{i}"], tag="xspill")
        if MAMBA_CUT <= 0:
            return
        xA_b = s_xA[:, :].bitcast(BF16)
        xi_t = [bslot(j) for j in range(12)] + \
               [s_xiex[:, j2 * 2048:(j2 + 1) * 2048] for j2 in range(4)]
        XIRl = [[BB[j]] for j in range(12)] + [["xe0"], ["xe1"], ["xe2"], ["xe3"]]
        winr = [s_xiex[:, 8192:10240], s_xiex[:, 10240:12288]]
        WINR = [["xe4"], ["xe5"]]
        def win_chunk(j):
            p = j % 2
            kb.dma("gpsimd", winr[p][:, 0:1024],
                   w_in[:, j * 128:(j + 1) * 128].rearrange("(k p) c -> p k c", p=128),
                   reads=WINR[p], writes=WINR[p], tag="w_in_ld")
            for t8 in range(4):
                tsl = slice(t8 * 512, (t8 + 1) * 512)
                bank = pbank[4 + t8 % 2]
                for k in range(8):
                    kb.add("tensor",
                           lambda g, k=k, tsl=tsl, bank=bank, p=p:
                           g.matmul(out=bank[0:128, 0:512],
                                    lhsT=winr[p][:, k * 128:(k + 1) * 128],
                                    rhs=s_xn[k][:, tsl], start=(k == 0), stop=(k == 7)),
                           reads=WINR[p] + [XNR[k]], writes=[PB_R[4 + t8 % 2]], tag="mm_win")
                if j < 16:
                    kb.add(V, lambda g, j=j, tsl=tsl, bank=bank:
                           g.tensor_copy(out=xi_t[j][:, tsl], in_=bank[0:128, 0:512]),
                           reads=[PB_R[4 + t8 % 2]], writes=XIRl[j], tag="xi_ev")
                else:
                    st = s_resst[:, 0:512]
                    kb.add(V, lambda g, tsl=tsl, bank=bank, st=st:
                           g.tensor_copy(out=st, in_=bank[0:128, 0:512]),
                           reads=[PB_R[4 + t8 % 2]], writes=["resst"], tag="res_ev")
                    kb.dma("sync",
                           dr_res[j - 16:j - 15, :]
                           .rearrange("one (p c) -> p (one c)", p=128)[:, t8 * 512:(t8 + 1) * 512],
                           st, reads=["resst"], writes=["drres"], tag="res_sp")
        for j in range(16):
            win_chunk(j)
        if MAMBA_CUT <= 1:
            return
        kb.add("gpsimd",
               lambda g: g.dma_start(
                   out=sh_halo[0:1536, :][:, bass.ts(state["rank"], 3)]
                   .rearrange("(c p) w -> p c w", p=128),
                   in_=s_bigB[:, 0:24576].rearrange("p (c l) -> p c l",
                                                    l=2048)[:, :, 2045:2048]),
               reads=[BB[j] for j in range(12)], writes=["shmem2"], tag="halo_w1",
               dma=True)
        kb.add("gpsimd",
               lambda g: g.dma_start(
                   out=sh_halo[1536:2048, :][:, bass.ts(state["rank"], 3)]
                   .rearrange("(c p) w -> p c w", p=128),
                   in_=s_xiex[:, 0:8192].rearrange("p (c l) -> p c l",
                                                   l=2048)[:, :, 2045:2048]),
               reads=["xe0", "xe1", "xe2", "xe3"], writes=["shmem2"], tag="halo_w2",
               dma=True)
        barrier(extra_reads=["shmem2"])
        if MAMBA_CUT <= 2:
            return
        for j in range(16):
            win_chunk(16 + j)
            kb.dma("gpsimd", s_hx[:, 0:3], sh_halo[j * 128:(j + 1) * 128, 0:3],
                   reads=["barrier"], writes=[s_hx], tag="halo_r")
            kb.add(V, lambda g: g.tensor_scalar(out=s_hx[:, 0:3], in0=s_hx[:, 0:3],
                                                scalar1=s_rankc[:, 0:1], scalar2=None,
                                                op0=OP.mult),
                   reads=[s_rankc], sreads=[s_hx], writes=[s_hx], tag="halo_m")
            kb.add(V, lambda g, j=j: g.tensor_copy(out=s_hx[:, 3:6], in_=xi_t[j][:, 0:3]),
                   reads=XIRl[j], writes=[s_hx], tag="halo_c")
            kb.add(V, lambda g, j=j: g.tensor_scalar(out=s_tt[:, 3:2048],
                                                     in0=xi_t[j][:, 0:2045],
                                                     scalar1=s_convw[0][:, j:j + 1],
                                                     scalar2=None, op0=OP.mult),
                   reads=XIRl[j] + [s_convw[0]], writes=["tt_a", "tt_b"], tag="cv0")
            for k in range(1, 4):
                kb.add(V, lambda g, j=j, k=k:
                       g.scalar_tensor_tensor(out=s_tt[:, 3:2048], in0=xi_t[j][:, k:2045 + k],
                                              scalar=s_convw[k][:, j:j + 1],
                                              in1=s_tt[:, 3:2048], op0=OP.mult, op1=OP.add),
                       reads=XIRl[j] + [s_convw[k]], sreads=["tt_a", "tt_b"],
                       writes=["tt_a", "tt_b"], tag="cvk")
            kb.add(V, lambda g, j=j: g.tensor_scalar(out=s_tt[:, 0:3], in0=s_hx[:, 0:3],
                                                     scalar1=s_convw[0][:, j:j + 1],
                                                     scalar2=None, op0=OP.mult),
                   reads=[s_convw[0]], sreads=[s_hx], writes=["tt_a"], tag="cvh0")
            for k in range(1, 4):
                kb.add(V, lambda g, j=j, k=k:
                       g.scalar_tensor_tensor(out=s_tt[:, 0:3], in0=s_hx[:, k:k + 3],
                                              scalar=s_convw[k][:, j:j + 1],
                                              in1=s_tt[:, 0:3], op0=OP.mult, op1=OP.add),
                       reads=[s_convw[k]], sreads=[s_hx, "tt_a"], writes=["tt_a"], tag="cvhk")
            kb.add(V, lambda g, j=j: g.tensor_scalar(out=s_tt[:, 0:2048], in0=s_tt[:, 0:2048],
                                                     scalar1=s_convb[:, j:j + 1],
                                                     scalar2=None, op0=OP.add),
                   reads=[s_convb], sreads=["tt_a", "tt_b"], writes=["tt_a", "tt_b"],
                   tag="cvb")
            kb.add("scalar", lambda g, j=j: g.activation(out=xi_t[j][:, :], in_=s_tt[:, 0:2048],
                                                         func=AF.Silu),
                   reads=["tt_a", "tt_b"], writes=XIRl[j], tag="cv_silu")

        kb.add("gpsimd",
               lambda g: g.dma_start(
                   out=sh_xis[0:1536, :][:, bass.ts(state["rank"], TC)]
                   .rearrange("(c p) l -> p c l", p=128),
                   in_=s_bigB[:, 0:24576]),
               reads=[BB[j] for j in range(12)], writes=["shmem3"], tag="xis_w1", dma=True)
        kb.add("gpsimd",
               lambda g: g.dma_start(
                   out=sh_xis[1536:2048, :][:, bass.ts(state["rank"], TC)]
                   .rearrange("(c p) l -> p c l", p=128),
                   in_=s_xiex[:, 0:8192]),
               reads=["xe0", "xe1", "xe2", "xe3"], writes=["shmem3"], tag="xis_w2", dma=True)
        for t8 in range(4):
            tsl = slice(t8 * 512, (t8 + 1) * 512)
            for k in range(16):
                kb.dma("gpsimd", s_wx[k % 2][:], w_x[k * 128:(k + 1) * 128, :],
                       writes=[f"swx{k % 2}"], tag="wx_ld")
                kb.add("tensor",
                       lambda g, k=k, tsl=tsl:
                       g.matmul(out=pbank[6][0:96, :], lhsT=s_wx[k % 2][:], rhs=xi_t[k][:, tsl],
                                start=(k == 0), stop=(k == 15)),
                       reads=[f"swx{k % 2}"] + XIRl[k], writes=[PB_R[6]], tag="mm_wx")
            kb.add(V, lambda g, tsl=tsl: g.tensor_copy(out=s_tt[0:96, 0:512],
                                                       in_=pbank[6][0:96, :]),
                   reads=[PB_R[6]], writes=["tt_a", "tt_b"], tag="wx_ev")
            kb.dma("gpsimd", dr_xdbl[:, tsl], s_tt[0:96, 0:512],
                   reads=["tt_a", "tt_b"], writes=["drxdbl"], tag="xdbl_st")
        kb.add("gpsimd",
               lambda g: g.dma_start(out=sh_xdbl[:, bass.ts(state["rank"], TC)],
                                     in_=dr_xdbl[:, :]),
               reads=["drxdbl"], writes=["shmem3"], tag="xdbl_w", dma=True)
        barrier(extra_reads=["shmem3"])
        if MAMBA_CUT <= 3:
            return
        kb.dma("gpsimd", dr_bcrows[:, :], sh_xdbl[64:96, :],
               reads=["barrier"], writes=["bcrows"], tag="bc_cast")
        kb.dma("gpsimd", s_di[:, :], sh_xdbl[0:64, :], reads=["barrier"],
               writes=DIR, tag="di_ld")
        kb.add("gpsimd",
               lambda g: g.dma_start(out=dr_uloc[:, :],
                                     in_=sh_xis[bass.ts(state["rank"], 1024), :]),
               reads=["barrier"], writes=["druloc"], tag="uloc", dma=True)
        kb.add("gpsimd",
               lambda g: g.dma_start(out=s_wdtloc[:, :],
                                     in_=w_dt[:, bass.ts(state["rank"], 1024)]),
               writes=[s_wdtloc], tag="wdtloc", dma=True)
        kb.add("gpsimd",
               lambda g: g.dma_start(out=s_bdtloc[:, :],
                                     in_=bdt_c[:, bass.ts(state["rank"], 8)]),
               writes=[s_bdtloc], tag="bdtloc", dma=True)
        SC_delta = s_xA[:, 0:4096]
        SC_pd = s_xA[:, 4096:8192]
        SC_w0 = s_xA[:, 8192:12288]
        SC_w1 = s_xA[:, 12288:16384]
        R_d = ["sx0", "sx1"]; R_pd = ["sx2", "sx3"]; R_w0 = ["sx4", "sx5"]; R_w1 = ["sx6", "sx7"]
        xnA_b = s_xnA[:, :]
        SB_u = xnA_b[:, 0:4096]; R_u = ["sxn0", "sxn1"]
        SB_h = xnA_b[:, 4096:8192]; R_h = ["sxn2", "sxn3"]
        SB_c = xnA_b[:, 8192:12288]; R_c = ["sxn4", "sxn5"]
        SB_bm = bslot(8, 2); R_bm = [BB[8], BB[9]]
        SB_cm = bslot(10, 2); R_cm = [BB[10], BB[11]]
        ABF = bslot(4, 2); R_af = [BB[4], BB[5]]
        W1B = bslot(6, 2); R_w1b = [BB[6], BB[7]]
        YF32 = bigB_f32[:, 0:4096]; R_y = [BB[0], BB[1], BB[2], BB[3]]
        YW0 = S - 1024
        YW = slice(YW0, S)
        for ci in range(8):
            rowsl = lambda t: t[bass.ts(state["rank"], 1024), :]

            for c8 in range(8):
                bank = pbank[c8 % 4]
                csl = slice(c8 * 512, (c8 + 1) * 512)
                kb.add("tensor",
                       lambda g, c8=c8, bank=bank, ci=ci:
                       g.matmul(out=bank[0:128, 0:512],
                                lhsT=s_wdtloc[:, ci * 128:(ci + 1) * 128],
                                rhs=s_di[:, c8 * 512:(c8 + 1) * 512], start=True, stop=True),
                       reads=[s_wdtloc] + DIR, writes=[PB_R[c8 % 4]], tag="mm_dt")
                kb.add("scalar",
                       lambda g, bank=bank, ci=ci:
                       g.activation(out=s_tt[:, 0:512], in_=bank[0:128, 0:512],
                                    func=AF.Identity, bias=s_bdtloc[:, ci:ci + 1]),
                       reads=[PB_R[c8 % 4], s_bdtloc], writes=["tt_a"], tag="sp_b")
                kb.add("scalar",
                       lambda g: g.activation(out=s_tt[:, 512:1024], in_=s_tt[:, 0:512],
                                              func=AF.Abs),
                       sreads=["tt_a"], writes=["tt_a"], tag="sp_abs")
                kb.add("scalar",
                       lambda g: g.activation(out=s_tt[:, 1024:1536], in_=s_tt[:, 512:1024],
                                              func=AF.Exp, scale=-1.0),
                       sreads=["tt_a"], writes=["tt_b"], tag="sp_exp")
                kb.add("scalar",
                       lambda g: g.activation(out=s_tt[:, 512:1024], in_=s_tt[:, 1024:1536],
                                              func=AF.Ln, bias=s_one11[:, 0:1]),
                       reads=[s_one11], sreads=["tt_b"], writes=["tt_a"], tag="sp_ln")
                kb.add("scalar",
                       lambda g: g.activation(out=s_tt[:, 1536:2048], in_=s_tt[:, 0:512],
                                              func=AF.Relu),
                       sreads=["tt_a"], writes=["tt_b"], tag="sp_rl")
                kb.add(V, lambda g, csl=csl: g.tensor_add(out=SC_delta[:, csl],
                                                          in0=s_tt[:, 512:1024],
                                                          in1=s_tt[:, 1536:2048]),
                       sreads=["tt_a", "tt_b"], writes=R_d, tag="sp_add")
            kb.add(V, lambda g: g.tensor_tensor_scan(out=SC_pd[:, :], data0=SC_delta[:, :],
                                                     data1=SC_delta[:, :], initial=0.0,
                                                     op0=OP.add, op1=OP.bypass),
                   reads=R_d, writes=R_pd, tag="pd_scan")
            kb.add(V, lambda g: g.tensor_copy(out=s_dcol[:, 0:1], in_=SC_pd[:, 4095:4096]),
                   sreads=R_pd, writes=[s_dcol], tag="dsum")
            kb.add(V, lambda g: g.tensor_scalar(out=SC_pd[:, :], in0=SC_pd[:, :],
                                                scalar1=s_dcol[:, 0:1], scalar2=None,
                                                op0=OP.subtract),
                   sreads=[s_dcol] + R_pd, writes=R_pd, tag="m_sub")
            kb.dma("gpsimd", SB_u[:, :], dr_uloc[ci * 128:(ci + 1) * 128, :],
                   reads=["druloc"], writes=R_u, tag="u_ld")
            kb.add(V, lambda g: g.tensor_mul(out=ABF[:, :], in0=SC_delta[:, :],
                                             in1=SB_u[:, :]),
                   reads=R_d + R_u, writes=R_af, tag="a_mul")
            for n in range(NST):
                def _exp(n=n):
                    kb.add("scalar",
                           lambda g, n=n: g.activation(out=SC_w0[:, :],
                                                       in_=SC_delta[:, :],
                                                       func=AF.Exp, scale=-(n + 1.0)),
                           reads=R_d, writes=R_w0, tag="sc_exp")
                def _sig(n=n):
                    kb.add("scalar",
                           lambda g, n=n: g.activation(out=SB_c[:, YW],
                                                       in_=SC_pd[:, YW],
                                                       func=AF.Sigmoid,
                                                       scale=(n + 1.0),
                                                       bias=s_b27[:, 0:1]),
                           reads=R_pd + [s_b27], writes=R_c, tag="sc_sig")
                if n % 2 == 0:
                    _exp(); _sig()
                else:
                    _sig(); _exp()
                kb.dma("gpsimd", SB_bm[:, :],
                       dr_bcrows[n:n + 1, :].broadcast_to([128, S]),
                       reads=["bcrows"], writes=R_bm, tag="bm_rep")
                kb.dma("gpsimd", SB_cm[:, :],
                       dr_bcrows[16 + n:17 + n, :].broadcast_to([128, S]),
                       reads=["bcrows"], writes=R_cm, tag="cm_rep")
                kb.add(V, lambda g: g.tensor_mul(out=W1B[:, :], in0=ABF[:, :],
                                                 in1=SB_bm[:, :]),
                       reads=R_af + R_bm, writes=R_w1b, tag="dbu")
                kb.add(V, lambda g: g.tensor_tensor_scan(out=SB_h[:, :],
                                                         data0=SC_w0[:, :], data1=W1B[:, :],
                                                         initial=0.0, op0=OP.mult, op1=OP.add),
                       reads=R_w0 + R_w1b, writes=R_h, tag="hscan")
                kb.add(V, lambda g: g.tensor_mul(out=SC_w0[:, YW],
                                                 in0=SB_h[:, YW], in1=SB_c[:, YW]),
                       reads=R_h + R_c, writes=R_w0, tag="hc")
                kb.add(V, lambda g: g.tensor_mul(out=SC_w1[:, YW], in0=SC_w0[:, YW],
                                                 in1=SB_cm[:, YW]),
                       reads=R_w0 + R_cm, writes=R_w1, tag="hcc")
                if n == 0:
                    kb.add(V, lambda g: g.memset(YF32[:, 0:YW0], 0.0),
                           writes=R_y, tag="ymem")
                    kb.add(V, lambda g: g.tensor_copy(out=YF32[:, YW], in_=SC_w1[:, YW]),
                           reads=R_w1 + R_y, writes=R_y, tag="y0")
                else:
                    kb.add(V, lambda g: g.tensor_add(out=YF32[:, YW], in0=YF32[:, YW],
                                                     in1=SC_w1[:, YW]),
                           reads=R_w1 + R_y, writes=R_y, tag="yadd")
            kb.dma("gpsimd", dr_yloc[ci * 128:(ci + 1) * 128, :], YF32[:, :],
                   reads=R_y, writes=["dryloc"], tag="y_st")
        kb.add("gpsimd",
               lambda g: g.dma_start(out=sh_y[bass.ts(state["rank"], 1024), :],
                                     in_=dr_yloc[:, :]),
               reads=["dryloc"], writes=["shmem4"], tag="y_wr", dma=True)
        barrier(extra_reads=["shmem4"])
        if MAMBA_CUT <= 4:
            return
        yt_t = [bslot(j) for j in range(12)] + \
               [s_xnA[:, (4 + j) * 2048:(5 + j) * 2048] for j in range(4)]
        YTR = [[BB[j]] for j in range(12)] + [["sxn4"], ["sxn5"], ["sxn6"], ["sxn7"]]
        kb.add("gpsimd",
               lambda g: g.dma_start(out=dr_yepi[:, :],
                                     in_=sh_y[:, bass.ts(state["rank"], TC)]),
               reads=["barrier"], writes=["dryepi"], tag="yepi", dma=True)
        kb.add("gpsimd",
               lambda g: g.dma_start(out=dr_uepi[:, :],
                                     in_=sh_xis[:, bass.ts(state["rank"], TC)]),
               reads=["barrier"], writes=["druepi"], tag="uepi", dma=True)
        for j in range(16):
            ybuf = s_xnA[:, 0:2048]
            ubuf = s_xnA[:, 2048:4096]
            rbuf = s_xnA[:, 4096:6144]
            kb.dma("gpsimd", ybuf, dr_yepi[j * 128:(j + 1) * 128, :],
                   reads=["dryepi"], writes=["sxn0"], tag="y_ld")
            kb.dma("gpsimd", ubuf, dr_uepi[j * 128:(j + 1) * 128, :],
                   reads=["druepi"], writes=["sxn1"], tag="u_ld2")
            kb.dma("sync", rbuf,
                   dr_res[j:j + 1, :].rearrange("one (p c) -> p (one c)", p=128),
                   reads=["drres"], writes=["sxn2"], tag="res_ld")
            kb.add("scalar", lambda g: g.activation(out=s_tt[:, 0:2048],
                                                    in_=rbuf,
                                                    func=AF.Silu),
                   reads=["sxn2"], writes=["tt_a", "tt_b"], tag="swish")
            kb.add(V, lambda g, j=j:
                   g.scalar_tensor_tensor(out=s_xA[:, 0:2048],
                                          in0=ubuf, scalar=s_dvec[:, j:j + 1],
                                          in1=ybuf,
                                          op0=OP.mult, op1=OP.add),
                   reads=["sxn0", "sxn1", s_dvec], writes=["sx0"], tag="ud_y")
            kb.add(V, lambda g, j=j: g.tensor_mul(out=yt_t[j][:, :], in0=s_xA[:, 0:2048],
                                                  in1=s_tt[:, 0:2048]),
                   reads=["sx0", "tt_a", "tt_b"], writes=YTR[j], tag="ytot")
        for i in range(8):
            kb.dma("sync", s_x[i], dr_x[i:i + 1, :].rearrange("one (p c) -> p (one c)", p=128),
                   reads=[f"drx{i}"], writes=[XR[i]], tag="x_rel")
        wor = [s_xiex[:, 8192:10240], s_xiex[:, 10240:12288]]
        WOR = [["xe4"], ["xe5"]]
        for m in range(8):
            p = m % 2
            kb.dma("gpsimd", wor[p][:, 0:2048],
                   w_out[:, m * 128:(m + 1) * 128].rearrange("(k p) c -> p k c", p=128),
                   reads=WOR[p], writes=WOR[p], tag="wo_ld")
            for t8 in range(4):
                tsl = slice(t8 * 512, (t8 + 1) * 512)
                bank = pbank[4 + t8 % 2]
                for k in range(16):
                    kb.add("tensor",
                           lambda g, k=k, tsl=tsl, bank=bank, p=p:
                           g.matmul(out=bank[0:128, 0:512],
                                    lhsT=wor[p][:, k * 128:(k + 1) * 128],
                                    rhs=yt_t[k][:, tsl], start=(k == 0), stop=(k == 15)),
                           reads=WOR[p] + YTR[k], writes=[PB_R[4 + t8 % 2]], tag="mm_wo")
                kb.add(V, lambda g, m=m, tsl=tsl, bank=bank:
                       g.tensor_add(out=s_x[m][:, tsl], in0=s_x[m][:, tsl],
                                    in1=bank[0:128, 0:512]),
                       reads=[PB_R[4 + t8 % 2], XR[m]], writes=[XR[m]], tag="wo_add")

    ffn_call(0)
    if DEV_STAGE >= 2:
        mamba()
    if DEV_STAGE >= 3:
        ffn_call(1)

    for tt in range(16):
        stg = s_stage[tt % 2]
        for dc in range(8):
            bank = pbank[dc % 4]
            kb.add("tensor",
                   lambda g, tt=tt, dc=dc, bank=bank:
                   g.transpose(out=bank[0:128, 0:128], in_=s_x[dc][:, tt * 128:(tt + 1) * 128],
                               identity=s_ident[:]),
                   reads=[XR[dc], s_ident], writes=[PB_R[dc % 4]], tag="otp")
            kb.add("vector",
                   lambda g, dc=dc, bank=bank, stg=stg:
                   g.tensor_copy(out=stg[:, dc * 128:(dc + 1) * 128], in_=bank[0:128, 0:128]),
                   reads=[PB_R[dc % 4]], writes=[STGR[tt % 2]], tag="oev")
        kb.dma("sync", out_tm[tt * 128:(tt + 1) * 128, :], stg[:, :],
               reads=[STGR[tt % 2]], writes=["out"], tag="ostore")

    kb.add("sync", lambda g: g.nop(), reads=["out", "dbg"], tag="fence")

    kb.analyze()
    if _os.environ.get("KBDUMP"):
        want = set(_os.environ["KBDUMP"].split(","))
        for op in kb.ops:
            if op.tag in want:
                print(f"[{op.tag}] eng={op.eng} q={op.queue} idx={op.idx} "
                      f"waits={op.waits} inc={op.need_inc}")
    skeys = sorted({op.skey for op in kb.ops} | set(QUEUES) | {"poll", "flag"})
    if _os.environ.get("KBDUMP"):
        print(f"semaphores: {len(skeys)}")
    assert len(skeys) <= 100, f"too many semaphores: {len(skeys)}"
    sems = {}
    for qn in skeys:
        sems[qn] = kb.es.enter_context(nc.semaphore(f"sem_{qn}"))
    per_eng = kb.materialize(sems)

    with nc.Block() as block:
        def runner(eng):
            def run(g):
                for op, waits in per_eng[eng]:
                    for s, v in waits:
                        g.wait_ge(s, v)
                    inst = op.fn(g)
                    if op.need_inc:
                        if inst is None:
                            inst = g.nop()
                        inst.then_inc(sems[op.skey], op.inc_amount)
            return run

        block.sync(runner("sync"))
        block.gpsimd(runner("gpsimd"))
        block.tensor(runner("tensor"))
        block.vector(runner("vector"))
        block.scalar(runner("scalar"))

    kb.es.close()
    return nc, state


_PROG = None

def _get_prog():
    global _PROG
    if _PROG is None:
        _PROG = build_program()[0]
    return _PROG


def kernel(**inputs):
    i = {k: np.asarray(v) for k, v in inputs.items()}
    x = np.ascontiguousarray(i["L0_attn_x"], np.float32)
    w_in = np.ascontiguousarray(i["L0_attn_W_in"], np.float32)
    conv_w = np.ascontiguousarray(i["L0_attn_conv_w"], np.float32)
    conv_b = np.ascontiguousarray(i["L0_attn_conv_b"], np.float32)
    w_x = np.ascontiguousarray(i["L0_attn_W_x"], np.float32)
    w_dt = np.ascontiguousarray(i["L0_attn_W_dt"], np.float32)
    b_dt = np.ascontiguousarray(i["L0_attn_b_dt"], np.float32)
    d_vec = np.ascontiguousarray(i["L0_attn_D"], np.float32)
    w_out = np.ascontiguousarray(i["L0_attn_W_out"], np.float32)
    gw_t = np.ascontiguousarray(i["L0_ffn_gate_w"].T, np.float32)
    wg_t = np.ascontiguousarray(np.swapaxes(i["L0_ffn_gate_proj_w"], 1, 2), np.float32)
    wu_t = np.ascontiguousarray(np.swapaxes(i["L0_ffn_up_proj_w"], 1, 2), np.float32)
    wd_t = np.ascontiguousarray(np.swapaxes(i["L0_ffn_down_proj_w"], 1, 2), np.float32)

    def cols(v):
        return np.ascontiguousarray(v.reshape(16, 128).T, np.float32)
    convw_c = [cols(conv_w[k]) for k in range(4)]
    convb_c = cols(conv_b)
    bdt_c = cols(b_dt)
    dvec_c = cols(d_vec)

    rng = np.random.default_rng()
    nonce = rng.integers(1, 2**31 - 1, size=(1, 8), dtype=np.int64).astype(np.uint32)

    nc = _get_prog()
    in_maps = []
    for c in range(8):
        b, h = c // 2, c % 2
        im = dict(
            x_tm=np.ascontiguousarray(x[b, h * TC:(h + 1) * TC, :]),
            w_in=w_in, convb_c=convb_c, bdt_c=bdt_c, dvec_c=dvec_c,
            w_x=w_x, w_dt=w_dt, w_out=w_out, gw_t=gw_t,
            wg_t=wg_t, wu_t=wu_t, wd_t=wd_t,
            rank_in=np.array([[h]], np.uint32),
            rankf_in=np.full((16, 1), float(h), np.float32),
            rankc_in=np.full((128, 1), float(h), np.float32),
            nonces=nonce,
        )
        for k in range(4):
            im[f"convw{k}"] = convw_c[k]
        in_maps.append(im)

    res = run_bass_kernel_spmd(nc, in_maps, core_ids=list(range(8)))
    out = np.empty((B, S, D), np.float32)
    for c in range(8):
        b, h = c // 2, c % 2
        out[b, h * TC:(h + 1) * TC, :] = res.results[c]["out_tm"]
    return out


# version 10
# speedup vs baseline: 2.2214x; 1.5515x over previous
import numpy as np
import ml_dtypes
import concourse.bass as bass
import concourse.mybir as mybir
from concourse.bass_utils import run_bass_kernel_spmd
from contextlib import ExitStack

F32 = mybir.dt.float32
BF16 = mybir.dt.bfloat16
U32 = mybir.dt.uint32
AF = mybir.ActivationFunctionType
OP = mybir.AluOpType
AX = mybir.AxisListType

B, S, D = 4, 4096, 1024
TC = 2048
DIN = 2048
NST = 16
DTR = 64
HF = 2048
NE = 8
CAP = 2560
LN_EPS = 1e-5
NEG_LN_EPS12 = 27.631021115928547

import os as _os
DEV_STAGE = int(_os.environ.get("DEV_STAGE", "3"))
MAMBA_CUT = int(_os.environ.get("MAMBA_CUT", "9"))

ENGS = ("sync", "gpsimd", "tensor", "vector", "scalar")
QUEUES = ENGS + ("dsync", "dgpsimd")


class Op:
    __slots__ = ("eng", "queue", "fn", "reads", "writes", "sreads", "tag", "idx",
                 "need_inc", "waits", "inc_amount", "skey", "sidx")

    def __init__(self, eng, queue, fn, reads, writes, tag, inc_amount):
        self.eng = eng
        self.queue = queue
        self.fn = fn
        self.reads = reads
        self.writes = writes
        self.sreads = ()
        self.tag = tag
        self.idx = None
        self.need_inc = False
        self.waits = {}
        self.inc_amount = inc_amount


class KB:
    def __init__(self):
        self.nc = bass.Bass("TRN2", target_bir_lowering=False, debug=False,
                            num_devices=8, dynamic_dma_scratch_size=16384)
        self.es = ExitStack()
        self.ops = []

    def sb(self, name, shape, dt=F32):
        return self.es.enter_context(self.nc.sbuf_tensor(name, list(shape), dt))

    @staticmethod
    def _key(r):
        if isinstance(r, str):
            return r
        n = getattr(r, "name", None)
        if n is not None:
            return n
        t = getattr(r, "tensor", None)
        if t is not None:
            return t.name
        raise TypeError(f"bad resource {r!r}")

    def add(self, eng, fn, reads=(), writes=(), sreads=(), tag="", dma=False,
            skey=None):
        queue = ("d" + eng) if dma else eng
        amount = 16 if dma else 1
        op = Op(eng, queue, fn,
                tuple(self._key(r) for r in reads) + tuple(self._key(r) for r in sreads),
                tuple(self._key(r) for r in writes), tag, amount)
        op.sreads = tuple(self._key(r) for r in sreads)
        if dma:
            op.skey = skey or (op.writes[0] if op.writes else queue)
        else:
            op.skey = queue
        self.ops.append(op)
        return op

    def dma(self, eng, out_ap, in_ap, reads=(), writes=(), tag="", skey=None, **kw):
        return self.add(
            eng,
            lambda g, o=out_ap, i=in_ap, kw=kw: g.dma_start(out=o, in_=i, **kw),
            reads=reads, writes=writes, tag=tag, dma=True, skey=skey)

    def analyze(self):
        counters = {q: 0 for q in QUEUES}
        scount = {}
        cur_last = {}
        last_writer = {}
        readers = {}
        for op in self.ops:
            counters[op.queue] += 1
            op.idx = counters[op.queue]
            scount[op.skey] = scount.get(op.skey, 0) + 1
            op.sidx = scount[op.skey]
            if op.queue.startswith("d"):
                op.need_inc = True
            deps = []
            for r in op.reads:
                w = last_writer.get(r)
                if w is not None:
                    deps.append((w, r))
            for w_res in op.writes:
                w = last_writer.get(w_res)
                if w is not None:
                    deps.append((w, w_res))
                deps.extend((x, w_res) for x in readers.get(w_res, ()))
            sread_set = set(op.sreads)
            for d, dres in deps:
                if d is op:
                    continue
                if d.queue == op.queue and d.queue.startswith("d"):
                    pass
                elif d.eng == op.eng and d.queue == d.eng:
                    if op.eng == "tensor":
                        continue
                    if dres not in sread_set and (op.idx - d.idx) > 4:
                        continue
                d.need_inc = True
                if d.queue.startswith("d"):
                    tgt = cur_last[d.skey]
                    cur = op.waits.get(d.skey, 0)
                    op.waits[d.skey] = max(cur, tgt.sidx)
                else:
                    cur = op.waits.get(d.queue, 0)
                    op.waits[d.queue] = max(cur, d.idx)
            cur_last[op.skey] = op
            for r in op.reads:
                readers.setdefault(r, []).append(op)
            for w_res in op.writes:
                last_writer[w_res] = op
                readers[w_res] = []

    def materialize(self, sems):
        inc_prefix = {}
        run = {}
        for op in self.ops:
            if op.need_inc:
                run[op.skey] = run.get(op.skey, 0) + op.inc_amount
            inc_prefix.setdefault(op.skey, {})[op.sidx] = run.get(op.skey, 0)
        per_eng = {e: [] for e in ENGS}
        seen = {e: {} for e in ENGS}
        for op in self.ops:
            waits = []
            for q, idx in sorted(op.waits.items()):
                val = inc_prefix[q].get(idx)
                assert val is not None
                if seen[op.eng].get(q, -1) >= val:
                    continue
                seen[op.eng][q] = val
                waits.append((sems[q], val))
            per_eng[op.eng].append((op, waits))
        return per_eng


def build_program():
    kb = KB()
    nc = kb.nc
    sb = kb.sb

    def ein(name, shape, dt=F32):
        return nc.dram_tensor(name, list(shape), dt, kind="ExternalInput").ap()

    x_tm = ein("x_tm", (TC, D))
    w_in = ein("w_in", (D, 2 * DIN))
    convw_c = [ein(f"convw{k}", (128, 16)) for k in range(4)]
    convb_c = ein("convb_c", (128, 16))
    bdt_c = ein("bdt_c", (128, 16))
    dvec_c = ein("dvec_c", (128, 16))
    w_x = ein("w_x", (DIN, 96))
    w_dt = ein("w_dt", (DTR, DIN))
    w_out = ein("w_out", (DIN, D))
    gw_t = ein("gw_t", (D, NE))
    wg_t = ein("wg_t", (NE, D, HF))
    wu_t = ein("wu_t", (NE, D, HF))
    wd_t = ein("wd_t", (NE, HF, D))
    rank_in = ein("rank_in", (1, 1), U32)
    rankf_in = ein("rankf_in", (16, 1))
    rankc_in = ein("rankc_in", (128, 1))
    nonces = ein("nonces", (1, 8), U32)

    out_tm = nc.dram_tensor("out_tm", [TC, D], F32, kind="ExternalOutput").ap()
    dbg_comb = nc.dram_tensor("dbg_comb", [NE, TC], F32, kind="ExternalOutput").ap()
    dbg_cnt = nc.dram_tensor("dbg_cnt", [16, 2], F32, kind="ExternalOutput").ap()
    dbg_glog = nc.dram_tensor("dbg_glog", [128, 128], F32, kind="ExternalOutput").ap()
    dbg_ohf = nc.dram_tensor("dbg_ohf", [16, TC], F32, kind="ExternalOutput").ap()
    dbg_xn = nc.dram_tensor("dbg_xn", [128, TC], BF16, kind="ExternalOutput").ap()
    dbg_rkf = nc.dram_tensor("dbg_rkf", [16, TC], F32, kind="ExternalOutput").ap()

    shared = lambda n, sh, dt=F32: nc.dram_tensor(n, list(sh), dt, addr_space="Shared").ap()
    sh_flag = shared("sh_flag", (1, 16), U32)
    sh_cnt = [shared(f"sh_cnt{i}", (2, 16)) for i in range(2)]
    if DEV_STAGE >= 2:
        sh_halo = shared("sh_halo", (DIN, 6))
        sh_xis = shared("sh_xis", (DIN, S), BF16)
        sh_xdbl = shared("sh_xdbl", (96, S))
        sh_y = shared("sh_y", (DIN, S), BF16)
    dr_comb = nc.dram_tensor("dr_comb", [NE, TC], F32).ap()
    dr_res = nc.dram_tensor("dr_res", [16, TC * 128], BF16).ap()
    dr_x = nc.dram_tensor("dr_x", [8, TC * 128], F32).ap()
    dr_bcrows = nc.dram_tensor("dr_bcrows", [32, S], BF16).ap()
    dr_uloc = nc.dram_tensor("dr_uloc", [1024, S], BF16).ap()
    dr_yloc = nc.dram_tensor("dr_yloc", [1024, S], BF16).ap()
    dr_yepi = nc.dram_tensor("dr_yepi", [DIN, TC], BF16).ap()
    dr_uepi = nc.dram_tensor("dr_uepi", [DIN, TC], BF16).ap()
    dr_xdbl = nc.dram_tensor("dr_xdbl", [96, TC], F32).ap()

    ident_h = nc.inline_tensor(np.eye(128, dtype=np.float32), name="c_ident")
    ones_r = nc.inline_tensor(np.ones((1, 128), np.float32), name="c_ones_r")
    ones_c = nc.inline_tensor(np.ones((128, 1), np.float32), name="c_ones_c")
    selk = np.zeros((16, 2), np.float32); selk[:8, 0] = 1; selk[8:, 1] = 1
    selk_h = nc.inline_tensor(selk, name="c_selk")
    selkT_h = nc.inline_tensor(np.ascontiguousarray(selk.T), name="c_selkT")
    sele = np.zeros((16, 8), np.float32)
    for i in range(16): sele[i, i % 8] = 1
    sele_h = nc.inline_tensor(sele, name="c_sele")
    eps_h = nc.inline_tensor(np.full((1, 1), LN_EPS, np.float32), name="c_eps")
    one_h = nc.inline_tensor(np.full((128, 1), 1.0, np.float32), name="c_one")
    b27_h = nc.inline_tensor(np.full((128, 1), NEG_LN_EPS12, np.float32), name="c_b27")

    s_ident = sb("s_ident", (128, 128))
    s_ones_r = sb("s_ones_r", (1, 128))
    s_ones_c = sb("s_ones_c", (128, 1))
    s_selk = sb("s_selk", (16, 2))
    s_selkT = sb("s_selkT", (2, 16))
    s_sele = sb("s_sele", (16, 8))
    s_rank = sb("s_rank", (1, 1), U32)
    s_rankf = sb("s_rankf", (16, 1))
    s_nonce = sb("s_nonce", (1, 8), U32)
    s_flags = sb("s_flags", (1, 16), U32)
    s_convw = [sb(f"s_convw{k}", (128, 16)) for k in range(4)]
    s_convb = sb("s_convb", (128, 16))
    s_bdt = sb("s_bdt", (128, 16))
    s_dvec = sb("s_dvec", (128, 16))
    s_gcol = sb("s_gcol", (128, 8))
    s_glog = sb("s_glog", (128, 128))
    s_g1 = sb("s_g1", (128, 8))
    s_g2 = sb("s_g2", (128, 8))
    s_srt = sb("s_srt", (128, 8))
    s_z8 = sb("s_z8", (128, 8))
    s_packT = sb("s_packT", (128, 18))
    s_cnt2 = sb("s_cnt2", (2, 16))
    s_cntT = sb("s_cntT", (16, 2))
    s_off = sb("s_off", (16, 1))
    s_gw = [sb(f"s_gw{i}", (128, NE), BF16) for i in range(8)]

    s_xA = sb("s_xA", (128, 8 * TC))
    s_x = [s_xA[:, i * TC:(i + 1) * TC] for i in range(8)]
    XR = [f"sx{i}" for i in range(8)]
    s_xnA = sb("s_xnA", (128, 8 * TC), BF16)
    s_xn = [s_xnA[:, i * TC:(i + 1) * TC] for i in range(8)]
    XNR = [f"sxn{i}" for i in range(8)]
    s_di = s_xnA[0:64, 12288:16384]
    DIR = ["sxn6", "sxn7"]
    s_tt = sb("s_tt", (128, TC))
    s_stage = [sb(f"s_stage{i}", (128, D)) for i in range(2)]
    STGR = ["stg0", "stg1"]
    s_rowsA = sb("s_rowsA", (128, TC))
    s_rowsB = sb("s_rowsB", (128, TC))
    ohf = s_rowsA[0:16, :]
    comb = s_rowsA[64:72, :]
    r_mean = s_rowsB[0:1, :]
    r_tmp = s_rowsA[0:1, :]
    wrow = s_rowsB[0:2, :]
    s_off0 = sb("s_off0", (16, 1))
    s_eps = sb("s_eps", (1, 1))
    s_one11 = sb("s_one11", (128, 1))
    s_b27 = sb("s_b27", (128, 1))
    s_rankc = sb("s_rankc", (128, 1))
    s_dcol = sb("s_dcol", (128, 1))
    s_wx = [sb(f"s_wx{k}", (128, 96), BF16) for k in range(2)]
    s_wdtloc = sb("s_wdtloc", (64, 1024), BF16)
    s_bdtloc = sb("s_bdtloc", (128, 8))
    s_hx = sb("s_hx", (128, 8))
    s_resst = sb("s_resst", (128, 512), BF16)
    s_xiex = sb("s_xiex", (128, 12288), BF16)

    s_bigB = sb("s_bigB", (128, 24576), BF16)
    bigB_f32 = s_bigB[:, :].bitcast(F32)
    BB = [f"bb{i}" for i in range(12)]
    def bslot(i, w=1):
        return s_bigB[:, i * 2048:(i + w) * 2048]
    s_wgq = [bslot(0, 2)[:, k * 512:(k + 1) * 512] for k in range(8)]
    WGR = [BB[0] if k < 4 else BB[1] for k in range(8)]
    s_wuq = [bslot(2, 2)[:, k * 512:(k + 1) * 512] for k in range(8)]
    WUR = [BB[2] if k < 4 else BB[3] for k in range(8)]
    s_wdq = [bslot(4, 2)[:, k * 1024:(k + 1) * 1024] for k in range(4)]
    WDR = [BB[4] if k < 2 else BB[5] for k in range(4)]
    s_p = [bslot(6 + k) for k in range(4)]
    PR = [BB[6 + k] for k in range(4)]
    s_sg = sb("s_sg", (128, 256), BF16)
    s_combbc = sb("s_combbc", (128, 256), BF16)

    pbank = [kb.es.enter_context(nc.psum_tensor(f"pbank{i}", [128, 512], F32))
             for i in range(8)]
    PB_R = [f"pb{i}" for i in range(8)]

    state = {"rank": None, "bk": 0, "pollbase": 0}

    for h_, d_ in ((ident_h, s_ident), (ones_r, s_ones_r), (ones_c, s_ones_c),
                   (selk_h, s_selk), (selkT_h, s_selkT), (sele_h, s_sele)):
        kb.dma("sync", d_[:], h_.ap(), writes=[d_], tag="consts", skey="consts")
    kb.dma("sync", s_eps[:], eps_h.ap(), writes=[s_eps], tag="consts", skey="consts")
    kb.dma("sync", s_one11[:], one_h.ap(), writes=[s_one11], tag="consts", skey="consts")
    kb.dma("sync", s_b27[:], b27_h.ap(), writes=[s_b27], tag="consts", skey="consts")
    kb.dma("sync", s_rankc[:], rankc_in, writes=[s_rankc], tag="consts", skey="consts")
    kb.dma("sync", s_rank[:], rank_in, writes=[s_rank], tag="consts", skey="consts")
    kb.dma("sync", s_rankf[:], rankf_in, writes=[s_rankf], tag="consts", skey="consts")
    kb.dma("sync", s_nonce[:], nonces, writes=[s_nonce], tag="consts", skey="consts")
    for k in range(4):
        kb.dma("sync", s_convw[k][:], convw_c[k], writes=[s_convw[k]], tag="consts", skey="consts")
    kb.dma("sync", s_convb[:], convb_c, writes=[s_convb], tag="consts", skey="consts")
    kb.dma("sync", s_bdt[:], bdt_c, writes=[s_bdt], tag="consts", skey="consts")
    kb.dma("sync", s_dvec[:], dvec_c, writes=[s_dvec], tag="consts", skey="consts")
    for dc in range(8):
        kb.dma("gpsimd", s_gw[dc][:], gw_t[dc * 128:(dc + 1) * 128, :],
               writes=[s_gw[dc]], tag="gw", skey="gws")

    def _ld_rank(g):
        state["rank"] = g.value_load(s_rank[0:1, 0:1])
        state["pollreg"] = g.alloc_register("pollcnt")
        g.reg_mov(state["pollreg"], 0)
        for rn in ("b_want", "b_f0", "b_f1", "b_nd"):
            state[rn] = g.alloc_register(rn)
        return None
    kb.add("gpsimd", _ld_rank, reads=[s_rank], tag="rank")

    def barrier(extra_reads=()):
        k = state["bk"]; state["bk"] += 1
        def _go(g, k=k):
            rank = state["rank"]
            g.dma_start(out=sh_flag[0:1, 2 * k + 0:2 * k + 2][0:1, bass.ts(rank, 1)],
                        in_=s_nonce[0:1, k:k + 1]).then_inc(sems["flag"], 16)
            g.wait_ge(sems["flag"], 16 * (k + 1))
            want, f0, f1, nd = (state["b_want"], state["b_f0"],
                                state["b_f1"], state["b_nd"])
            g.reg_load(want, s_nonce[0:1, k:k + 1])
            pc = state["pollreg"]
            g.reg_mov(nd, 1)
            with g.While(nd):
                g.reg_add(pc, pc, 16)
                g.dma_start(out=s_flags[0:1, 2 * k:2 * k + 2],
                            in_=sh_flag[0:1, 2 * k:2 * k + 2]).then_inc(sems["poll"], 16)
                g.wait_ge(sems["poll"], pc)
                g.reg_load(f0, s_flags[0:1, 2 * k:2 * k + 1])
                g.reg_load(f1, s_flags[0:1, 2 * k + 1:2 * k + 2])
                g.reg_alu(f0, f0, want, OP.subtract)
                g.reg_alu(f1, f1, want, OP.subtract)
                g.reg_alu(nd, f0, f1, OP.bitwise_or)
            return g.nop()
        return kb.add("gpsimd", _go, reads=[s_nonce, s_flags, *extra_reads],
                      writes=["barrier", s_flags], tag=f"barrier{k}")

    for tt in range(16):
        stg = s_stage[tt % 2]
        kb.dma("sync", stg[:, :], x_tm[tt * 128:(tt + 1) * 128, :],
               writes=[STGR[tt % 2]], tag="xload")
        for dc in range(8):
            bank = pbank[dc % 4]
            kb.add("tensor",
                   lambda g, stg=stg, dc=dc, bank=bank:
                   g.transpose(out=bank[0:128, 0:128],
                               in_=stg[:, dc * 128:(dc + 1) * 128], identity=s_ident[:]),
                   reads=[STGR[tt % 2], s_ident], writes=[PB_R[dc % 4]], tag="xtp")
            kb.add("vector",
                   lambda g, tt=tt, dc=dc, bank=bank:
                   g.tensor_copy(out=s_x[dc][:, tt * 128:(tt + 1) * 128],
                                 in_=bank[0:128, 0:128]),
                   reads=[PB_R[dc % 4]], writes=[XR[dc]], tag="xev")

    def layernorm():
        for qt in range(4):
            sl = slice(qt * 512, (qt + 1) * 512)
            for dc in range(8):
                kb.add("scalar",
                       lambda g, dc=dc, sl=sl:
                       g.activation(out=s_tt[:, 0:512], in_=s_x[dc][:, sl], func=AF.Square),
                       reads=[XR[dc]], writes=["tt_a"], tag="ln_sq")
                kb.add("tensor",
                       lambda g, dc=dc, sl=sl:
                       g.matmul(out=pbank[4][0:1, :], lhsT=s_ones_c[:], rhs=s_x[dc][:, sl],
                                start=(dc == 0), stop=(dc == 7)),
                       reads=[XR[dc], s_ones_c], writes=[PB_R[4]], tag="ln_s1")
                kb.add("tensor",
                       lambda g, dc=dc:
                       g.matmul(out=pbank[7][0:1, :], lhsT=s_ones_c[:], rhs=s_tt[:, 0:512],
                                start=(dc == 0), stop=(dc == 7)),
                       reads=["tt_a", s_ones_c], writes=[PB_R[7]], tag="ln_s2")
            kb.add("vector",
                   lambda g, sl=sl: g.tensor_scalar(out=r_mean[0:1, sl], in0=pbank[4][0:1, :],
                                                    scalar1=1.0 / D, scalar2=None, op0=OP.mult),
                   reads=[PB_R[4]], writes=["rB0"], tag="ln_mean")
            kb.add("vector",
                   lambda g, sl=sl: g.tensor_scalar(out=r_tmp[0:1, sl], in0=pbank[7][0:1, :],
                                                    scalar1=1.0 / D, scalar2=None, op0=OP.mult),
                   reads=[PB_R[7]], writes=["rA"], tag="ln_ssq")
            kb.add("scalar",
                   lambda g, sl=sl: g.activation(out=pbank[7][0:1, 0:512],
                                                 in_=r_mean[0:1, sl], func=AF.Square),
                   reads=["rB0"], writes=[PB_R[7]], tag="ln_m2")
            kb.add("vector",
                   lambda g, sl=sl: g.tensor_sub(out=r_tmp[0:1, sl], in0=r_tmp[0:1, sl],
                                                 in1=pbank[7][0:1, 0:512]),
                   reads=[PB_R[7], "rA"], writes=["rA"], tag="ln_var")
            kb.add("scalar",
                   lambda g, sl=sl: g.activation(out=pbank[7][0:1, 0:512], in_=r_tmp[0:1, sl],
                                                 func=AF.Ln, bias=s_eps[0:1, 0:1]),
                   reads=["rA", s_eps], writes=[PB_R[7]], tag="ln_ln")
            kb.add("scalar",
                   lambda g, sl=sl: g.activation(out=r_tmp[0:1, sl], in_=pbank[7][0:1, 0:512],
                                                 func=AF.Exp, scale=-0.5),
                   reads=[PB_R[7]], writes=["rA"], tag="ln_exp")
            kb.add("tensor",
                   lambda g, sl=sl: g.matmul(out=pbank[5][:, :], lhsT=s_ones_r[:],
                                             rhs=r_mean[0:1, sl], start=True, stop=True),
                   reads=["rB0", s_ones_r], writes=[PB_R[5]], tag="ln_bc1")
            kb.add("tensor",
                   lambda g, sl=sl: g.matmul(out=pbank[6][:, :], lhsT=s_ones_r[:],
                                             rhs=r_tmp[0:1, sl], start=True, stop=True),
                   reads=["rA", s_ones_r], writes=[PB_R[6]], tag="ln_bc2")
            for dc in range(8):
                kb.add("vector",
                       lambda g, dc=dc, sl=sl:
                       g.tensor_sub(out=s_tt[:, 512:1024], in0=s_x[dc][:, sl],
                                    in1=pbank[5][:, :]),
                       reads=[XR[dc], PB_R[5]], writes=["tt_b"], tag="ln_xn1")
                kb.add("vector",
                       lambda g, dc=dc, sl=sl:
                       g.tensor_mul(out=s_xn[dc][:, sl], in0=s_tt[:, 512:1024],
                                    in1=pbank[6][:, :]),
                       reads=["tt_b", PB_R[6]], writes=[XNR[dc]], tag="ln_xn2")

    def gating(which):
        shc = sh_cnt[which]
        for qt in range(4):
            sl = slice(qt * 512, (qt + 1) * 512)
            for dc in range(8):
                kb.add("tensor",
                       lambda g, dc=dc, sl=sl:
                       g.matmul(out=pbank[7][0:8, :], lhsT=s_gw[dc][:], rhs=s_xn[dc][:, sl],
                                start=(dc == 0), stop=(dc == 7)),
                       reads=[s_gw[dc], XNR[dc]], writes=[PB_R[7]], tag="g_log")
            kb.add("vector",
                   lambda g, sl=sl: g.tensor_copy(out=ohf[0:8, sl], in_=pbank[7][0:8, :]),
                   reads=[PB_R[7]], writes=["rA"], tag="g_ev")
        for tt in range(16):
            kb.add("tensor",
                   lambda g, tt=tt:
                   g.transpose(out=pbank[7][0:128, 0:8],
                               in_=ohf[0:8, tt * 128:(tt + 1) * 128],
                               identity=s_ident[0:8, 0:8]),
                   reads=["rA", s_ident], writes=[PB_R[7]], tag="g_tp")
            kb.add("vector",
                   lambda g, tt=tt:
                   g.tensor_copy(out=s_glog[:, tt * 8:(tt + 1) * 8], in_=pbank[7][0:128, 0:8]),
                   reads=[PB_R[7]], writes=[s_glog], tag="g_tpev")
        kb.add("vector", lambda g: g.memset(s_z8[:], 0.0), writes=[s_z8], tag="g_z")
        V = "vector"
        for tt in range(16):
            lg = s_glog[:, tt * 8:(tt + 1) * 8]
            kb.add(V, lambda g, lg=lg: g.reduce_max(out=s_gcol[:, 0:1], in_=lg, axis=AX.X),
                   reads=[s_glog], writes=[s_gcol], tag="gm")
            kb.add(V, lambda g, lg=lg: g.tensor_scalar(out=s_g1[:], in0=lg,
                                                       scalar1=s_gcol[:, 0:1], scalar2=None,
                                                       op0=OP.subtract),
                   reads=[s_glog], sreads=[s_gcol], writes=[s_g1], tag="gs")
            kb.add("scalar", lambda g: g.activation(out=s_g2[:], in_=s_g1[:], func=AF.Exp),
                   reads=[s_g1], writes=[s_g2], tag="ge")
            kb.add(V, lambda g: g.reduce_sum(out=s_gcol[:, 1:2], in_=s_g2[:], axis=AX.X),
                   reads=[s_g2], writes=[s_gcol], tag="gr")
            kb.add(V, lambda g: g.reciprocal(out=s_gcol[:, 2:3], in_=s_gcol[:, 1:2]),
                   reads=[], sreads=[s_gcol], writes=[s_gcol], tag="gi")
            kb.add(V, lambda g: g.tensor_scalar(out=s_g1[:], in0=s_g2[:],
                                                scalar1=s_gcol[:, 2:3], scalar2=None,
                                                op0=OP.mult),
                   reads=[s_g2], sreads=[s_gcol], writes=[s_g1], tag="gp")
            kb.add(V, lambda g: g.max(out=s_srt[:], in_=s_g1[:]),
                   reads=[s_g1], writes=[s_srt], tag="gmx")
            kb.add(V, lambda g: g.tensor_add(out=s_gcol[:, 3:4], in0=s_srt[:, 0:1],
                                             in1=s_srt[:, 1:2]),
                   reads=[], sreads=[s_srt], writes=[s_gcol], tag="gw1")
            kb.add(V, lambda g: g.tensor_scalar(out=s_gcol[:, 3:4], in0=s_gcol[:, 3:4],
                                                scalar1=1e-9, scalar2=None, op0=OP.add),
                   reads=[], sreads=[s_gcol], writes=[s_gcol], tag="gw2")
            kb.add(V, lambda g: g.reciprocal(out=s_gcol[:, 3:4], in_=s_gcol[:, 3:4]),
                   reads=[], sreads=[s_gcol], writes=[s_gcol], tag="gw3")
            kb.add(V, lambda g: g.tensor_scalar(out=s_gcol[:, 3:4], in0=s_gcol[:, 3:4],
                                                scalar1=0.5, scalar2=None, op0=OP.mult),
                   reads=[], sreads=[s_gcol], writes=[s_gcol], tag="gw4")
            kb.add(V, lambda g: g.tensor_scalar(out=s_packT[:, 16:17], in0=s_srt[:, 0:1],
                                                scalar1=s_gcol[:, 3:4], scalar2=None,
                                                op0=OP.mult),
                   reads=[], sreads=[s_srt, s_gcol], writes=[s_packT], tag="gw5")
            kb.add(V, lambda g: g.tensor_scalar(out=s_packT[:, 17:18], in0=s_srt[:, 1:2],
                                                scalar1=s_gcol[:, 3:4], scalar2=None,
                                                op0=OP.mult),
                   reads=[], sreads=[s_srt, s_gcol], writes=[s_packT], tag="gw6")
            kb.add(V, lambda g: g.tensor_scalar(out=s_g2[:], in0=s_g1[:],
                                                scalar1=s_srt[:, 0:1], scalar2=None,
                                                op0=OP.is_equal),
                   reads=[s_g1], sreads=[s_srt], writes=[s_g2], tag="geq0")
            kb.add(V, lambda g: g.tensor_tensor_scan(out=s_packT[:, 0:8], data0=s_g2[:],
                                                     data1=s_z8[:], initial=0.0,
                                                     op0=OP.add, op1=OP.add),
                   reads=[s_z8], sreads=[s_g2], writes=[s_packT], tag="gcs0")
            kb.add(V, lambda g: g.tensor_scalar(out=s_packT[:, 0:8], in0=s_packT[:, 0:8],
                                                scalar1=1.0, scalar2=None, op0=OP.is_equal),
                   reads=[], sreads=[s_packT], writes=[s_packT], tag="gff0a")
            kb.add(V, lambda g: g.tensor_mul(out=s_packT[:, 0:8], in0=s_packT[:, 0:8],
                                             in1=s_g2[:]),
                   reads=[s_g2], sreads=[s_packT], writes=[s_packT], tag="gff0b")
            kb.add(V, lambda g: g.tensor_scalar(out=s_g2[:], in0=s_g1[:],
                                                scalar1=s_srt[:, 1:2], scalar2=None,
                                                op0=OP.is_equal),
                   reads=[s_g1], sreads=[s_srt, s_packT], writes=[s_g2], tag="geq1")
            kb.add(V, lambda g: g.tensor_scalar(out=s_g1[:], in0=s_packT[:, 0:8],
                                                scalar1=-1.0, scalar2=1.0,
                                                op0=OP.mult, op1=OP.add),
                   reads=[], sreads=[s_packT], writes=[s_g1], tag="gnot")
            kb.add(V, lambda g: g.tensor_mul(out=s_g2[:], in0=s_g2[:], in1=s_g1[:]),
                   reads=[], sreads=[s_g1, s_g2], writes=[s_g2], tag="gm1")
            kb.add(V, lambda g: g.tensor_tensor_scan(out=s_g1[:], data0=s_g2[:],
                                                     data1=s_z8[:], initial=0.0,
                                                     op0=OP.add, op1=OP.add),
                   reads=[s_z8], sreads=[s_g2], writes=[s_g1], tag="gcs1")
            kb.add(V, lambda g: g.tensor_scalar(out=s_g1[:], in0=s_g1[:], scalar1=1.0,
                                                scalar2=None, op0=OP.is_equal),
                   reads=[], sreads=[s_g1], writes=[s_g1], tag="gff1a")
            kb.add(V, lambda g: g.tensor_mul(out=s_packT[:, 8:16], in0=s_g1[:], in1=s_g2[:]),
                   reads=[], sreads=[s_g1, s_g2], writes=[s_packT], tag="gff1b")
            kb.add("tensor",
                   lambda g: g.transpose(out=pbank[7][0:16, 0:128], in_=s_packT[:, 0:16],
                                         identity=s_ident[:]),
                   reads=[s_packT, s_ident], writes=[PB_R[7]], tag="g_tp2")
            kb.add("vector",
                   lambda g, tt=tt:
                   g.tensor_copy(out=ohf[:, tt * 128:(tt + 1) * 128], in_=pbank[7][0:16, 0:128]),
                   reads=[PB_R[7]], writes=["rA"], tag="g_tpev2")
            kb.add("tensor",
                   lambda g: g.transpose(out=pbank[6][0:2, 0:128], in_=s_packT[:, 16:18],
                                         identity=s_ident[:]),
                   reads=[s_packT, s_ident], writes=[PB_R[6]], tag="g_tp3")
            kb.add("vector",
                   lambda g, tt=tt:
                   g.tensor_copy(out=wrow[:, tt * 128:(tt + 1) * 128], in_=pbank[6][0:2, 0:128]),
                   reads=[PB_R[6]], writes=["rB0"], tag="g_tpev3")
        rk16 = s_tt[0:16, :]
        TTRES = ["tt_a", "tt_b"]
        kb.add(V, lambda g: g.tensor_tensor_scan(out=rk16[:, :], data0=ohf[0:16, :],
                                                 data1=ohf[0:16, :], initial=0.0,
                                                 op0=OP.add, op1=OP.bypass),
               sreads=["rA"], writes=TTRES, tag="g_rsc")
        kb.add(V, lambda g: g.tensor_copy(out=s_cntT[:, 0:1], in_=rk16[:, TC - 1:TC]),
               sreads=TTRES, writes=[s_cntT], tag="g_cnt")
        def _wcnt(g, shc=shc):
            return g.dma_start(out=shc[bass.ts(state["rank"], 1), 0:16], in_=s_cntT[:, 0:1])
        kb.add("gpsimd", _wcnt, reads=[s_cntT], writes=["shmem"], tag="g_wcnt", dma=True)
        barrier(extra_reads=["shmem"])
        kb.dma("gpsimd", s_cnt2[:, :], shc[:, :], reads=["barrier"], writes=[s_cnt2],
               tag="g_rcnt")
        kb.add("tensor",
               lambda g: g.transpose(out=pbank[7][0:16, 0:2], in_=s_cnt2[0:2, 0:16],
                                     identity=s_ident[0:2, 0:2]),
               reads=[s_cnt2, s_ident], writes=[PB_R[7]], tag="g_tpc")
        kb.add(V, lambda g: g.tensor_copy(out=s_cntT[:, 0:2], in_=pbank[7][0:16, 0:2]),
               reads=[PB_R[7]], writes=[s_cntT], tag="g_cntT")
        kb.add(V, lambda g: g.tensor_scalar(out=s_off0[:, 0:1], in0=s_cntT[:, 0:1],
                                            scalar1=s_rankf[:, 0:1], scalar2=-1.0,
                                            op0=OP.mult, op1=OP.add),
               reads=[s_rankf], sreads=[s_cntT], writes=[s_off0], tag="g_off")
        for qt in range(4):
            sl = slice(qt * 512, (qt + 1) * 512)
            kb.add(V, lambda g, sl=sl: g.tensor_scalar(out=pbank[5][0:16, :],
                                                       in0=rk16[:, sl],
                                                       scalar1=s_off0[:, 0:1], scalar2=None,
                                                       op0=OP.add),
                   reads=TTRES, sreads=[s_off0], writes=[PB_R[5]], tag="g_rk1")
            kb.add(V, lambda g, sl=sl: g.tensor_mul(out=rk16[:, sl], in0=ohf[0:16, sl],
                                                    in1=pbank[5][0:16, :]),
                   reads=["rA", PB_R[5]], writes=TTRES, tag="g_rk2")
        for qt in range(4):
            sl = slice(qt * 512, (qt + 1) * 512)
            kb.add("tensor",
                   lambda g, sl=sl: g.matmul(out=pbank[7][0:2, :], lhsT=s_selk[:, :],
                                             rhs=rk16[:, sl], start=True, stop=True),
                   reads=[s_selk] + TTRES, writes=[PB_R[7]], tag="g_mr")
            kb.add(V, lambda g, sl=sl: g.tensor_scalar(out=pbank[6][0:2, :],
                                                       in0=pbank[7][0:2, :],
                                                       scalar1=float(CAP) - 0.5,
                                                       scalar2=None, op0=OP.is_lt),
                   reads=[PB_R[7]], writes=[PB_R[6]], tag="g_keep1")
            kb.add(V, lambda g, sl=sl: g.tensor_mul(out=wrow[:, sl], in0=wrow[:, sl],
                                                    in1=pbank[6][0:2, :]),
                   reads=["rB0", PB_R[6]], writes=["rB0"], tag="g_keep2")
            kb.add("tensor",
                   lambda g, sl=sl: g.matmul(out=pbank[7][0:16, :], lhsT=s_selkT[:, :],
                                             rhs=wrow[:, sl], start=True, stop=True),
                   reads=[s_selkT, "rB0"], writes=[PB_R[7]], tag="g_wkbc")
            kb.add(V, lambda g, sl=sl: g.tensor_mul(out=rk16[:, sl], in0=ohf[0:16, sl],
                                                    in1=pbank[7][0:16, :]),
                   reads=[PB_R[7], "rA"], writes=TTRES, tag="g_wko")
            kb.add("tensor",
                   lambda g, sl=sl: g.matmul(out=pbank[6][0:8, :], lhsT=s_sele[:, :],
                                             rhs=rk16[:, sl], start=True, stop=True),
                   reads=[s_sele] + TTRES, writes=[PB_R[6]], tag="g_comb")
            kb.add(V, lambda g, sl=sl: g.tensor_copy(out=comb[:, sl], in_=pbank[6][0:8, :]),
                   reads=[PB_R[6]], writes=["comb"], tag="g_combev")
        kb.dma("sync", dr_comb[:, :], comb[:, :], reads=["comb"], writes=["dr_comb"],
               tag="g_spill")

    def ffn_call(which):
        layernorm()
        gating(which)
        if which == 0:
            kb.dma("sync", dbg_comb, comb[:, :], reads=["comb"], writes=["dbg"], tag="dbg")
            kb.dma("sync", dbg_cnt, s_cntT[:, :], reads=[s_cntT], writes=["dbg"], tag="dbg")
            kb.dma("sync", dbg_glog, s_glog[:, :], reads=[s_glog], writes=["dbg"], tag="dbg")
            kb.dma("sync", dbg_ohf, ohf[:, :], reads=["rA"], writes=["dbg"], tag="dbg")
            kb.dma("sync", dbg_xn, s_xn[0][:, :], reads=[XNR[0]], writes=["dbg"], tag="dbg")
            kb.dma("sync", dbg_rkf, s_tt[0:16, :], reads=["tt_a", "tt_b"], writes=["dbg"], tag="dbg")
        groups = [(e, q) for e in range(NE) for q in range(4)]

        def load_gu(e, q):
            for k in range(8):
                kb.dma("gpsimd", s_wgq[k][:],
                       wg_t[e, k * 128:(k + 1) * 128, q * 512:(q + 1) * 512],
                       writes=[WGR[k]], tag="w_gu")
                kb.dma("gpsimd", s_wuq[k][:],
                       wu_t[e, k * 128:(k + 1) * 128, q * 512:(q + 1) * 512],
                       writes=[WUR[k]], tag="w_gu")

        def load_wd(e, q):
            for k in range(4):
                kb.dma("gpsimd", s_wdq[k][:],
                       wd_t[e, q * 512 + k * 128: q * 512 + (k + 1) * 128, :],
                       writes=[WDR[k]], tag="w_d")

        load_gu(*groups[0])
        load_wd(*groups[0])
        for gi, (e, q) in enumerate(groups):
            for t8 in range(8):
                tsl = slice(t8 * 256, (t8 + 1) * 256)
                kb.dma("gpsimd", s_combbc[:],
                       dr_comb[e:e + 1, tsl].broadcast_to([128, 256]),
                       reads=["dr_comb"], writes=[s_combbc], tag="combbc")
                for hc in range(4):
                    gbank, gsl = pbank[4 + (hc % 2)], slice(0, 256)
                    ubank, usl = pbank[6 + (hc % 2)], slice(0, 256)
                    for k in range(8):
                        kb.add("tensor",
                               lambda g, k=k, hc=hc, tsl=tsl, gbank=gbank, gsl=gsl:
                               g.matmul(out=gbank[0:128, gsl],
                                        lhsT=s_wgq[k][:, hc * 128:(hc + 1) * 128],
                                        rhs=s_xn[k][:, tsl], start=(k == 0), stop=(k == 7)),
                               reads=[WGR[k], XNR[k]], writes=[PB_R[4 + (hc % 2)]],
                               tag="mm_g")
                    for k in range(8):
                        kb.add("tensor",
                               lambda g, k=k, hc=hc, tsl=tsl, ubank=ubank, usl=usl:
                               g.matmul(out=ubank[0:128, usl],
                                        lhsT=s_wuq[k][:, hc * 128:(hc + 1) * 128],
                                        rhs=s_xn[k][:, tsl], start=(k == 0), stop=(k == 7)),
                               reads=[WUR[k], XNR[k]], writes=[PB_R[6 + (hc % 2)]],
                               tag="mm_u")
                    kb.add("scalar",
                           lambda g, gbank=gbank, gsl=gsl:
                           g.activation(out=s_sg[:], in_=gbank[0:128, gsl], func=AF.Silu),
                           reads=[PB_R[4 + (hc % 2)]], writes=[s_sg], tag="silu")
                    kb.add("vector",
                           lambda g, ubank=ubank, usl=usl:
                           g.scalar_tensor_tensor(out=s_tt[:, 0:256], in0=ubank[0:128, usl],
                                                  scalar=1.0, in1=s_sg[:],
                                                  op0=OP.mult, op1=OP.mult),
                           reads=[PB_R[6 + (hc % 2)], s_sg], writes=["tt_a"], tag="pmul1")
                    kb.add("vector",
                           lambda g, hc=hc, tsl=tsl:
                           g.tensor_mul(out=s_p[hc][:, tsl], in0=s_tt[:, 0:256],
                                        in1=s_combbc[:]),
                           reads=[s_combbc], sreads=["tt_a"], writes=[PR[hc]], tag="pmul2")
            if gi + 1 < len(groups):
                load_gu(*groups[gi + 1])
            for t8 in range(8):
                tsl = slice(t8 * 256, (t8 + 1) * 256)
                for dc in range(8):
                    bank = pbank[dc % 4]
                    bsl = slice((dc // 4) * 256, (dc // 4) * 256 + 256)
                    for hc in range(4):
                        kb.add("tensor",
                               lambda g, hc=hc, dc=dc, tsl=tsl, bank=bank, bsl=bsl:
                               g.matmul(out=bank[0:128, bsl],
                                        lhsT=s_wdq[hc][:, dc * 128:(dc + 1) * 128],
                                        rhs=s_p[hc][:, tsl], start=(hc == 0), stop=(hc == 3)),
                               reads=[WDR[hc], PR[hc]], writes=[PB_R[dc % 4]],
                               tag="mm_d")
                    kb.add("vector",
                           lambda g, dc=dc, tsl=tsl, bank=bank, bsl=bsl:
                           g.tensor_add(out=s_x[dc][:, tsl], in0=s_x[dc][:, tsl],
                                        in1=bank[0:128, bsl]),
                           reads=[PB_R[dc % 4], XR[dc]], writes=[XR[dc]], tag="dn_add")
            if gi + 1 < len(groups):
                load_wd(*groups[gi + 1])

    def mamba():
        V = "vector"
        layernorm()
        for i in range(8):
            kb.dma("sync", dr_x[i:i + 1, :].rearrange("one (p c) -> p (one c)", p=128),
                   s_x[i], reads=[XR[i]], writes=[f"drx{i}"], tag="xspill")
        if MAMBA_CUT <= 0:
            return
        xA_b = s_xA[:, :].bitcast(BF16)
        xi_t = [bslot(j) for j in range(12)] + \
               [s_xiex[:, j2 * 2048:(j2 + 1) * 2048] for j2 in range(4)]
        XIRl = [[BB[j]] for j in range(12)] + [["xe0"], ["xe1"], ["xe2"], ["xe3"]]
        winr = [s_xiex[:, 8192:10240], s_xiex[:, 10240:12288]]
        WINR = [["xe4"], ["xe5"]]
        def win_chunk(j):
            p = j % 2
            kb.dma("gpsimd", winr[p][:, 0:1024],
                   w_in[:, j * 128:(j + 1) * 128].rearrange("(k p) c -> p k c", p=128),
                   reads=WINR[p], writes=WINR[p], tag="w_in_ld")
            for t8 in range(4):
                tsl = slice(t8 * 512, (t8 + 1) * 512)
                bank = pbank[4 + t8 % 2]
                for k in range(8):
                    kb.add("tensor",
                           lambda g, k=k, tsl=tsl, bank=bank, p=p:
                           g.matmul(out=bank[0:128, 0:512],
                                    lhsT=winr[p][:, k * 128:(k + 1) * 128],
                                    rhs=s_xn[k][:, tsl], start=(k == 0), stop=(k == 7)),
                           reads=WINR[p] + [XNR[k]], writes=[PB_R[4 + t8 % 2]], tag="mm_win")
                if j < 16:
                    kb.add(V, lambda g, j=j, tsl=tsl, bank=bank:
                           g.tensor_copy(out=xi_t[j][:, tsl], in_=bank[0:128, 0:512]),
                           reads=[PB_R[4 + t8 % 2]], writes=XIRl[j], tag="xi_ev")
                else:
                    st = s_resst[:, 0:512]
                    kb.add(V, lambda g, tsl=tsl, bank=bank, st=st:
                           g.tensor_copy(out=st, in_=bank[0:128, 0:512]),
                           reads=[PB_R[4 + t8 % 2]], writes=["resst"], tag="res_ev")
                    kb.dma("sync",
                           dr_res[j - 16:j - 15, :]
                           .rearrange("one (p c) -> p (one c)", p=128)[:, t8 * 512:(t8 + 1) * 512],
                           st, reads=["resst"], writes=["drres"], tag="res_sp")
        for j in range(16):
            win_chunk(j)
        if MAMBA_CUT <= 1:
            return
        kb.add("gpsimd",
               lambda g: g.dma_start(
                   out=sh_halo[0:1536, :][:, bass.ts(state["rank"], 3)]
                   .rearrange("(c p) w -> p c w", p=128),
                   in_=s_bigB[:, 0:24576].rearrange("p (c l) -> p c l",
                                                    l=2048)[:, :, 2045:2048]),
               reads=[BB[j] for j in range(12)], writes=["shmem2"], tag="halo_w1",
               dma=True)
        kb.add("gpsimd",
               lambda g: g.dma_start(
                   out=sh_halo[1536:2048, :][:, bass.ts(state["rank"], 3)]
                   .rearrange("(c p) w -> p c w", p=128),
                   in_=s_xiex[:, 0:8192].rearrange("p (c l) -> p c l",
                                                   l=2048)[:, :, 2045:2048]),
               reads=["xe0", "xe1", "xe2", "xe3"], writes=["shmem2"], tag="halo_w2",
               dma=True)
        barrier(extra_reads=["shmem2"])
        if MAMBA_CUT <= 2:
            return
        for j in range(16):
            win_chunk(16 + j)
            kb.dma("gpsimd", s_hx[:, 0:3], sh_halo[j * 128:(j + 1) * 128, 0:3],
                   reads=["barrier"], writes=[s_hx], tag="halo_r")
            kb.add(V, lambda g: g.tensor_scalar(out=s_hx[:, 0:3], in0=s_hx[:, 0:3],
                                                scalar1=s_rankc[:, 0:1], scalar2=None,
                                                op0=OP.mult),
                   reads=[s_rankc], sreads=[s_hx], writes=[s_hx], tag="halo_m")
            kb.add(V, lambda g, j=j: g.tensor_copy(out=s_hx[:, 3:6], in_=xi_t[j][:, 0:3]),
                   reads=XIRl[j], writes=[s_hx], tag="halo_c")
            kb.add(V, lambda g, j=j: g.tensor_scalar(out=s_tt[:, 3:2048],
                                                     in0=xi_t[j][:, 0:2045],
                                                     scalar1=s_convw[0][:, j:j + 1],
                                                     scalar2=None, op0=OP.mult),
                   reads=XIRl[j] + [s_convw[0]], writes=["tt_a", "tt_b"], tag="cv0")
            for k in range(1, 4):
                kb.add(V, lambda g, j=j, k=k:
                       g.scalar_tensor_tensor(out=s_tt[:, 3:2048], in0=xi_t[j][:, k:2045 + k],
                                              scalar=s_convw[k][:, j:j + 1],
                                              in1=s_tt[:, 3:2048], op0=OP.mult, op1=OP.add),
                       reads=XIRl[j] + [s_convw[k]], sreads=["tt_a", "tt_b"],
                       writes=["tt_a", "tt_b"], tag="cvk")
            kb.add(V, lambda g, j=j: g.tensor_scalar(out=s_tt[:, 0:3], in0=s_hx[:, 0:3],
                                                     scalar1=s_convw[0][:, j:j + 1],
                                                     scalar2=None, op0=OP.mult),
                   reads=[s_convw[0]], sreads=[s_hx], writes=["tt_a"], tag="cvh0")
            for k in range(1, 4):
                kb.add(V, lambda g, j=j, k=k:
                       g.scalar_tensor_tensor(out=s_tt[:, 0:3], in0=s_hx[:, k:k + 3],
                                              scalar=s_convw[k][:, j:j + 1],
                                              in1=s_tt[:, 0:3], op0=OP.mult, op1=OP.add),
                       reads=[s_convw[k]], sreads=[s_hx, "tt_a"], writes=["tt_a"], tag="cvhk")
            kb.add(V, lambda g, j=j: g.tensor_scalar(out=s_tt[:, 0:2048], in0=s_tt[:, 0:2048],
                                                     scalar1=s_convb[:, j:j + 1],
                                                     scalar2=None, op0=OP.add),
                   reads=[s_convb], sreads=["tt_a", "tt_b"], writes=["tt_a", "tt_b"],
                   tag="cvb")
            kb.add("scalar", lambda g, j=j: g.activation(out=xi_t[j][:, :], in_=s_tt[:, 0:2048],
                                                         func=AF.Silu),
                   reads=["tt_a", "tt_b"], writes=XIRl[j], tag="cv_silu")

        kb.add("gpsimd",
               lambda g: g.dma_start(
                   out=sh_xis[0:1536, :][:, bass.ts(state["rank"], TC)]
                   .rearrange("(c p) l -> p c l", p=128),
                   in_=s_bigB[:, 0:24576]),
               reads=[BB[j] for j in range(12)], writes=["shmem3"], tag="xis_w1", dma=True)
        kb.add("gpsimd",
               lambda g: g.dma_start(
                   out=sh_xis[1536:2048, :][:, bass.ts(state["rank"], TC)]
                   .rearrange("(c p) l -> p c l", p=128),
                   in_=s_xiex[:, 0:8192]),
               reads=["xe0", "xe1", "xe2", "xe3"], writes=["shmem3"], tag="xis_w2", dma=True)
        for t8 in range(4):
            tsl = slice(t8 * 512, (t8 + 1) * 512)
            for k in range(16):
                kb.dma("gpsimd", s_wx[k % 2][:], w_x[k * 128:(k + 1) * 128, :],
                       writes=[f"swx{k % 2}"], tag="wx_ld")
                kb.add("tensor",
                       lambda g, k=k, tsl=tsl:
                       g.matmul(out=pbank[6][0:96, :], lhsT=s_wx[k % 2][:], rhs=xi_t[k][:, tsl],
                                start=(k == 0), stop=(k == 15)),
                       reads=[f"swx{k % 2}"] + XIRl[k], writes=[PB_R[6]], tag="mm_wx")
            kb.add(V, lambda g, tsl=tsl: g.tensor_copy(out=s_tt[0:96, 0:512],
                                                       in_=pbank[6][0:96, :]),
                   reads=[PB_R[6]], writes=["tt_a", "tt_b"], tag="wx_ev")
            kb.dma("gpsimd", dr_xdbl[:, tsl], s_tt[0:96, 0:512],
                   reads=["tt_a", "tt_b"], writes=["drxdbl"], tag="xdbl_st")
        kb.add("gpsimd",
               lambda g: g.dma_start(out=sh_xdbl[:, bass.ts(state["rank"], TC)],
                                     in_=dr_xdbl[:, :]),
               reads=["drxdbl"], writes=["shmem3"], tag="xdbl_w", dma=True)
        barrier(extra_reads=["shmem3"])
        if MAMBA_CUT <= 3:
            return
        kb.dma("gpsimd", dr_bcrows[:, :], sh_xdbl[64:96, :],
               reads=["barrier"], writes=["bcrows"], tag="bc_cast")
        kb.dma("gpsimd", s_di[:, :], sh_xdbl[0:64, :], reads=["barrier"],
               writes=DIR, tag="di_ld")
        kb.add("gpsimd",
               lambda g: g.dma_start(out=dr_uloc[:, :],
                                     in_=sh_xis[bass.ts(state["rank"], 1024), :]),
               reads=["barrier"], writes=["druloc"], tag="uloc", dma=True)
        kb.add("gpsimd",
               lambda g: g.dma_start(out=s_wdtloc[:, :],
                                     in_=w_dt[:, bass.ts(state["rank"], 1024)]),
               writes=[s_wdtloc], tag="wdtloc", dma=True)
        kb.add("gpsimd",
               lambda g: g.dma_start(out=s_bdtloc[:, :],
                                     in_=bdt_c[:, bass.ts(state["rank"], 8)]),
               writes=[s_bdtloc], tag="bdtloc", dma=True)
        SC_delta = s_xA[:, 0:4096]
        SC_pd = s_xA[:, 4096:8192]
        SC_w0 = s_xA[:, 8192:12288]
        SC_w1 = s_xA[:, 12288:16384]
        R_d = ["sx0", "sx1"]; R_pd = ["sx2", "sx3"]; R_w0 = ["sx4", "sx5"]; R_w1 = ["sx6", "sx7"]
        xnA_b = s_xnA[:, :]
        SB_u = xnA_b[:, 0:4096]; R_u = ["sxn0", "sxn1"]
        SB_h = xnA_b[:, 4096:8192]; R_h = ["sxn2", "sxn3"]
        SB_c = xnA_b[:, 8192:12288]; R_c = ["sxn4", "sxn5"]
        SB_bm = bslot(8, 2); R_bm = [BB[8], BB[9]]
        SB_cm = bslot(10, 2); R_cm = [BB[10], BB[11]]
        ABF = bslot(4, 2); R_af = [BB[4], BB[5]]
        W1B = bslot(6, 2); R_w1b = [BB[6], BB[7]]
        YF32 = bigB_f32[:, 0:4096]; R_y = [BB[0], BB[1], BB[2], BB[3]]
        YW0 = S - 1024
        YW = slice(YW0, S)
        for ci in range(8):
            rowsl = lambda t: t[bass.ts(state["rank"], 1024), :]

            for c8 in range(8):
                bank = pbank[c8 % 4]
                csl = slice(c8 * 512, (c8 + 1) * 512)
                kb.add("tensor",
                       lambda g, c8=c8, bank=bank, ci=ci:
                       g.matmul(out=bank[0:128, 0:512],
                                lhsT=s_wdtloc[:, ci * 128:(ci + 1) * 128],
                                rhs=s_di[:, c8 * 512:(c8 + 1) * 512], start=True, stop=True),
                       reads=[s_wdtloc] + DIR, writes=[PB_R[c8 % 4]], tag="mm_dt")
                kb.add("scalar",
                       lambda g, bank=bank, ci=ci:
                       g.activation(out=s_tt[:, 0:512], in_=bank[0:128, 0:512],
                                    func=AF.Identity, bias=s_bdtloc[:, ci:ci + 1]),
                       reads=[PB_R[c8 % 4], s_bdtloc], writes=["tt_a"], tag="sp_b")
                kb.add("scalar",
                       lambda g: g.activation(out=s_tt[:, 512:1024], in_=s_tt[:, 0:512],
                                              func=AF.Abs),
                       sreads=["tt_a"], writes=["tt_a"], tag="sp_abs")
                kb.add("scalar",
                       lambda g: g.activation(out=s_tt[:, 1024:1536], in_=s_tt[:, 512:1024],
                                              func=AF.Exp, scale=-1.0),
                       sreads=["tt_a"], writes=["tt_b"], tag="sp_exp")
                kb.add("scalar",
                       lambda g: g.activation(out=s_tt[:, 512:1024], in_=s_tt[:, 1024:1536],
                                              func=AF.Ln, bias=s_one11[:, 0:1]),
                       reads=[s_one11], sreads=["tt_b"], writes=["tt_a"], tag="sp_ln")
                kb.add("scalar",
                       lambda g: g.activation(out=s_tt[:, 1536:2048], in_=s_tt[:, 0:512],
                                              func=AF.Relu),
                       sreads=["tt_a"], writes=["tt_b"], tag="sp_rl")
                kb.add(V, lambda g, csl=csl: g.tensor_add(out=SC_delta[:, csl],
                                                          in0=s_tt[:, 512:1024],
                                                          in1=s_tt[:, 1536:2048]),
                       sreads=["tt_a", "tt_b"], writes=R_d, tag="sp_add")
            kb.add(V, lambda g: g.tensor_tensor_scan(out=SC_pd[:, :], data0=SC_delta[:, :],
                                                     data1=SC_delta[:, :], initial=0.0,
                                                     op0=OP.add, op1=OP.bypass),
                   reads=R_d, writes=R_pd, tag="pd_scan")
            kb.add(V, lambda g: g.tensor_copy(out=s_dcol[:, 0:1], in_=SC_pd[:, 4095:4096]),
                   sreads=R_pd, writes=[s_dcol], tag="dsum")
            kb.add(V, lambda g: g.tensor_scalar(out=SC_pd[:, :], in0=SC_pd[:, :],
                                                scalar1=s_dcol[:, 0:1], scalar2=None,
                                                op0=OP.subtract),
                   sreads=[s_dcol] + R_pd, writes=R_pd, tag="m_sub")
            kb.dma("gpsimd", SB_u[:, :], dr_uloc[ci * 128:(ci + 1) * 128, :],
                   reads=["druloc"], writes=R_u, tag="u_ld")
            kb.add(V, lambda g: g.tensor_mul(out=ABF[:, :], in0=SC_delta[:, :],
                                             in1=SB_u[:, :]),
                   reads=R_d + R_u, writes=R_af, tag="a_mul")
            for n in range(NST):
                def _exp(n=n):
                    kb.add("scalar",
                           lambda g, n=n: g.activation(out=SC_w0[:, :],
                                                       in_=SC_delta[:, :],
                                                       func=AF.Exp, scale=-(n + 1.0)),
                           reads=R_d, writes=R_w0, tag="sc_exp")
                def _sig(n=n):
                    kb.add("scalar",
                           lambda g, n=n: g.activation(out=SB_c[:, YW],
                                                       in_=SC_pd[:, YW],
                                                       func=AF.Sigmoid,
                                                       scale=(n + 1.0),
                                                       bias=s_b27[:, 0:1]),
                           reads=R_pd + [s_b27], writes=R_c, tag="sc_sig")
                if n % 2 == 0:
                    _exp(); _sig()
                else:
                    _sig(); _exp()
                kb.dma("gpsimd", SB_bm[:, :],
                       dr_bcrows[n:n + 1, :].broadcast_to([128, S]),
                       reads=["bcrows"], writes=R_bm, tag="bm_rep")
                kb.dma("gpsimd", SB_cm[:, :],
                       dr_bcrows[16 + n:17 + n, :].broadcast_to([128, S]),
                       reads=["bcrows"], writes=R_cm, tag="cm_rep")
                kb.add(V, lambda g: g.tensor_mul(out=W1B[:, :], in0=ABF[:, :],
                                                 in1=SB_bm[:, :]),
                       reads=R_af + R_bm, writes=R_w1b, tag="dbu")
                kb.add(V, lambda g: g.tensor_tensor_scan(out=SB_h[:, :],
                                                         data0=SC_w0[:, :], data1=W1B[:, :],
                                                         initial=0.0, op0=OP.mult, op1=OP.add),
                       reads=R_w0 + R_w1b, writes=R_h, tag="hscan")
                kb.add(V, lambda g: g.tensor_mul(out=SC_w0[:, YW],
                                                 in0=SB_h[:, YW], in1=SB_c[:, YW]),
                       reads=R_h + R_c, writes=R_w0, tag="hc")
                kb.add(V, lambda g: g.tensor_mul(out=SC_w1[:, YW], in0=SC_w0[:, YW],
                                                 in1=SB_cm[:, YW]),
                       reads=R_w0 + R_cm, writes=R_w1, tag="hcc")
                if n == 0:
                    kb.add(V, lambda g: g.memset(YF32[:, 0:YW0], 0.0),
                           writes=R_y, tag="ymem")
                    kb.add(V, lambda g: g.tensor_copy(out=YF32[:, YW], in_=SC_w1[:, YW]),
                           reads=R_w1 + R_y, writes=R_y, tag="y0")
                else:
                    kb.add(V, lambda g: g.tensor_add(out=YF32[:, YW], in0=YF32[:, YW],
                                                     in1=SC_w1[:, YW]),
                           reads=R_w1 + R_y, writes=R_y, tag="yadd")
            kb.dma("gpsimd", dr_yloc[ci * 128:(ci + 1) * 128, :], YF32[:, :],
                   reads=R_y, writes=["dryloc"], tag="y_st")
        kb.add("gpsimd",
               lambda g: g.dma_start(out=sh_y[bass.ts(state["rank"], 1024), :],
                                     in_=dr_yloc[:, :]),
               reads=["dryloc"], writes=["shmem4"], tag="y_wr", dma=True)
        barrier(extra_reads=["shmem4"])
        if MAMBA_CUT <= 4:
            return
        yt_t = [bslot(j) for j in range(12)] + \
               [s_xnA[:, (4 + j) * 2048:(5 + j) * 2048] for j in range(4)]
        YTR = [[BB[j]] for j in range(12)] + [["sxn4"], ["sxn5"], ["sxn6"], ["sxn7"]]
        kb.add("gpsimd",
               lambda g: g.dma_start(out=dr_yepi[:, :],
                                     in_=sh_y[:, bass.ts(state["rank"], TC)]),
               reads=["barrier"], writes=["dryepi"], tag="yepi", dma=True)
        kb.add("gpsimd",
               lambda g: g.dma_start(out=dr_uepi[:, :],
                                     in_=sh_xis[:, bass.ts(state["rank"], TC)]),
               reads=["barrier"], writes=["druepi"], tag="uepi", dma=True)
        for j in range(16):
            ybuf = s_xnA[:, 0:2048]
            ubuf = s_xnA[:, 2048:4096]
            rbuf = s_xnA[:, 4096:6144]
            kb.dma("gpsimd", ybuf, dr_yepi[j * 128:(j + 1) * 128, :],
                   reads=["dryepi"], writes=["sxn0"], tag="y_ld")
            kb.dma("gpsimd", ubuf, dr_uepi[j * 128:(j + 1) * 128, :],
                   reads=["druepi"], writes=["sxn1"], tag="u_ld2")
            kb.dma("sync", rbuf,
                   dr_res[j:j + 1, :].rearrange("one (p c) -> p (one c)", p=128),
                   reads=["drres"], writes=["sxn2"], tag="res_ld")
            kb.add("scalar", lambda g: g.activation(out=s_tt[:, 0:2048],
                                                    in_=rbuf,
                                                    func=AF.Silu),
                   reads=["sxn2"], writes=["tt_a", "tt_b"], tag="swish")
            kb.add(V, lambda g, j=j:
                   g.scalar_tensor_tensor(out=s_xA[:, 0:2048],
                                          in0=ubuf, scalar=s_dvec[:, j:j + 1],
                                          in1=ybuf,
                                          op0=OP.mult, op1=OP.add),
                   reads=["sxn0", "sxn1", s_dvec], writes=["sx0"], tag="ud_y")
            kb.add(V, lambda g, j=j: g.tensor_mul(out=yt_t[j][:, :], in0=s_xA[:, 0:2048],
                                                  in1=s_tt[:, 0:2048]),
                   reads=["sx0", "tt_a", "tt_b"], writes=YTR[j], tag="ytot")
        for i in range(8):
            kb.dma("sync", s_x[i], dr_x[i:i + 1, :].rearrange("one (p c) -> p (one c)", p=128),
                   reads=[f"drx{i}"], writes=[XR[i]], tag="x_rel")
        wor = [s_xiex[:, 8192:10240], s_xiex[:, 10240:12288]]
        WOR = [["xe4"], ["xe5"]]
        for m in range(8):
            p = m % 2
            kb.dma("gpsimd", wor[p][:, 0:2048],
                   w_out[:, m * 128:(m + 1) * 128].rearrange("(k p) c -> p k c", p=128),
                   reads=WOR[p], writes=WOR[p], tag="wo_ld")
            for t8 in range(4):
                tsl = slice(t8 * 512, (t8 + 1) * 512)
                bank = pbank[4 + t8 % 2]
                for k in range(16):
                    kb.add("tensor",
                           lambda g, k=k, tsl=tsl, bank=bank, p=p:
                           g.matmul(out=bank[0:128, 0:512],
                                    lhsT=wor[p][:, k * 128:(k + 1) * 128],
                                    rhs=yt_t[k][:, tsl], start=(k == 0), stop=(k == 15)),
                           reads=WOR[p] + YTR[k], writes=[PB_R[4 + t8 % 2]], tag="mm_wo")
                kb.add(V, lambda g, m=m, tsl=tsl, bank=bank:
                       g.tensor_add(out=s_x[m][:, tsl], in0=s_x[m][:, tsl],
                                    in1=bank[0:128, 0:512]),
                       reads=[PB_R[4 + t8 % 2], XR[m]], writes=[XR[m]], tag="wo_add")

    ffn_call(0)
    if DEV_STAGE >= 2:
        mamba()
    if DEV_STAGE >= 3:
        ffn_call(1)

    for tt in range(16):
        stg = s_stage[tt % 2]
        for dc in range(8):
            bank = pbank[dc % 4]
            kb.add("tensor",
                   lambda g, tt=tt, dc=dc, bank=bank:
                   g.transpose(out=bank[0:128, 0:128], in_=s_x[dc][:, tt * 128:(tt + 1) * 128],
                               identity=s_ident[:]),
                   reads=[XR[dc], s_ident], writes=[PB_R[dc % 4]], tag="otp")
            kb.add("vector",
                   lambda g, dc=dc, bank=bank, stg=stg:
                   g.tensor_copy(out=stg[:, dc * 128:(dc + 1) * 128], in_=bank[0:128, 0:128]),
                   reads=[PB_R[dc % 4]], writes=[STGR[tt % 2]], tag="oev")
        kb.dma("sync", out_tm[tt * 128:(tt + 1) * 128, :], stg[:, :],
               reads=[STGR[tt % 2]], writes=["out"], tag="ostore")

    kb.add("sync", lambda g: g.nop(), reads=["out", "dbg"], tag="fence")

    kb.analyze()
    if _os.environ.get("KBDUMP"):
        want = set(_os.environ["KBDUMP"].split(","))
        for op in kb.ops:
            if op.tag in want:
                print(f"[{op.tag}] eng={op.eng} q={op.queue} idx={op.idx} "
                      f"waits={op.waits} inc={op.need_inc}")
    skeys = sorted({op.skey for op in kb.ops} | set(QUEUES) | {"poll", "flag"})
    if _os.environ.get("KBDUMP"):
        print(f"semaphores: {len(skeys)}")
    assert len(skeys) <= 100, f"too many semaphores: {len(skeys)}"
    sems = {}
    for qn in skeys:
        sems[qn] = kb.es.enter_context(nc.semaphore(f"sem_{qn}"))
    per_eng = kb.materialize(sems)

    with nc.Block() as block:
        def runner(eng):
            def run(g):
                for op, waits in per_eng[eng]:
                    for s, v in waits:
                        g.wait_ge(s, v)
                    inst = op.fn(g)
                    if op.need_inc:
                        if inst is None:
                            inst = g.nop()
                        inst.then_inc(sems[op.skey], op.inc_amount)
            return run

        block.sync(runner("sync"))
        block.gpsimd(runner("gpsimd"))
        block.tensor(runner("tensor"))
        block.vector(runner("vector"))
        block.scalar(runner("scalar"))

    kb.es.close()
    return nc, state


_PROG = None

def _get_prog():
    global _PROG
    if _PROG is None:
        _PROG = build_program()[0]
    return _PROG


def kernel(**inputs):
    i = {k: np.asarray(v) for k, v in inputs.items()}
    x = np.ascontiguousarray(i["L0_attn_x"], np.float32)
    w_in = np.ascontiguousarray(i["L0_attn_W_in"], np.float32)
    conv_w = np.ascontiguousarray(i["L0_attn_conv_w"], np.float32)
    conv_b = np.ascontiguousarray(i["L0_attn_conv_b"], np.float32)
    w_x = np.ascontiguousarray(i["L0_attn_W_x"], np.float32)
    w_dt = np.ascontiguousarray(i["L0_attn_W_dt"], np.float32)
    b_dt = np.ascontiguousarray(i["L0_attn_b_dt"], np.float32)
    d_vec = np.ascontiguousarray(i["L0_attn_D"], np.float32)
    w_out = np.ascontiguousarray(i["L0_attn_W_out"], np.float32)
    gw_t = np.ascontiguousarray(i["L0_ffn_gate_w"].T, np.float32)
    wg_t = np.ascontiguousarray(np.swapaxes(i["L0_ffn_gate_proj_w"], 1, 2), np.float32)
    wu_t = np.ascontiguousarray(np.swapaxes(i["L0_ffn_up_proj_w"], 1, 2), np.float32)
    wd_t = np.ascontiguousarray(np.swapaxes(i["L0_ffn_down_proj_w"], 1, 2), np.float32)

    def cols(v):
        return np.ascontiguousarray(v.reshape(16, 128).T, np.float32)
    convw_c = [cols(conv_w[k]) for k in range(4)]
    convb_c = cols(conv_b)
    bdt_c = cols(b_dt)
    dvec_c = cols(d_vec)

    rng = np.random.default_rng()
    nonce = rng.integers(1, 2**31 - 1, size=(1, 8), dtype=np.int64).astype(np.uint32)

    nc = _get_prog()
    in_maps = []
    for c in range(8):
        b, h = c // 2, c % 2
        im = dict(
            x_tm=np.ascontiguousarray(x[b, h * TC:(h + 1) * TC, :]),
            w_in=w_in, convb_c=convb_c, bdt_c=bdt_c, dvec_c=dvec_c,
            w_x=w_x, w_dt=w_dt, w_out=w_out, gw_t=gw_t,
            wg_t=wg_t, wu_t=wu_t, wd_t=wd_t,
            rank_in=np.array([[h]], np.uint32),
            rankf_in=np.full((16, 1), float(h), np.float32),
            rankc_in=np.full((128, 1), float(h), np.float32),
            nonces=nonce,
        )
        for k in range(4):
            im[f"convw{k}"] = convw_c[k]
        in_maps.append(im)

    res = run_bass_kernel_spmd(nc, in_maps, core_ids=list(range(8)))
    out = np.empty((B, S, D), np.float32)
    for c in range(8):
        b, h = c // 2, c % 2
        out[b, h * TC:(h + 1) * TC, :] = res.results[c]["out_tm"]
    return out
